# version 01
# baseline (speedup 1.0000x reference)
import numpy as np
import ml_dtypes
import concourse.bass as bass
import concourse.mybir as mybir
from concourse.bass_utils import run_bass_kernel_spmd

F32 = mybir.dt.float32
F16 = mybir.dt.float16
BF16 = mybir.dt.bfloat16
AF = mybir.ActivationFunctionType
ALU = mybir.AluOpType
AX = mybir.AxisListType

D = 4096
S = 4096
QO = 1024
NT = 32
NQ = 8
DC = 32
DL = 512
H = 32
DH = 128
NE = 8
ENGS = ("sync", "scalar", "vector", "tensor", "gpsimd")


AUTOSEM = {"scalar": "eS", "vector": "eV", "tensor": "eT"}


class Plan:
    def __init__(self):
        self.ops = {e: [] for e in ENGS}
        self.cnt = {}

    def emit(self, eng, fn, inc=None, waits=()):
        s = AUTOSEM.get(eng)
        if s is not None and fn is not None:
            waits = tuple(waits) + ((s, self.cnt[s]),)
        self.ops[eng].append((tuple(waits), fn, inc))
        if inc is not None:
            self.cnt[inc[0]] += inc[1]

    def sem(self, name):
        self.cnt[name] = 0

    def snap(self, sem):
        return self.cnt[sem]


def build_program(upto=99, p1n=40, p1trans=True):
    nc = bass.Bass("TRN2", target_bir_lowering=False)

    xf = nc.dram_tensor("xf", [S, D], F32, kind="ExternalInput")
    xo = nc.dram_tensor("xo", [QO, D], F32, kind="ExternalInput")
    wq = nc.dram_tensor("wq", [H, 128, DC * 128], F32, kind="ExternalInput")
    wc = nc.dram_tensor("wc", [128, DC, DL], F32, kind="ExternalInput")
    wk = nc.dram_tensor("wk", [H, 128, 4 * 128], F32, kind="ExternalInput")
    wv = nc.dram_tensor("wv", [16, 128, 4 * 256], F32, kind="ExternalInput")
    wo = nc.dram_tensor("wo", [8, 128, DC * 512], F32, kind="ExternalInput")
    at_ = nc.dram_tensor("at_", [128, DC, 128], F32, kind="ExternalInput")
    bc_ = nc.dram_tensor("bc_", [128, D], F32, kind="ExternalInput")
    wr_ = nc.dram_tensor("wr_", [128, DC, NE], F32, kind="ExternalInput")
    brt = nc.dram_tensor("brt", [128, NE], F32, kind="ExternalInput")
    etat = nc.dram_tensor("etat", [128, 1], F32, kind="ExternalInput")
    cose = nc.dram_tensor("cose", [128, S], F32, kind="ExternalInput")
    sine = nc.dram_tensor("sine", [128, S], F32, kind="ExternalInput")
    cosq = nc.dram_tensor("cosq", [128, QO], F32, kind="ExternalInput")
    sinq = nc.dram_tensor("sinq", [128, QO], F32, kind="ExternalInput")
    y = nc.dram_tensor("y", [QO, D], F32, kind="ExternalOutput")

    m1s = nc.dram_tensor("m1s", [128, NT, D], F16)
    m1os = nc.dram_tensor("m1os", [128, NQ, D], F16)
    m1ts = nc.dram_tensor("m1ts", [128, NT, D], F16)
    m1nts = nc.dram_tensor("m1nts", [128, NQ, D], F16)
    sS = nc.dram_tensor("sS", [QO, S], F32)
    pP = nc.dram_tensor("pP", [QO, S], F16)
    x1s = nc.dram_tensor("x1s", [QO, D], F32)
    ln2s = nc.dram_tensor("ln2s", [QO, D], BF16)
    x2s = nc.dram_tensor("x2s", [QO, D], F32)
    ln3s = nc.dram_tensor("ln3s", [QO, D], BF16)
    lato = nc.dram_tensor("lato", [QO, DL], BF16)
    latf = nc.dram_tensor("latf", [S, DL], BF16)
    kts = nc.dram_tensor("kts", [H, 128, S], BF16)
    vts = nc.dram_tensor("vts", [H, 128, S], BF16)
    x3s = nc.dram_tensor("x3s", [QO, D], F32)

    import contextlib
    es = contextlib.ExitStack()
    sb = lambda *a, **k: es.enter_context(nc.sbuf_tensor(*a, **k))
    psb = lambda *a, **k: es.enter_context(nc.psum_tensor(*a, **k))

    A = sb("A", [128, 32768], F16, is_transpose_dest=True)
    B = sb("B", [128, 32768], F16)
    C = sb("C", [128, 16384], F16)
    Dn = sb("Dn", [128, 8192], F16)
    G = sb("G", [128, 12288], F16)

    def f32v(ar, off, n):
        return ar[:, off:off + 2 * n].bitcast(F32)

    def f16v(ar, off, n):
        return ar[:, off:off + n]

    def bf16v(ar, off, n):
        return ar[:, off:off + n].bitcast(BF16)

    o = 0
    def galloc(n):
        nonlocal o
        r = o
        o += n + (n & 1)
        return r
    at_sb = bf16v(G, galloc(DC * 128), DC * 128)
    bc_sb = bf16v(G, galloc(D), D)
    wr_sb = bf16v(G, galloc(DC * NE), DC * NE)
    br_sb = f32v(G, galloc(2 * NE), NE)
    eta_sb = f32v(G, galloc(2), 1)
    ones_bf = bf16v(G, galloc(2), 1)
    onesr = f32v(G, galloc(256), 128)
    cq_sb = bf16v(G, galloc(QO), QO)
    sq_sb = bf16v(G, galloc(QO), QO)
    mus = f32v(G, galloc(80), 40)
    sqs = f32v(G, galloc(80), 40)
    var_ = f32v(G, galloc(80), 40)
    nmu = f32v(G, galloc(80), 40)
    rstd = f32v(G, galloc(80), 40)
    nrstd = f32v(G, galloc(80), 40)
    nbias = f32v(G, galloc(80), 40)
    pbias = f32v(G, galloc(80), 40)
    rm = f32v(G, galloc(2 * NQ), NQ)
    nrm = f32v(G, galloc(2 * NQ), NQ)
    sums_a = f32v(G, galloc(2 * NQ), NQ)
    etars = f32v(G, galloc(2 * NQ), NQ)
    pm = f32v(G, galloc(128), 64)
    tdum = f32v(G, galloc(32), 16)
    idt = bf16v(G, galloc(128), 128)
    idt16 = f16v(G, galloc(128), 128)
    rlr = f32v(G, galloc(2 * NE), NE)
    nmx8 = f32v(G, galloc(2), 1)
    ssum8 = f32v(G, galloc(2), 1)
    sr8 = f32v(G, galloc(2), 1)
    scors = f32v(G, galloc(2 * NE), NE)
    assert o <= 12288, o

    PA = psb("PA", [128, 1024], F32)
    PB = psb("PB", [128, 1024], F32)
    PC = psb("PC", [128, 1024], F32)
    PD = psb("PD", [128, 512], F32)
    PE_ = psb("PE_", [128, 512], F32)
    pa_blocks = [PA[:, 0:512], PA[:, 512:1024], PB[:, 0:512], PB[:, 512:1024]]

    P = Plan()
    sem_objs = {}
    for name in ("g0", "g1", "g2", "g3", "g4", "g5", "g6", "g7", "s0", "s1", "s2",
                 "x0", "x1p", "x2p", "eV", "eT", "eS", "eG", "cc", "bar"):
        sem_objs[name] = es.enter_context(nc.semaphore(name))
        P.sem(name)

    emit = P.emit

    DSEMS = ("g0", "g1", "g2", "g3", "g4", "g5", "g6", "g7", "s0", "s1", "s2", "x0", "x1p", "x2p")

    def bar_all():
        k = P.snap("bar")
        for e in ENGS:
            def _inc(h):
                h.sem_inc(sem_objs["bar"], 1)
            pre = [(s_, P.snap(s_)) for s_ in DSEMS] if e == "gpsimd" else []
            emit(e, _inc, waits=pre)
            P.cnt["bar"] += 1
        tgt = k + 5
        for e in ENGS:
            emit(e, None, waits=(("bar", tgt),))

    emit("gpsimd", lambda h: h.dma_start(out=at_sb, in_=at_.ap()), inc=("g0", 16))
    emit("gpsimd", lambda h: h.dma_start(out=bc_sb, in_=bc_.ap()), inc=("g0", 16))
    emit("gpsimd", lambda h: h.dma_start(out=wr_sb, in_=wr_.ap()), inc=("g0", 16))
    emit("gpsimd", lambda h: h.dma_start(out=cq_sb, in_=cosq.ap()), inc=("g0", 16))
    emit("gpsimd", lambda h: h.dma_start(out=sq_sb, in_=sinq.ap()), inc=("g0", 16))
    emit("sync", lambda h: h.dma_start(out=br_sb, in_=brt.ap()), inc=("s0", 16))
    emit("sync", lambda h: h.dma_start(out=eta_sb, in_=etat.ap()), inc=("s0", 16))
    emit("gpsimd", lambda h: h.memset(ones_bf, 1.0), inc=("eG", 1))
    emit("gpsimd", lambda h: h.memset(onesr[0:1, :], 1.0), inc=("eG", 1))
    ident_np = np.eye(128, dtype=ml_dtypes.bfloat16)
    identt = nc.inline_tensor(ident_np, "identt")
    ident16_np = np.eye(128, dtype=np.float16)
    identt16 = nc.inline_tensor(ident16_np, "identt16")
    emit("sync", lambda h: h.dma_start(out=idt, in_=identt.ap()), inc=("s0", 16))
    emit("sync", lambda h: h.dma_start(out=idt16, in_=identt16.ap()), inc=("s0", 16))
    cfg_s0 = P.snap("s0")


    def _finish(dump=None, nrows=QO):
        if dump is not None:
            emit("gpsimd", lambda h: h.dma_start(out=y.ap()[0:nrows, :], in_=dump), inc=("g3", 16))
        emit("gpsimd", None, waits=[(s_, P.snap(s_)) for s_ in DSEMS])
        with nc.Block() as block:
            def replay(eng_name):
                def _f(h):
                    for waits, fn, inc in P.ops[eng_name]:
                        for (semname, cval) in waits:
                            if cval > 0:
                                h.wait_ge(sem_objs[semname], cval)
                        if fn is None:
                            continue
                        r = fn(h)
                        if inc is not None:
                            r.then_inc(sem_objs[inc[0]], inc[1])
                return _f
            block.sync(replay("sync"))
            block.scalar(replay("scalar"))
            block.vector(replay("vector"))
            block.tensor(replay("tensor"))
            block.gpsimd(replay("gpsimd"))
        es.close()
        return nc

    if upto == 0:
        return _finish()

    xt_bufs = [f32v(A, i * 8192, D) for i in range(4)]
    m1stag = [f16v(B, i * D, D) for i in range(2)]
    trstag = [f16v(B, 2 * D + i * D, D) for i in range(2)]
    negstag = [f16v(B, 4 * D + i * D, D) for i in range(2)]
    jnk = f16v(B, 6 * D, D)

    def ln_stats_chain(a, b, eps=1e-5):
        def _f(h):
            h.tensor_scalar(mus[:, a:b], mus[:, a:b], 1.0 / D, None, op0=ALU.mult)
            h.tensor_scalar(var_[:, a:b], sqs[:, a:b], 1.0 / D, eps, op0=ALU.mult, op1=ALU.add)
            h.drain()
            h.tensor_tensor(sqs[:, a:b], mus[:, a:b], mus[:, a:b], op=ALU.mult)
            h.tensor_scalar(nmu[:, a:b], mus[:, a:b], -1.0, None, op0=ALU.mult)
            h.drain()
            h.tensor_tensor(var_[:, a:b], var_[:, a:b], sqs[:, a:b], op=ALU.subtract)
            h.drain()
            return h.reciprocal(rstd[:, a:b], var_[:, a:b])
        return _f

    def ln_post_chain(a, b):
        def _f(h):
            h.tensor_tensor(nbias[:, a:b], nmu[:, a:b], rstd[:, a:b], op=ALU.mult)
            h.tensor_scalar(nrstd[:, a:b], rstd[:, a:b], -1.0, None, op0=ALU.mult)
            h.drain()
            return h.tensor_scalar(pbias[:, a:b], nbias[:, a:b], -1.0, None, op0=ALU.mult)
        return _f

    n_all = min(NT + NQ, p1n)
    for g in range(0, n_all, 4):
        gsem = "s1" if (g // 4) % 2 == 0 else "s2"
        gtiles = list(range(g, min(g + 4, n_all)))
        for i, t in enumerate(gtiles):
            src = xf.ap()[t * 128:(t + 1) * 128, :] if t < NT else xo.ap()[(t - NT) * 128:(t - NT + 1) * 128, :]
            w = []
            if g >= 4:
                w = [("eS", P.snap("eS")), ("eV", P.snap("eV")), ("eT", P.snap("eT"))]
            emit("sync", (lambda s_, d_: lambda h: h.dma_start(out=d_, in_=s_))(src, xt_bufs[i]),
                 inc=(gsem, 16), waits=w)
        ldc = P.snap(gsem)
        for i, t in enumerate(gtiles):
            emit("vector", (lambda i, t: lambda h: h.reduce_sum(mus[:, t:t + 1], xt_bufs[i], axis=AX.X))(i, t),
                 inc=("eV", 1), waits=[(gsem, ldc)])
            emit("scalar", (lambda i, t: lambda h: h.activation(out=jnk, in_=xt_bufs[i],
                                                               func=AF.Square, accum_out=sqs[:, t:t + 1]))(i, t),
                 inc=("eS", 1), waits=[(gsem, ldc)])
        a, b = gtiles[0], gtiles[0] + len(gtiles)
        emit("vector", ln_stats_chain(a, b), inc=("eV", 1), waits=[("eS", P.snap("eS"))])
        emit("scalar", (lambda a, b: lambda h: h.activation(out=rstd[:, a:b], in_=rstd[:, a:b], func=AF.Sqrt))(a, b),
             inc=("eS", 1), waits=[("eV", P.snap("eV"))])
        emit("vector", ln_post_chain(a, b), inc=("eV", 1), waits=[("eS", P.snap("eS"))])
        statc = P.snap("eV")
        for i, t in enumerate(gtiles):
            par = t % 2
            own = t >= NT
            wst = [("g3", P.snap("g3")), ("g4", P.snap("g4"))] if t >= 2 else []
            emit("scalar", (lambda i, t, par: lambda h: h.activation(
                out=m1stag[par], in_=xt_bufs[i], func=AF.Identity,
                bias=nbias[:, t:t + 1], scale=rstd[:, t:t + 1]))(i, t, par),
                inc=("eS", 1), waits=[("eV", statc)] + wst)
            sc_m1 = P.snap("eS")
            dstm = m1os.ap()[:, t - NT, :] if own else m1s.ap()[:, t, :]
            emit("gpsimd", (lambda par, dstm: lambda h: h.dma_start(out=dstm, in_=m1stag[par]))(par, dstm),
                 inc=("g3", 16), waits=[("eS", sc_m1)])
            if own:
                emit("scalar", (lambda i, t, par: lambda h: h.activation(
                    out=negstag[par], in_=xt_bufs[i], func=AF.Identity,
                    bias=pbias[:, t:t + 1], scale=nrstd[:, t:t + 1]))(i, t, par),
                    inc=("eS", 1))
            srcstag = negstag[par] if own else m1stag[par]
            wsrc = [("eS", P.snap("eS")), ("s0", P.snap("s0"))]
            for dc in range(DC if p1trans else 0):
                pv = PD[:, 0:64].bitcast(F16) if dc % 2 == 0 else PE_[:, 0:64].bitcast(F16)
                tw = (wsrc if dc == 0 else []) + [("eV", P.snap("eV"))]
                emit("tensor", (lambda srcstag, dc, pv: lambda h: h.transpose(
                    pv, srcstag[:, dc * 128:(dc + 1) * 128], idt16))(srcstag, dc, pv),
                    inc=("eT", 1), waits=tw)
                tc = P.snap("eT")
                cw = [("eT", tc)] + ([("g4", P.snap("g4"))] if dc == 0 else [])
                emit("vector", (lambda par, dc, pv: lambda h: h.tensor_copy(
                    trstag[par][:, dc * 128:(dc + 1) * 128], pv))(par, dc, pv),
                    inc=("eV", 1), waits=cw)
            if p1trans:
                dstt = m1nts.ap()[:, t - NT, :] if own else m1ts.ap()[:, t, :]
                emit("gpsimd", (lambda par, dstt: lambda h: h.dma_start(out=dstt, in_=trstag[par]))(par, dstt),
                     inc=("g4", 16), waits=[("eV", P.snap("eV"))])
    bar_all()

    if upto == 1:
        return _finish(m1os.ap())

    Q = f16v(A, 0, NQ * D)
    emit("sync", lambda h: h.dma_start(out=Q, in_=m1nts.ap()), inc=("s0", 16))
    qldc = P.snap("s0")
    rbuf = [f16v(B, i * (4 * D), 4 * D) for i in range(2)]

    def q_ap(qt, dc):
        off = (qt * DC + dc) * 128
        return Q[:, off:off + 128]

    def rb_ap(par, dc):
        v = rbuf[par].rearrange("p (tt dc j) -> p tt (dc j)", tt=4, dc=DC)
        return v[:, :, dc * 128:(dc + 1) * 128]

    sstag = [f32v(C, i * 1024, 512) for i in range(4)]
    sp_mm_snap = {}
    sp_st_snap = {}
    sp_ev_snap = {}
    for sp in range(8):
        par = sp % 2
        spsem = "s1" if par == 0 else "s2"
        w = [("eT", sp_mm_snap[sp - 2])] if sp >= 2 else []
        for i in range(4):
            emit("sync", (lambda sp, i, par: lambda h: h.dma_start(
                out=f16v(B, par * 4 * D + i * D, D), in_=m1ts.ap()[:, sp * 4 + i, :]))(sp, i, par),
                inc=(spsem, 16), waits=w if i == 0 else ())
        ldc = P.snap(spsem)
        for qt in range(NQ):
            idx = sp * NQ + qt
            blk = pa_blocks[idx % 4]
            wpsum = [("eV", sp_ev_snap[idx - 4])] if idx >= 4 else []
            for dc in range(DC):
                emit("tensor", (lambda qt, dc, par, blk: lambda h: h.matmul(
                    blk, q_ap(qt, dc), rb_ap(par, dc), start=(dc == 0), stop=(dc == DC - 1)))(qt, dc, par, blk),
                    inc=("eT", 1) if dc == DC - 1 else None,
                    waits=([(spsem, ldc), ("s0", qldc)] + wpsum) if dc == 0 else ())
            mmc = P.snap("eT")

            def s_evac(qt, sp, idx, blk):
                def _f(h):
                    h.tensor_copy(sstag[idx % 4], blk)
                    return h.reduce_max(pm[:, qt * 8 + sp:qt * 8 + sp + 1], blk, axis=AX.X)
                return _f
            wstore = [("x0", P.snap("x0"))] if idx >= 4 else []
            emit("vector", s_evac(qt, sp, idx, blk), inc=("eV", 1), waits=[("eT", mmc)] + wstore)
            sp_ev_snap[idx] = P.snap("eV")
            emit("scalar", (lambda qt, sp, idx: lambda h: h.dma_start(
                out=sS.ap()[qt * 128:(qt + 1) * 128, sp * 512:(sp + 1) * 512], in_=sstag[idx % 4]))(qt, sp, idx),
                inc=("x0", 16), waits=[("eV", P.snap("eV"))])
            sp_st_snap[idx] = P.snap("x0")
        sp_mm_snap[sp] = P.snap("eT")

    def rowmax_fin(h):
        for qt in range(NQ):
            h.reduce_max(rm[:, qt:qt + 1], pm[:, qt * 8:(qt + 1) * 8], axis=AX.X)
        h.drain()
        return h.tensor_scalar(nrm[:, 0:NQ], rm[:, 0:NQ], -1.0, None, op0=ALU.mult)
    emit("vector", rowmax_fin, inc=("eV", 1), waits=[("eV", P.snap("eV"))])
    bar_all()

    if upto == 2:
        return _finish(sS.ap())

    sbuf_s = [f32v(C, i * 8192, D) for i in range(2)]
    pstag = [f16v(Dn, i * D, D) for i in range(2)]
    for qt in range(NQ):
        par = qt % 2
        qsem = "s1" if par == 0 else "s2"
        w = [("eS", P.snap("eS"))] if qt >= 2 else []
        emit("sync", (lambda qt, par: lambda h: h.dma_start(
            out=sbuf_s[par], in_=sS.ap()[qt * 128:(qt + 1) * 128, :]))(qt, par),
            inc=(qsem, 16), waits=w)
        ldc = P.snap(qsem)
        wst = [("g3", P.snap("g3"))] if qt >= 2 else []
        emit("scalar", (lambda qt, par: lambda h: h.activation(
            out=pstag[par], in_=sbuf_s[par], func=AF.Exp,
            bias=nrm[:, qt:qt + 1], accum_out=sums_a[:, qt:qt + 1]))(qt, par),
            inc=("eS", 1), waits=[(qsem, ldc)] + wst)
        emit("gpsimd", (lambda qt, par: lambda h: h.dma_start(
            out=pP.ap()[qt * 128:(qt + 1) * 128, :], in_=pstag[par]))(qt, par),
            inc=("g3", 16), waits=[("eS", P.snap("eS"))])

    def recip_chain(h):
        h.reciprocal(etars[:, 0:NQ], sums_a[:, 0:NQ])
        h.drain()
        return h.tensor_scalar(etars[:, 0:NQ], etars[:, 0:NQ], eta_sb, None, op0=ALU.mult)
    emit("vector", recip_chain, inc=("eV", 1), waits=[("eS", P.snap("eS"))])
    bar_all()

    if upto == 3:
        return _finish(pP.ap())

    for c in range(DC):
        emit("sync", (lambda c: lambda h: h.dma_start_transpose(
            f16v(A, c * 1024, 1024), pP.ap()[:, c * 128:(c + 1) * 128]))(c), inc=("s0", 16))
    ptldc = P.snap("s0")
    mbuf = [f16v(B, i * DC * 512, DC * 512) for i in range(2)]
    xblk = [f32v(C, i * 1024, 512) for i in range(2)]
    m1oblk = [f16v(C, 4096 + i * 512, 512) for i in range(2)]
    xm1blk = [f32v(C, 6144 + i * 1024, 512) for i in range(2)]
    x1blk = [f32v(C, 10240 + i * 1024, 512) for i in range(2)]
    ob_mm_snap = {}
    for ob in range(8):
        par = ob % 2
        obsem = "s1" if par == 0 else "s2"
        w = [("eT", ob_mm_snap[ob - 2])] if ob >= 2 else []
        emit("sync", (lambda ob, par: lambda h: h.dma_start(
            out=mbuf[par], in_=m1s.ap()[:, :, ob * 512:(ob + 1) * 512]))(ob, par),
            inc=(obsem, 16), waits=w)
        mldc = P.snap(obsem)
        for qt in range(NQ):
            idx = ob * NQ + qt
            bpar = idx % 2
            blk = pa_blocks[idx % 2]
            wblk = [("eV", P.snap("eV"))] if idx >= 2 else []
            bsem = "x1p" if bpar == 0 else "x2p"
            emit("scalar", (lambda qt, ob, bpar: lambda h: h.dma_start(
                out=xblk[bpar], in_=xo.ap()[qt * 128:(qt + 1) * 128, ob * 512:(ob + 1) * 512]))(qt, ob, bpar),
                inc=(bsem, 16), waits=wblk)
            emit("scalar", (lambda qt, ob, bpar: lambda h: h.dma_start(
                out=m1oblk[bpar], in_=m1os.ap()[:, qt, ob * 512:(ob + 1) * 512]))(qt, ob, bpar),
                inc=(bsem, 16))
            blkldc = P.snap(bsem)
            wpsum = [("eV", P.snap("eV"))] if idx >= 2 else []
            for c in range(DC):
                emit("tensor", (lambda qt, ob, c, par, blk: lambda h: h.matmul(
                    blk, f16v(A, c * 1024 + qt * 128, 128), mbuf[par][:, c * 512:(c + 1) * 512],
                    start=(c == 0), stop=(c == DC - 1)))(qt, ob, c, par, blk),
                    inc=("eT", 1) if c == DC - 1 else None,
                    waits=([(obsem, mldc), ("s0", ptldc)] + wpsum) if c == 0 else ())
            mmc = P.snap("eT")

            def evac(qt, bpar, blk):
                def _f(h):
                    h.tensor_tensor(xm1blk[bpar], xblk[bpar], m1oblk[bpar], op=ALU.add)
                    h.drain()
                    return h.scalar_tensor_tensor(x1blk[bpar], blk, etars[:, qt:qt + 1], xm1blk[bpar],
                                                  op0=ALU.mult, op1=ALU.add)
                return _f
            emit("vector", evac(qt, bpar, blk), inc=("eV", 1),
                 waits=[("eT", mmc), (bsem, blkldc), ("g3", P.snap("g3"))])
            emit("gpsimd", (lambda qt, ob, bpar: lambda h: h.dma_start(
                out=x1s.ap()[qt * 128:(qt + 1) * 128, ob * 512:(ob + 1) * 512], in_=x1blk[bpar]))(qt, ob, bpar),
                inc=("g3", 16), waits=[("eV", P.snap("eV"))])
        ob_mm_snap[ob] = P.snap("eT")
    bar_all()

    if upto == 4:
        return _finish(x1s.ap())

    def ffn_phase(xin, lnmid, lnout, base_t, to_output=False, xout=None):
        xtb = [f32v(B, i * 8192, D) for i in range(2)]
        lstag = [bf16v(B, 16384 + i * D, D) for i in range(2)]
        fjnk = f16v(B, 24576, D)
        for g in range(0, NQ, 2):
            gsem = "s1" if (g // 2) % 2 == 0 else "s2"
            for i, t in enumerate((g, g + 1)):
                w = [("eS", P.snap("eS")), ("eV", P.snap("eV"))] if g >= 2 else []
                emit("sync", (lambda t, i: lambda h: h.dma_start(
                    out=xtb[i], in_=xin.ap()[t * 128:(t + 1) * 128, :]))(t, i),
                    inc=(gsem, 16), waits=w)
            ldc = P.snap(gsem)
            for i, t in enumerate((g, g + 1)):
                st = base_t + t
                emit("vector", (lambda i, st: lambda h: h.reduce_sum(mus[:, st:st + 1], xtb[i], axis=AX.X))(i, st),
                     inc=("eV", 1), waits=[(gsem, ldc)])
                emit("scalar", (lambda i, st: lambda h: h.activation(
                    out=fjnk, in_=xtb[i], func=AF.Square, accum_out=sqs[:, st:st + 1]))(i, st),
                    inc=("eS", 1), waits=[(gsem, ldc)])
            a, b = base_t + g, base_t + g + 2
            emit("vector", ln_stats_chain(a, b), inc=("eV", 1), waits=[("eS", P.snap("eS"))])
            emit("scalar", (lambda a, b: lambda h: h.activation(out=rstd[:, a:b], in_=rstd[:, a:b], func=AF.Sqrt))(a, b),
                 inc=("eS", 1), waits=[("eV", P.snap("eV"))])
            emit("vector", ln_post_chain(a, b), inc=("eV", 1), waits=[("eS", P.snap("eS"))])
            statc = P.snap("eV")
            for i, t in enumerate((g, g + 1)):
                st = base_t + t
                wst = [("g3", P.snap("g3"))] if g >= 2 else []
                emit("scalar", (lambda i, t, st: lambda h: h.activation(
                    out=lstag[i], in_=xtb[i], func=AF.Identity,
                    bias=nbias[:, st:st + 1], scale=rstd[:, st:st + 1]))(i, t, st),
                    inc=("eS", 1), waits=[("eV", statc)] + wst)
                emit("gpsimd", (lambda i, t: lambda h: h.dma_start(
                    out=lnmid.ap()[t * 128:(t + 1) * 128, :], in_=lstag[i]))(i, t),
                    inc=("g3", 16), waits=[("eS", P.snap("eS"))])
        bar_all()
        for c in range(DC):
            emit("sync", (lambda c: lambda h: h.dma_start_transpose(
                bf16v(A, c * 1024, 1024), lnmid.ap()[:, c * 128:(c + 1) * 128]))(c), inc=("s0", 16))
        lnldc = P.snap("s0")
        x2stag = [f32v(B, i * 8192, D) for i in range(2)]
        lnstag2 = [bf16v(B, 16384 + i * D, D) for i in range(2)]
        lowt = [bf16v(Dn, i * 256, 128) for i in range(2)]
        lostag = [bf16v(Dn, 512 + i * 128, 128) for i in range(2)]
        for qt in range(NQ):
            par = qt % 2
            wr_w = [("eV", P.snap("eV"))] if qt >= 1 else []
            for c in range(DC):
                emit("tensor", (lambda qt, c: lambda h: h.matmul(
                    PD[:, 0:NE], bf16v(A, c * 1024 + qt * 128, 128), wr_sb[:, c * NE:(c + 1) * NE],
                    start=(c == 0), stop=(c == DC - 1)))(qt, c),
                    inc=("eT", 1) if c == DC - 1 else None,
                    waits=([("s0", lnldc), ("g0", P.snap("g0"))] + wr_w) if c == 0 else ())
            rc = P.snap("eT")

            def router_sm(h):
                h.tensor_tensor(rlr[:, 0:NE], PD[:, 0:NE], br_sb, op=ALU.add)
                h.drain()
                return h.reduce_max(nmx8, rlr[:, 0:NE], axis=AX.X, negate=True)
            emit("vector", router_sm, inc=("eV", 1), waits=[("eT", rc)])
            emit("scalar", lambda h: h.activation(out=scors[:, 0:NE], in_=rlr[:, 0:NE], func=AF.Exp,
                                                  bias=nmx8, accum_out=ssum8),
                 inc=("eS", 1), waits=[("eV", P.snap("eV"))])

            def router_fin(h):
                h.reciprocal(sr8, ssum8)
                h.drain()
                return h.tensor_scalar(scors[:, 0:NE], scors[:, 0:NE], sr8, 8.0, op0=ALU.mult, op1=ALU.mult)
            emit("vector", router_fin, inc=("eV", 1), waits=[("eS", P.snap("eS"))])
            rfin = P.snap("eV")
            for c in range(DC):
                emit("tensor", (lambda qt, c: lambda h: h.matmul(
                    PE_[:, 0:128], bf16v(A, c * 1024 + qt * 128, 128), at_sb[:, c * 128:(c + 1) * 128],
                    start=(c == 0), stop=(c == DC - 1)))(qt, c),
                    inc=("eT", 1) if c == DC - 1 else None,
                    waits=[("eV", rfin)] if c == 0 else ())
            lc = P.snap("eT")

            def low_evac(par):
                def _f(h):
                    r = None
                    for e in range(NE):
                        r = h.tensor_scalar(lowt[par][:, e * 16:(e + 1) * 16],
                                            PE_[:, e * 16:(e + 1) * 16], scors[:, e:e + 1], None, op0=ALU.mult)
                    return r
                return _f
            emit("vector", low_evac(par), inc=("eV", 1), waits=[("eT", lc)])
            lec = P.snap("eV")
            emit("tensor", (lambda par: lambda h: h.transpose(
                PD[:, 64:128].bitcast(BF16), lowt[par], idt))(par),
                inc=("eT", 1), waits=[("eV", lec), ("s0", cfg_s0)])
            emit("scalar", (lambda par: lambda h: h.copy(lostag[par], PD[:, 64:128].bitcast(BF16)))(par),
                 inc=("eS", 1), waits=[("eT", P.snap("eT"))])
            ltc = P.snap("eS")
            xsem = "s1" if par == 0 else "s2"
            emit("sync", (lambda qt, par: lambda h: h.dma_start(
                out=x2stag[par], in_=xin.ap()[qt * 128:(qt + 1) * 128, :]))(qt, par),
                inc=(xsem, 16), waits=[("g4", P.snap("g4"))] if qt >= 2 else [])
            xldc = P.snap(xsem)
            for ob in range(8):
                pb = pa_blocks[ob % 2]
                wv_ = [("eV", P.snap("eV"))] if ob >= 2 else []
                emit("tensor", (lambda par, ob, pb: lambda h: h.matmul(
                    pb, lostag[par], bc_sb[:, ob * 512:(ob + 1) * 512], start=True, stop=True))(par, ob, pb),
                    inc=("eT", 1), waits=[("eS", ltc)] + wv_)
                emit("vector", (lambda par, ob, pb: lambda h: h.scalar_tensor_tensor(
                    x2stag[par][:, ob * 512:(ob + 1) * 512], pb, 2.0,
                    x2stag[par][:, ob * 512:(ob + 1) * 512], op0=ALU.mult, op1=ALU.add))(par, ob, pb),
                    inc=("eV", 1), waits=[("eT", P.snap("eT")), (xsem, xldc)])
            xec = P.snap("eV")
            dst = y.ap() if to_output else xout.ap()
            emit("gpsimd", (lambda qt, par, dst: lambda h: h.dma_start(
                out=dst[qt * 128:(qt + 1) * 128, :], in_=x2stag[par]))(qt, par, dst),
                inc=("g4", 16), waits=[("eV", xec)])
            if lnout is not None:
                st = base_t + NQ + qt
                emit("vector", (lambda par, st: lambda h: h.reduce_sum(mus[:, st:st + 1], x2stag[par], axis=AX.X))(par, st),
                     inc=("eV", 1), waits=[("eV", xec)])
                emit("scalar", (lambda par, st: lambda h: h.activation(
                    out=fjnk, in_=x2stag[par], func=AF.Square, accum_out=sqs[:, st:st + 1]))(par, st),
                    inc=("eS", 1), waits=[("eV", xec)])
                emit("vector", ln_stats_chain(st, st + 1), inc=("eV", 1), waits=[("eS", P.snap("eS"))])
                emit("scalar", (lambda st: lambda h: h.activation(out=rstd[:, st:st + 1], in_=rstd[:, st:st + 1],
                                                                  func=AF.Sqrt))(st),
                     inc=("eS", 1), waits=[("eV", P.snap("eV"))])
                emit("vector", ln_post_chain(st, st + 1), inc=("eV", 1), waits=[("eS", P.snap("eS"))])
                emit("scalar", (lambda par, st: lambda h: h.activation(
                    out=lnstag2[par], in_=x2stag[par], func=AF.Identity,
                    bias=nbias[:, st:st + 1], scale=rstd[:, st:st + 1]))(par, st),
                    inc=("eS", 1), waits=[("eV", P.snap("eV"))])
                emit("gpsimd", (lambda qt, par: lambda h: h.dma_start(
                    out=lnout.ap()[qt * 128:(qt + 1) * 128, :], in_=lnstag2[par]))(qt, par),
                    inc=("g3", 16), waits=[("eS", P.snap("eS"))])
        bar_all()

    ffn_phase(x1s, ln2s, ln3s, base_t=0, xout=x2s)

    if upto == 5:
        return _finish(x2s.ap())

    for c in range(DC):
        emit("sync", (lambda c: lambda h: h.dma_start_transpose(
            bf16v(A, c * 1024, 1024), ln3s.ap()[:, c * 128:(c + 1) * 128]))(c), inc=("s0", 16))
    l3ldc = P.snap("s0")
    wc_sb = bf16v(C, 0, DC * 512)
    emit("gpsimd", lambda h: h.dma_start(out=wc_sb, in_=wc.ap()), inc=("g7", 16))
    wcldc = P.snap("g7")
    latstag = [bf16v(Dn, i * 512, 512) for i in range(2)]
    for qt in range(NQ):
        par = qt % 2
        pb = pa_blocks[qt % 2]
        w = [("eV", P.snap("eV"))] if qt >= 2 else []
        for c in range(DC):
            emit("tensor", (lambda qt, c, pb: lambda h: h.matmul(
                pb, bf16v(A, c * 1024 + qt * 128, 128), wc_sb[:, c * 512:(c + 1) * 512],
                start=(c == 0), stop=(c == DC - 1)))(qt, c, pb),
                inc=("eT", 1) if c == DC - 1 else None,
                waits=([("s0", l3ldc), ("g7", wcldc)] + w) if c == 0 else ())
        wv2 = [("g3", P.snap("g3"))] if qt >= 2 else []
        emit("vector", (lambda par, pb: lambda h: h.tensor_copy(latstag[par], pb))(par, pb),
             inc=("eV", 1), waits=[("eT", P.snap("eT"))] + wv2)
        emit("gpsimd", (lambda qt, par: lambda h: h.dma_start(
            out=lato.ap()[qt * 128:(qt + 1) * 128, :], in_=latstag[par]))(qt, par),
            inc=("g3", 16), waits=[("eV", P.snap("eV"))])
    emit("gpsimd", lambda h: h.collective_compute(
        "AllGather", ALU.bypass, replica_groups=[[0, 1, 2, 3], [4, 5, 6, 7]],
        ins=[lato.ap().opt()], outs=[latf.ap().opt()]),
        inc=("cc", 1), waits=[("g3", P.snap("g3"))])
    p6_eT = P.snap("eT")
    p6_eV = P.snap("eV")

    if upto == 6:
        emit("gpsimd", None, waits=[("cc", P.snap("cc"))])
        return _finish(latf.ap(), nrows=512)

    qT = bf16v(B, 0, H * 1024)
    wqb = [bf16v(C, i * D, D) for i in range(2)]
    ro1 = f32v(Dn, 2048, 512)
    ro2 = f32v(Dn, 3072, 512)
    ro3 = f32v(Dn, 4096, 512)
    ro4 = f32v(Dn, 5120, 512)
    h_mm_snap = {}
    for h_ in range(H):
        par = h_ % 2
        hsem = "g1" if par == 0 else "g2"
        w = [("eT", h_mm_snap[h_ - 2])] if h_ >= 2 else [("eT", p6_eT), ("eV", p6_eV)]
        emit("gpsimd", (lambda h_, par: lambda h: h.dma_start(out=wqb[par], in_=wq.ap()[h_, :, :]))(h_, par),
             inc=(hsem, 16), waits=w)
        wldc = P.snap(hsem)
        for tb in range(2):
            pb = pa_blocks[(h_ * 2 + tb) % 2]
            wv_ = [("eV", P.snap("eV"))]
            for c in range(DC):
                emit("tensor", (lambda par, c, tb, pb: lambda h: h.matmul(
                    pb, wqb[par][:, c * 128:(c + 1) * 128], bf16v(A, c * 1024 + tb * 512, 512),
                    start=(c == 0), stop=(c == DC - 1)))(par, c, tb, pb),
                    inc=("eT", 1) if c == DC - 1 else None,
                    waits=([(hsem, wldc)] + wv_) if c == 0 else ())
            mmc = P.snap("eT")

            def rope_q(h_, tb, pb):
                lo, hi = tb * 512, (tb + 1) * 512
                dst = qT[:, h_ * 1024 + lo:h_ * 1024 + hi]

                def _f(h):
                    h.tensor_tensor(ro1[0:64, :], pb[0:64, :], cq_sb[0:64, lo:hi], op=ALU.mult)
                    h.tensor_tensor(ro2[0:64, :], pb[64:128, :], sq_sb[0:64, lo:hi], op=ALU.mult)
                    h.tensor_tensor(ro3[0:64, :], pb[64:128, :], cq_sb[64:128, lo:hi], op=ALU.mult)
                    h.tensor_tensor(ro4[0:64, :], pb[0:64, :], sq_sb[64:128, lo:hi], op=ALU.mult)
                    h.drain()
                    h.tensor_tensor(dst[0:64, :], ro1[0:64, :], ro2[0:64, :], op=ALU.subtract)
                    return h.tensor_tensor(dst[64:128, :], ro3[0:64, :], ro4[0:64, :], op=ALU.add)
                return _f
            emit("vector", rope_q(h_, tb, pb), inc=("eV", 1), waits=[("eT", mmc)])
        h_mm_snap[h_] = P.snap("eT")
    bar_all()

    if upto == 7:
        bar_all()
        return _finish(qT)

    latT = bf16v(A, 0, 4 * S)
    emit("sync", None, waits=[("cc", P.snap("cc"))])
    for lc in range(4):
        emit("sync", (lambda lc: lambda h: h.dma_start_transpose(
            bf16v(A, lc * S, S), latf.ap()[:, lc * 128:(lc + 1) * 128]))(lc), inc=("s0", 16))
    latldc = P.snap("s0")
    ce_sb = bf16v(A, 4 * S, S)
    se_sb = bf16v(A, 5 * S, S)
    emit("gpsimd", lambda h: h.dma_start(out=ce_sb, in_=cose.ap()), inc=("g7", 16))
    emit("gpsimd", lambda h: h.dma_start(out=se_sb, in_=sine.ap()), inc=("g7", 16))
    tbldc = P.snap("g7")
    wkb = [bf16v(C, i * 512, 512) for i in range(2)]
    wvb = [bf16v(C, 1024 + i * 1024, 1024) for i in range(2)]
    kstag = [bf16v(C, 3072 + i * 512, 512) for i in range(2)]
    vstag0 = bf16v(A, 6 * S, DC * 128)
    vstag1 = bf16v(A, 7 * S, DC * 128)
    hk_mm_snap = {}
    for h_ in range(H):
        par = h_ % 2
        hsem = "g1" if par == 0 else "g2"
        w = [("eT", hk_mm_snap[h_ - 2])] if h_ >= 2 else []
        emit("gpsimd", (lambda h_, par: lambda h: h.dma_start(out=wkb[par], in_=wk.ap()[h_, :, :]))(h_, par),
             inc=(hsem, 16), waits=w)
        wldc = P.snap(hsem)
        for kb in range(8):
            pb = PD[:, 0:512] if kb % 2 == 0 else PE_[:, 0:512]
            wv_ = [("eV", P.snap("eV"))]
            for lcc in range(4):
                emit("tensor", (lambda par, lcc, kb, pb: lambda h: h.matmul(
                    pb, wkb[par][:, lcc * 128:(lcc + 1) * 128], bf16v(A, lcc * S + kb * 512, 512),
                    start=(lcc == 0), stop=(lcc == 3)))(par, lcc, kb, pb),
                    inc=("eT", 1) if lcc == 3 else None,
                    waits=([(hsem, wldc), ("g7", tbldc), ("s0", latldc)] + wv_) if lcc == 0 else ())
            mmc = P.snap("eT")

            def rope_k(par, kb, pb):
                lo, hi = kb * 512, (kb + 1) * 512
                dst = kstag[par]

                def _f(h):
                    h.tensor_tensor(ro1[0:64, :], pb[0:64, :], ce_sb[0:64, lo:hi], op=ALU.mult)
                    h.tensor_tensor(ro2[0:64, :], pb[64:128, :], se_sb[0:64, lo:hi], op=ALU.mult)
                    h.tensor_tensor(ro3[0:64, :], pb[64:128, :], ce_sb[64:128, lo:hi], op=ALU.mult)
                    h.tensor_tensor(ro4[0:64, :], pb[0:64, :], se_sb[64:128, lo:hi], op=ALU.mult)
                    h.drain()
                    h.tensor_tensor(dst[0:64, :], ro1[0:64, :], ro2[0:64, :], op=ALU.subtract)
                    return h.tensor_tensor(dst[64:128, :], ro3[0:64, :], ro4[0:64, :], op=ALU.add)
                return _f
            wkst = [("x0", P.snap("x0"))] if (h_ * 8 + kb) >= 2 else []
            emit("vector", rope_k(par, kb, pb), inc=("eV", 1), waits=[("eT", mmc)] + wkst)
            emit("scalar", (lambda h_, kb, par: lambda h: h.dma_start(
                out=kts.ap()[h_, :, kb * 512:(kb + 1) * 512], in_=kstag[par]))(h_, kb, par),
                inc=("x0", 16), waits=[("eV", P.snap("eV"))])
        if h_ % 2 == 0:
            gidx = h_ // 2
            vsem = "g5" if gidx % 2 == 0 else "g6"
            emit("gpsimd", (lambda gidx, par: lambda h: h.dma_start(out=wvb[par], in_=wv.ap()[gidx, :, :]))(gidx, par),
                 inc=(vsem, 16), waits=[("eT", P.snap("eT"))] if gidx >= 1 else [("eG", P.snap("eG"))])
            wvldc = P.snap(vsem)
            for kt in range(DC):
                pb2 = PC[:, 0:256] if kt % 2 == 0 else PC[:, 512:768]
                wv_ = [("eV", P.snap("eV"))]
                for lcc in range(4):
                    emit("tensor", (lambda kt, lcc, par, pb2: lambda h: h.matmul(
                        pb2, bf16v(A, lcc * S + kt * 128, 128),
                        wvb[par][:, lcc * 256:(lcc + 1) * 256], start=(lcc == 0), stop=(lcc == 3)))(kt, lcc, par, pb2),
                        inc=("eT", 1) if lcc == 3 else None,
                        waits=([(vsem, wvldc)] + wv_) if lcc == 0 else ())
                mmc = P.snap("eT")

                def v_evac(kt, pb2):
                    def _f(h):
                        h.tensor_copy(vstag0[:, kt * 128:(kt + 1) * 128], pb2[:, 0:128])
                        return h.tensor_copy(vstag1[:, kt * 128:(kt + 1) * 128], pb2[:, 128:256])
                    return _f
                wvs = [("g4", P.snap("g4"))] if kt < 2 and gidx >= 1 else []
                emit("vector", v_evac(kt, pb2), inc=("eV", 1), waits=[("eT", mmc)] + wvs)
            vec = P.snap("eV")
            emit("gpsimd", (lambda h_: lambda h: h.dma_start(out=vts.ap()[h_, :, :], in_=vstag0))(h_),
                 inc=("g4", 16), waits=[("eV", vec)])
            emit("gpsimd", (lambda h_: lambda h: h.dma_start(out=vts.ap()[h_ + 1, :, :], in_=vstag1))(h_),
                 inc=("g4", 16))
        hk_mm_snap[h_] = P.snap("eT")
    bar_all()

    if upto == 8:
        return _finish(kts.ap()[0:8, :, :])

    kbuf = [bf16v(C, i * S, S) for i in range(2)]
    vbuf = [bf16v(C, 2 * S + i * S, S) for i in range(2)]
    ptb = [bf16v(Dn, i * 1024, 1024) for i in range(4)]
    drow = f32v(Dn, 4096, 1024)
    rrow = f32v(Dn, 6144, 1024)
    rbs = f32v(A, 0, 1024)
    sc18 = float(1.0 / np.sqrt(128.0))
    prev_fin_eV = P.snap("eV")
    ha_mm_snap = {}
    for h_ in range(H):
        par = h_ % 2
        hsem = "s1" if par == 0 else "s2"
        wld = [("eT", ha_mm_snap[h_ - 2])] if h_ >= 2 else []
        emit("sync", (lambda h_, par: lambda h: h.dma_start(out=kbuf[par], in_=kts.ap()[h_, :, :]))(h_, par),
             inc=(hsem, 16), waits=wld)
        emit("sync", (lambda h_, par: lambda h: h.dma_start(out=vbuf[par], in_=vts.ap()[h_, :, :]))(h_, par),
             inc=(hsem, 16))
        kvldc = P.snap(hsem)
        for c in range(DC):
            spb = PA[:, 0:1024] if c % 2 == 0 else PB[:, 0:1024]
            wsc = [("eS", P.snap("eS"))] if c >= 2 else []
            if c == 0:
                wsc = wsc + [("eV", prev_fin_eV)]
            for qb in range(2):
                emit("tensor", (lambda h_, c, qb, par, spb: lambda h: h.matmul(
                    spb[:, qb * 512:(qb + 1) * 512], kbuf[par][:, c * 128:(c + 1) * 128],
                    qT[:, h_ * 1024 + qb * 512:h_ * 1024 + (qb + 1) * 512],
                    start=True, stop=True))(h_, c, qb, par, spb),
                    inc=("eT", 1),
                    waits=([(hsem, kvldc)] + wsc) if qb == 0 else ())
            mmc = P.snap("eT")
            wpt = [("eT", P.snap("eT"))] if c >= 4 else []
            emit("scalar", (lambda c, spb: lambda h: h.activation(
                out=ptb[c % 4], in_=spb[:, 0:1024], func=AF.Exp, scale=sc18))(c, spb),
                inc=("eS", 1), waits=[("eT", mmc)] + wpt)
            ec = P.snap("eS")
            for qb in range(2):
                emit("tensor", (lambda c, qb, par: lambda h: h.matmul(
                    PC[:, qb * 512:(qb + 1) * 512], vbuf[par][:, c * 128:(c + 1) * 128],
                    ptb[c % 4][:, qb * 512:(qb + 1) * 512],
                    start=(c == 0), stop=(c == DC - 1)))(c, qb, par),
                    inc=("eT", 1), waits=[("eS", ec)] if qb == 0 else ())
                emit("tensor", (lambda c, qb: lambda h: h.matmul(
                    (PD if qb == 0 else PE_)[0:1, :], ones_bf,
                    ptb[c % 4][:, qb * 512:(qb + 1) * 512],
                    start=(c == 0), stop=(c == DC - 1)))(c, qb),
                    inc=("eT", 1))
        pvc = P.snap("eT")

        def dfin(h):
            h.tensor_copy(drow[0:1, 0:512], PD[0:1, :])
            h.tensor_copy(drow[0:1, 512:1024], PE_[0:1, :])
            h.drain()
            return h.reciprocal(rrow[0:1, :], drow[0:1, :])
        emit("vector", dfin, inc=("eV", 1), waits=[("eT", pvc)])
        dc_ = P.snap("eV")
        for qb in range(2):
            emit("tensor", (lambda qb: lambda h: h.matmul(
                PA[:, qb * 512:(qb + 1) * 512], onesr[0:1, :],
                rrow[0:1, qb * 512:(qb + 1) * 512], start=True, stop=True))(qb),
                inc=("eT", 1), waits=[("eV", dc_)] if qb == 0 else ())
        bcc = P.snap("eT")

        def ofin(h_):
            def _f(h):
                h.tensor_copy(rbs, PA[:, 0:1024])
                h.drain()
                return h.tensor_tensor(qT[:, h_ * 1024:(h_ + 1) * 1024], PC[:, 0:1024], rbs, op=ALU.mult)
            return _f
        emit("vector", ofin(h_), inc=("eV", 1), waits=[("eT", bcc)])
        prev_fin_eV = P.snap("eV")
        ha_mm_snap[h_] = P.snap("eT")
    bar_all()

    if upto == 9:
        return _finish(qT)

    wob = [bf16v(C, i * DC * 256, DC * 256) for i in range(2)]
    x2blk = [f32v(A, i * 512, 256) for i in range(2)]
    x3blk = [f32v(A, 1024 + i * 512, 256) for i in range(2)]
    obh_mm_snap = {}
    for obh in range(16):
        par = obh % 2
        ob, half = obh // 2, obh % 2
        osem = "g1" if par == 0 else "g2"
        w = [("eT", obh_mm_snap[obh - 2])] if obh >= 2 else []
        wsrc = wo.ap()[obh // 2, :, :].rearrange("p (ec j) -> p ec j", ec=DC)[:, :, (obh % 2) * 256:(obh % 2 + 1) * 256]
        emit("gpsimd", (lambda wsrc, par: lambda h: h.dma_start(out=wob[par], in_=wsrc))(wsrc, par),
             inc=(osem, 16), waits=w)
        wldc = P.snap(osem)
        for qt in range(NQ):
            idx9 = obh * NQ + qt
            bpar = idx9 % 2
            pb = pa_blocks[bpar]
            wblk = [("eV", P.snap("eV"))]
            xsem2 = "x1p" if bpar == 0 else "x2p"
            emit("scalar", (lambda qt, ob, half, bpar: lambda h: h.dma_start(
                out=x2blk[bpar], in_=x2s.ap()[qt * 128:(qt + 1) * 128,
                                              ob * 512 + half * 256:ob * 512 + half * 256 + 256]))(qt, ob, half, bpar),
                inc=(xsem2, 16), waits=wblk)
            xldc = P.snap(xsem2)
            for c in range(DC):
                emit("tensor", (lambda qt, c, par, pb: lambda h: h.matmul(
                    pb[:, 0:256], qT[:, c * 1024 + qt * 128:c * 1024 + qt * 128 + 128],
                    wob[par][:, c * 256:(c + 1) * 256],
                    start=(c == 0), stop=(c == DC - 1)))(qt, c, par, pb),
                    inc=("eT", 1) if c == DC - 1 else None,
                    waits=([(osem, wldc)] + ([("eV", P.snap("eV"))] if obh * NQ + qt >= 2 else [])) if c == 0 else ())
            mmc = P.snap("eT")
            emit("vector", (lambda bpar, pb: lambda h: h.tensor_tensor(
                x3blk[bpar], pb[:, 0:256], x2blk[bpar], op=ALU.add))(bpar, pb),
                inc=("eV", 1), waits=[("eT", mmc), (xsem2, xldc), ("g3", P.snap("g3"))])
            emit("gpsimd", (lambda qt, ob, half, bpar: lambda h: h.dma_start(
                out=x3s.ap()[qt * 128:(qt + 1) * 128,
                             ob * 512 + half * 256:ob * 512 + half * 256 + 256], in_=x3blk[bpar]))(qt, ob, half, bpar),
                inc=("g3", 16), waits=[("eV", P.snap("eV"))])
        obh_mm_snap[obh] = P.snap("eT")
    bar_all()

    if upto == 10:
        return _finish(x3s.ap())

    ffn_phase(x3s, ln2s, None, base_t=16, to_output=True)

    return _finish()


_NC_CACHE = None
UPTO = 99


def kernel(L0_attn_a_minibatch, L0_attn_a_eta, L0_attn_b_Wq, L0_attn_b_Wc, L0_attn_b_Wk,
           L0_attn_b_Wv, L0_attn_b_Wo, L0_ffn_A, L0_ffn_B, L0_ffn_Wr, L0_ffn_br):
    global _NC_CACHE
    x = np.asarray(L0_attn_a_minibatch, dtype=np.float32)
    eta = np.asarray(L0_attn_a_eta, dtype=np.float32).reshape(1)
    Wq = np.asarray(L0_attn_b_Wq, dtype=np.float32)
    Wc = np.asarray(L0_attn_b_Wc, dtype=np.float32)
    Wk = np.asarray(L0_attn_b_Wk, dtype=np.float32)
    Wv = np.asarray(L0_attn_b_Wv, dtype=np.float32)
    Wo = np.asarray(L0_attn_b_Wo, dtype=np.float32)
    A_ = np.asarray(L0_ffn_A, dtype=np.float32)
    B_ = np.asarray(L0_ffn_B, dtype=np.float32)
    Wr = np.asarray(L0_ffn_Wr, dtype=np.float32)
    br = np.asarray(L0_ffn_br, dtype=np.float32)

    wq_dev = np.ascontiguousarray(Wq.reshape(DC, 128, H, DH).transpose(2, 1, 0, 3).reshape(H, 128, DC * 128))
    wc_dev = np.ascontiguousarray(Wc.reshape(DC, 128, DL).transpose(1, 0, 2))
    wk_dev = np.ascontiguousarray(Wk.reshape(4, 128, H, DH).transpose(2, 1, 0, 3).reshape(H, 128, 4 * 128))
    wv_dev = np.ascontiguousarray(Wv.reshape(4, 128, 16, 256).transpose(2, 1, 0, 3).reshape(16, 128, 4 * 256))
    wo_dev = np.ascontiguousarray(Wo.reshape(DC, 128, 8, 512).transpose(2, 1, 0, 3).reshape(8, 128, DC * 512))
    at_dev = np.ascontiguousarray(A_.reshape(128, D).T.reshape(DC, 128, 128).transpose(1, 0, 2))
    bc_dev = np.ascontiguousarray(B_.transpose(0, 2, 1).reshape(128, D))
    wr_dev = np.ascontiguousarray(Wr.T.reshape(DC, 128, NE).transpose(1, 0, 2))
    br_dev = np.ascontiguousarray(np.broadcast_to(br[None, :], (128, NE)))
    eta_dev = np.ascontiguousarray(np.broadcast_to(eta[None, :], (128, 1)))

    inv = 1.0 / 10000.0 ** (np.arange(0, DH, 2, dtype=np.float64) / DH)
    pos = np.arange(S, dtype=np.float64)
    fr = pos[:, None] * inv[None, :]
    emb = np.concatenate([fr, fr], axis=-1)
    cosT = np.ascontiguousarray(np.cos(emb).T.astype(np.float32))
    sinT = np.ascontiguousarray(np.sin(emb).T.astype(np.float32))

    if _NC_CACHE is None:
        _NC_CACHE = build_program(UPTO)
    nc = _NC_CACHE

    in_maps = []
    for core in range(8):
        b = core // 4
        s0 = (core % 4) * QO
        in_maps.append({
            "xf": np.ascontiguousarray(x[b]),
            "xo": np.ascontiguousarray(x[b, s0:s0 + QO]),
            "wq": wq_dev, "wc": wc_dev, "wk": wk_dev, "wv": wv_dev, "wo": wo_dev,
            "at_": at_dev, "bc_": bc_dev, "wr_": wr_dev, "brt": br_dev, "etat": eta_dev,
            "cose": cosT, "sine": sinT,
            "cosq": np.ascontiguousarray(cosT[:, s0:s0 + QO]),
            "sinq": np.ascontiguousarray(sinT[:, s0:s0 + QO]),
        })
    res = run_bass_kernel_spmd(nc, in_maps, core_ids=list(range(8)))
    out = np.empty((2, S, D), dtype=np.float32)
    for core in range(8):
        b = core // 4
        s0 = (core % 4) * QO
        out[b, s0:s0 + QO] = res.results[core]["y"]
    return out


# version 02
# speedup vs baseline: 1.0980x; 1.0980x over previous
import numpy as np
import ml_dtypes
import concourse.bass as bass
import concourse.mybir as mybir
from concourse.bass_utils import run_bass_kernel_spmd

F32 = mybir.dt.float32
F16 = mybir.dt.float16
BF16 = mybir.dt.bfloat16
AF = mybir.ActivationFunctionType
ALU = mybir.AluOpType
AX = mybir.AxisListType

D = 4096
S = 4096
QO = 1024
NT = 32
NQ = 8
DC = 32
DL = 512
H = 32
DH = 128
NE = 8
ENGS = ("sync", "scalar", "vector", "tensor", "gpsimd")


AUTOSEM = {"scalar": "eS", "vector": "eV", "tensor": "eT"}


class Plan:
    def __init__(self):
        self.ops = {e: [] for e in ENGS}
        self.cnt = {}

    def emit(self, eng, fn, inc=None, waits=()):
        s = AUTOSEM.get(eng)
        if s is not None and fn is not None:
            waits = tuple(waits) + ((s, self.cnt[s]),)
        self.ops[eng].append((tuple(waits), fn, inc))
        if inc is not None:
            self.cnt[inc[0]] += inc[1]

    def sem(self, name):
        self.cnt[name] = 0

    def snap(self, sem):
        return self.cnt[sem]


def build_program(upto=99, p1n=40, p1trans=True):
    nc = bass.Bass("TRN2", target_bir_lowering=False)

    xf = nc.dram_tensor("xf", [S, D], F32, kind="ExternalInput")
    xo = nc.dram_tensor("xo", [QO, D], F32, kind="ExternalInput")
    wq = nc.dram_tensor("wq", [H, 128, DC * 128], F32, kind="ExternalInput")
    wc = nc.dram_tensor("wc", [128, DC, DL], F32, kind="ExternalInput")
    wk = nc.dram_tensor("wk", [H, 128, 4 * 128], F32, kind="ExternalInput")
    wv = nc.dram_tensor("wv", [16, 128, 4 * 256], F32, kind="ExternalInput")
    wo = nc.dram_tensor("wo", [8, 128, DC * 512], F32, kind="ExternalInput")
    at_ = nc.dram_tensor("at_", [128, DC, 128], F32, kind="ExternalInput")
    bc_ = nc.dram_tensor("bc_", [128, D], F32, kind="ExternalInput")
    wr_ = nc.dram_tensor("wr_", [128, DC, NE], F32, kind="ExternalInput")
    brt = nc.dram_tensor("brt", [128, NE], F32, kind="ExternalInput")
    etat = nc.dram_tensor("etat", [128, 1], F32, kind="ExternalInput")
    cose = nc.dram_tensor("cose", [128, S], F32, kind="ExternalInput")
    sine = nc.dram_tensor("sine", [128, S], F32, kind="ExternalInput")
    cosq = nc.dram_tensor("cosq", [128, QO], F32, kind="ExternalInput")
    sinq = nc.dram_tensor("sinq", [128, QO], F32, kind="ExternalInput")
    y = nc.dram_tensor("y", [QO, D], F32, kind="ExternalOutput")

    m1s = nc.dram_tensor("m1s", [128, NT, D], F16)
    m1os = nc.dram_tensor("m1os", [128, NQ, D], F16)
    m1ts = nc.dram_tensor("m1ts", [128, NT, D], F16)
    m1nts = nc.dram_tensor("m1nts", [128, NQ, D], F16)
    sS = nc.dram_tensor("sS", [QO, S], F32)
    pP = nc.dram_tensor("pP", [QO, S], F16)
    x1s = nc.dram_tensor("x1s", [QO, D], F32)
    ln2s = nc.dram_tensor("ln2s", [QO, D], BF16)
    x2s = nc.dram_tensor("x2s", [QO, D], F32)
    ln3s = nc.dram_tensor("ln3s", [QO, D], BF16)
    lato = nc.dram_tensor("lato", [QO, DL], BF16)
    latf = nc.dram_tensor("latf", [S, DL], BF16)
    kts = nc.dram_tensor("kts", [H, 128, S], BF16)
    vts = nc.dram_tensor("vts", [H, 128, S], BF16)
    x3s = nc.dram_tensor("x3s", [QO, D], F32)

    import contextlib
    es = contextlib.ExitStack()
    sb = lambda *a, **k: es.enter_context(nc.sbuf_tensor(*a, **k))
    psb = lambda *a, **k: es.enter_context(nc.psum_tensor(*a, **k))

    A = sb("A", [128, 32768], F16, is_transpose_dest=True)
    B = sb("B", [128, 32768], F16)
    C = sb("C", [128, 16384], F16)
    Dn = sb("Dn", [128, 8192], F16)
    G = sb("G", [128, 12288], F16)

    def f32v(ar, off, n):
        return ar[:, off:off + 2 * n].bitcast(F32)

    def f16v(ar, off, n):
        return ar[:, off:off + n]

    def bf16v(ar, off, n):
        return ar[:, off:off + n].bitcast(BF16)

    o = 0
    def galloc(n):
        nonlocal o
        r = o
        o += n + (n & 1)
        return r
    at_sb = bf16v(G, galloc(DC * 128), DC * 128)
    bc_sb = bf16v(G, galloc(D), D)
    wr_sb = bf16v(G, galloc(DC * NE), DC * NE)
    br_sb = f32v(G, galloc(2 * NE), NE)
    eta_sb = f32v(G, galloc(2), 1)
    ones_bf = bf16v(G, galloc(2), 1)
    onesr = f32v(G, galloc(256), 128)
    cq_sb = bf16v(G, galloc(QO), QO)
    sq_sb = bf16v(G, galloc(QO), QO)
    mus = f32v(G, galloc(80), 40)
    sqs = f32v(G, galloc(80), 40)
    var_ = f32v(G, galloc(80), 40)
    nmu = f32v(G, galloc(80), 40)
    rstd = f32v(G, galloc(80), 40)
    nrstd = f32v(G, galloc(80), 40)
    nbias = f32v(G, galloc(80), 40)
    pbias = f32v(G, galloc(80), 40)
    rm = f32v(G, galloc(2 * NQ), NQ)
    nrm = f32v(G, galloc(2 * NQ), NQ)
    sums_a = f32v(G, galloc(2 * NQ), NQ)
    etars = f32v(G, galloc(2 * NQ), NQ)
    pm = f32v(G, galloc(128), 64)
    tdum = f32v(G, galloc(32), 16)
    idt = bf16v(G, galloc(128), 128)
    idt16 = f16v(G, galloc(128), 128)
    rlr = f32v(G, galloc(2 * NE), NE)
    nmx8 = f32v(G, galloc(2), 1)
    ssum8 = f32v(G, galloc(2), 1)
    sr8 = f32v(G, galloc(2), 1)
    scors = f32v(G, galloc(2 * NE), NE)
    assert o <= 12288, o

    PA = psb("PA", [128, 1024], F32)
    PB = psb("PB", [128, 1024], F32)
    PC = psb("PC", [128, 1024], F32)
    PD = psb("PD", [128, 512], F32)
    PE_ = psb("PE_", [128, 512], F32)
    pa_blocks = [PA[:, 0:512], PA[:, 512:1024], PB[:, 0:512], PB[:, 512:1024]]

    P = Plan()
    sem_objs = {}
    for name in ("g0", "g1", "g2", "g3", "g4", "g5", "g6", "g7", "s0", "s1", "s2",
                 "x0", "x1p", "x2p", "eV", "eT", "eS", "eG", "cc", "bar"):
        sem_objs[name] = es.enter_context(nc.semaphore(name))
        P.sem(name)

    emit = P.emit

    DSEMS = ("g0", "g1", "g2", "g3", "g4", "g5", "g6", "g7", "s0", "s1", "s2", "x0", "x1p", "x2p")

    def bar_all():
        k = P.snap("bar")
        for e in ENGS:
            def _inc(h):
                h.sem_inc(sem_objs["bar"], 1)
            pre = [(s_, P.snap(s_)) for s_ in DSEMS] if e == "gpsimd" else []
            emit(e, _inc, waits=pre)
            P.cnt["bar"] += 1
        tgt = k + 5
        for e in ENGS:
            emit(e, None, waits=(("bar", tgt),))

    emit("gpsimd", lambda h: h.dma_start(out=at_sb, in_=at_.ap()), inc=("g0", 16))
    emit("gpsimd", lambda h: h.dma_start(out=bc_sb, in_=bc_.ap()), inc=("g0", 16))
    emit("gpsimd", lambda h: h.dma_start(out=wr_sb, in_=wr_.ap()), inc=("g0", 16))
    emit("gpsimd", lambda h: h.dma_start(out=cq_sb, in_=cosq.ap()), inc=("g0", 16))
    emit("gpsimd", lambda h: h.dma_start(out=sq_sb, in_=sinq.ap()), inc=("g0", 16))
    emit("sync", lambda h: h.dma_start(out=br_sb, in_=brt.ap()), inc=("s0", 16))
    emit("sync", lambda h: h.dma_start(out=eta_sb, in_=etat.ap()), inc=("s0", 16))
    emit("gpsimd", lambda h: h.memset(ones_bf, 1.0), inc=("eG", 1))
    emit("gpsimd", lambda h: h.memset(onesr[0:1, :], 1.0), inc=("eG", 1))
    ident_np = np.eye(128, dtype=ml_dtypes.bfloat16)
    identt = nc.inline_tensor(ident_np, "identt")
    ident16_np = np.eye(128, dtype=np.float16)
    identt16 = nc.inline_tensor(ident16_np, "identt16")
    emit("sync", lambda h: h.dma_start(out=idt, in_=identt.ap()), inc=("s0", 16))
    emit("sync", lambda h: h.dma_start(out=idt16, in_=identt16.ap()), inc=("s0", 16))
    cfg_s0 = P.snap("s0")


    def _finish(dump=None, nrows=QO):
        if dump is not None:
            emit("gpsimd", lambda h: h.dma_start(out=y.ap()[0:nrows, :], in_=dump), inc=("g3", 16))
        emit("gpsimd", None, waits=[(s_, P.snap(s_)) for s_ in DSEMS])
        with nc.Block() as block:
            def replay(eng_name):
                def _f(h):
                    for waits, fn, inc in P.ops[eng_name]:
                        for (semname, cval) in waits:
                            if cval > 0:
                                h.wait_ge(sem_objs[semname], cval)
                        if fn is None:
                            continue
                        r = fn(h)
                        if inc is not None:
                            r.then_inc(sem_objs[inc[0]], inc[1])
                return _f
            block.sync(replay("sync"))
            block.scalar(replay("scalar"))
            block.vector(replay("vector"))
            block.tensor(replay("tensor"))
            block.gpsimd(replay("gpsimd"))
        es.close()
        return nc

    if upto == 0:
        return _finish()

    xt_bufs = [f32v(A, i * 8192, D) for i in range(4)]
    m1stag = [f16v(B, i * D, D) for i in range(2)]
    trstag = [f16v(B, 2 * D + i * D, D) for i in range(2)]
    negstag = [f16v(B, 4 * D + i * D, D) for i in range(2)]
    jnk = f16v(B, 6 * D, D)

    def ln_stats_chain(a, b, eps=1e-5):
        def _f(h):
            h.tensor_scalar(mus[:, a:b], mus[:, a:b], 1.0 / D, None, op0=ALU.mult)
            h.tensor_scalar(var_[:, a:b], sqs[:, a:b], 1.0 / D, eps, op0=ALU.mult, op1=ALU.add)
            h.drain()
            h.tensor_tensor(sqs[:, a:b], mus[:, a:b], mus[:, a:b], op=ALU.mult)
            h.tensor_scalar(nmu[:, a:b], mus[:, a:b], -1.0, None, op0=ALU.mult)
            h.drain()
            h.tensor_tensor(var_[:, a:b], var_[:, a:b], sqs[:, a:b], op=ALU.subtract)
            h.drain()
            return h.reciprocal(rstd[:, a:b], var_[:, a:b])
        return _f

    def ln_post_chain(a, b):
        def _f(h):
            h.tensor_tensor(nbias[:, a:b], nmu[:, a:b], rstd[:, a:b], op=ALU.mult)
            h.tensor_scalar(nrstd[:, a:b], rstd[:, a:b], -1.0, None, op0=ALU.mult)
            h.drain()
            return h.tensor_scalar(pbias[:, a:b], nbias[:, a:b], -1.0, None, op0=ALU.mult)
        return _f

    n_all = min(NT + NQ, p1n)
    for g in range(0, n_all, 4):
        gsem = "s1" if (g // 4) % 2 == 0 else "s2"
        gtiles = list(range(g, min(g + 4, n_all)))
        for i, t in enumerate(gtiles):
            src = xf.ap()[t * 128:(t + 1) * 128, :] if t < NT else xo.ap()[(t - NT) * 128:(t - NT + 1) * 128, :]
            w = []
            if g >= 4:
                w = [("eS", P.snap("eS")), ("eV", P.snap("eV")), ("eT", P.snap("eT"))]
            emit("sync", (lambda s_, d_: lambda h: h.dma_start(out=d_, in_=s_))(src, xt_bufs[i]),
                 inc=(gsem, 16), waits=w)
        ldc = P.snap(gsem)
        for i, t in enumerate(gtiles):
            emit("vector", (lambda i, t: lambda h: h.reduce_sum(mus[:, t:t + 1], xt_bufs[i], axis=AX.X))(i, t),
                 inc=("eV", 1), waits=[(gsem, ldc)])
            emit("scalar", (lambda i, t: lambda h: h.activation(out=jnk, in_=xt_bufs[i],
                                                               func=AF.Square, accum_out=sqs[:, t:t + 1]))(i, t),
                 inc=("eS", 1), waits=[(gsem, ldc)])
        a, b = gtiles[0], gtiles[0] + len(gtiles)
        emit("vector", ln_stats_chain(a, b), inc=("eV", 1), waits=[("eS", P.snap("eS"))])
        emit("scalar", (lambda a, b: lambda h: h.activation(out=rstd[:, a:b], in_=rstd[:, a:b], func=AF.Sqrt))(a, b),
             inc=("eS", 1), waits=[("eV", P.snap("eV"))])
        emit("vector", ln_post_chain(a, b), inc=("eV", 1), waits=[("eS", P.snap("eS"))])
        statc = P.snap("eV")
        for i, t in enumerate(gtiles):
            par = t % 2
            own = t >= NT
            wst = [("g3", P.snap("g3")), ("g4", P.snap("g4"))] if t >= 2 else []
            emit("scalar", (lambda i, t, par: lambda h: h.activation(
                out=m1stag[par], in_=xt_bufs[i], func=AF.Identity,
                bias=nbias[:, t:t + 1], scale=rstd[:, t:t + 1]))(i, t, par),
                inc=("eS", 1), waits=[("eV", statc)] + wst)
            sc_m1 = P.snap("eS")
            dstm = m1os.ap()[:, t - NT, :] if own else m1s.ap()[:, t, :]
            emit("gpsimd", (lambda par, dstm: lambda h: h.dma_start(out=dstm, in_=m1stag[par]))(par, dstm),
                 inc=("g3", 16), waits=[("eS", sc_m1)])
            if own:
                emit("scalar", (lambda i, t, par: lambda h: h.activation(
                    out=negstag[par], in_=xt_bufs[i], func=AF.Identity,
                    bias=pbias[:, t:t + 1], scale=nrstd[:, t:t + 1]))(i, t, par),
                    inc=("eS", 1))
            srcstag = negstag[par] if own else m1stag[par]
            wsrc = [("eS", P.snap("eS")), ("s0", P.snap("s0"))]
            for dc in range(DC if p1trans else 0):
                pv = PD[:, 0:64].bitcast(F16) if dc % 2 == 0 else PE_[:, 0:64].bitcast(F16)
                tw = (wsrc if dc == 0 else []) + [("eV", P.snap("eV"))]
                emit("tensor", (lambda srcstag, dc, pv: lambda h: h.transpose(
                    pv, srcstag[:, dc * 128:(dc + 1) * 128], idt16))(srcstag, dc, pv),
                    inc=("eT", 1), waits=tw)
                tc = P.snap("eT")
                cw = [("eT", tc)] + ([("g4", P.snap("g4"))] if dc == 0 else [])
                emit("vector", (lambda par, dc, pv: lambda h: h.tensor_copy(
                    trstag[par][:, dc * 128:(dc + 1) * 128], pv))(par, dc, pv),
                    inc=("eV", 1), waits=cw)
            if p1trans:
                dstt = m1nts.ap()[:, t - NT, :] if own else m1ts.ap()[:, t, :]
                emit("gpsimd", (lambda par, dstt: lambda h: h.dma_start(out=dstt, in_=trstag[par]))(par, dstt),
                     inc=("g4", 16), waits=[("eV", P.snap("eV"))])
    bar_all()

    if upto == 1:
        return _finish(m1os.ap())

    Q = f16v(A, 0, NQ * D)
    emit("sync", lambda h: h.dma_start(out=Q, in_=m1nts.ap()), inc=("s0", 16))
    qldc = P.snap("s0")
    rbuf = [f16v(B, i * (4 * D), 4 * D) for i in range(2)]

    def q_ap(qt, dc):
        off = (qt * DC + dc) * 128
        return Q[:, off:off + 128]

    def rb_ap(par, dc):
        v = rbuf[par].rearrange("p (tt dc j) -> p tt (dc j)", tt=4, dc=DC)
        return v[:, :, dc * 128:(dc + 1) * 128]

    sstag = [f32v(C, i * 1024, 512) for i in range(4)]
    sp_mm_snap = {}
    sp_st_snap = {}
    sp_ev_snap = {}
    for sp in range(8):
        par = sp % 2
        spsem = "s1" if par == 0 else "s2"
        w = [("eT", sp_mm_snap[sp - 2])] if sp >= 2 else []
        for i in range(4):
            emit("sync", (lambda sp, i, par: lambda h: h.dma_start(
                out=f16v(B, par * 4 * D + i * D, D), in_=m1ts.ap()[:, sp * 4 + i, :]))(sp, i, par),
                inc=(spsem, 16), waits=w if i == 0 else ())
        ldc = P.snap(spsem)
        for qt in range(NQ):
            idx = sp * NQ + qt
            blk = pa_blocks[idx % 4]
            wpsum = [("eV", sp_ev_snap[idx - 4])] if idx >= 4 else []
            for dc in range(DC):
                emit("tensor", (lambda qt, dc, par, blk: lambda h: h.matmul(
                    blk, q_ap(qt, dc), rb_ap(par, dc), start=(dc == 0), stop=(dc == DC - 1)))(qt, dc, par, blk),
                    inc=("eT", 1) if dc == DC - 1 else None,
                    waits=([(spsem, ldc), ("s0", qldc)] + wpsum) if dc == 0 else ())
            mmc = P.snap("eT")

            def s_evac(qt, sp, idx, blk):
                def _f(h):
                    h.tensor_copy(sstag[idx % 4], blk)
                    return h.reduce_max(pm[:, qt * 8 + sp:qt * 8 + sp + 1], blk, axis=AX.X)
                return _f
            wstore = [("x0", P.snap("x0"))] if idx >= 4 else []
            emit("vector", s_evac(qt, sp, idx, blk), inc=("eV", 1), waits=[("eT", mmc)] + wstore)
            sp_ev_snap[idx] = P.snap("eV")
            emit("scalar", (lambda qt, sp, idx: lambda h: h.dma_start(
                out=sS.ap()[qt * 128:(qt + 1) * 128, sp * 512:(sp + 1) * 512], in_=sstag[idx % 4]))(qt, sp, idx),
                inc=("x0", 16), waits=[("eV", P.snap("eV"))])
            sp_st_snap[idx] = P.snap("x0")
        sp_mm_snap[sp] = P.snap("eT")

    def rowmax_fin(h):
        for qt in range(NQ):
            h.reduce_max(rm[:, qt:qt + 1], pm[:, qt * 8:(qt + 1) * 8], axis=AX.X)
        h.drain()
        return h.tensor_scalar(nrm[:, 0:NQ], rm[:, 0:NQ], -1.0, None, op0=ALU.mult)
    emit("vector", rowmax_fin, inc=("eV", 1), waits=[("eV", P.snap("eV"))])
    bar_all()

    if upto == 2:
        return _finish(sS.ap())

    sbuf_s = [f32v(C, i * 8192, D) for i in range(2)]
    pstag = [f16v(Dn, i * D, D) for i in range(2)]
    for qt in range(NQ):
        par = qt % 2
        qsem = "s1" if par == 0 else "s2"
        w = [("eS", P.snap("eS"))] if qt >= 2 else []
        emit("sync", (lambda qt, par: lambda h: h.dma_start(
            out=sbuf_s[par], in_=sS.ap()[qt * 128:(qt + 1) * 128, :]))(qt, par),
            inc=(qsem, 16), waits=w)
        ldc = P.snap(qsem)
        wst = [("g3", P.snap("g3"))] if qt >= 2 else []
        emit("scalar", (lambda qt, par: lambda h: h.activation(
            out=pstag[par], in_=sbuf_s[par], func=AF.Exp,
            bias=nrm[:, qt:qt + 1], accum_out=sums_a[:, qt:qt + 1]))(qt, par),
            inc=("eS", 1), waits=[(qsem, ldc)] + wst)
        emit("gpsimd", (lambda qt, par: lambda h: h.dma_start(
            out=pP.ap()[qt * 128:(qt + 1) * 128, :], in_=pstag[par]))(qt, par),
            inc=("g3", 16), waits=[("eS", P.snap("eS"))])

    def recip_chain(h):
        h.reciprocal(etars[:, 0:NQ], sums_a[:, 0:NQ])
        h.drain()
        return h.tensor_scalar(etars[:, 0:NQ], etars[:, 0:NQ], eta_sb, None, op0=ALU.mult)
    emit("vector", recip_chain, inc=("eV", 1), waits=[("eS", P.snap("eS"))])
    bar_all()

    if upto == 3:
        return _finish(pP.ap())

    for c in range(DC):
        emit("sync", (lambda c: lambda h: h.dma_start_transpose(
            f16v(A, c * 1024, 1024), pP.ap()[:, c * 128:(c + 1) * 128]))(c), inc=("s0", 16))
    ptldc = P.snap("s0")
    mbuf = [f16v(B, i * DC * 512, DC * 512) for i in range(2)]
    xblk = [f32v(C, i * 1024, 512) for i in range(2)]
    m1oblk = [f16v(C, 4096 + i * 512, 512) for i in range(2)]
    xm1blk = [f32v(C, 6144 + i * 1024, 512) for i in range(2)]
    x1blk = [f32v(C, 10240 + i * 1024, 512) for i in range(2)]
    ob_mm_snap = {}
    for ob in range(8):
        par = ob % 2
        obsem = "s1" if par == 0 else "s2"
        w = [("eT", ob_mm_snap[ob - 2])] if ob >= 2 else []
        emit("sync", (lambda ob, par: lambda h: h.dma_start(
            out=mbuf[par], in_=m1s.ap()[:, :, ob * 512:(ob + 1) * 512]))(ob, par),
            inc=(obsem, 16), waits=w)
        mldc = P.snap(obsem)
        for qt in range(NQ):
            idx = ob * NQ + qt
            bpar = idx % 2
            blk = pa_blocks[idx % 2]
            wblk = [("eV", P.snap("eV"))] if idx >= 2 else []
            bsem = "x1p" if bpar == 0 else "x2p"
            emit("scalar", (lambda qt, ob, bpar: lambda h: h.dma_start(
                out=xblk[bpar], in_=xo.ap()[qt * 128:(qt + 1) * 128, ob * 512:(ob + 1) * 512]))(qt, ob, bpar),
                inc=(bsem, 16), waits=wblk)
            emit("scalar", (lambda qt, ob, bpar: lambda h: h.dma_start(
                out=m1oblk[bpar], in_=m1os.ap()[:, qt, ob * 512:(ob + 1) * 512]))(qt, ob, bpar),
                inc=(bsem, 16))
            blkldc = P.snap(bsem)
            wpsum = [("eV", P.snap("eV"))] if idx >= 2 else []
            for c in range(DC):
                emit("tensor", (lambda qt, ob, c, par, blk: lambda h: h.matmul(
                    blk, f16v(A, c * 1024 + qt * 128, 128), mbuf[par][:, c * 512:(c + 1) * 512],
                    start=(c == 0), stop=(c == DC - 1)))(qt, ob, c, par, blk),
                    inc=("eT", 1) if c == DC - 1 else None,
                    waits=([(obsem, mldc), ("s0", ptldc)] + wpsum) if c == 0 else ())
            mmc = P.snap("eT")

            def evac(qt, bpar, blk):
                def _f(h):
                    h.tensor_tensor(xm1blk[bpar], xblk[bpar], m1oblk[bpar], op=ALU.add)
                    h.drain()
                    return h.scalar_tensor_tensor(x1blk[bpar], blk, etars[:, qt:qt + 1], xm1blk[bpar],
                                                  op0=ALU.mult, op1=ALU.add)
                return _f
            emit("vector", evac(qt, bpar, blk), inc=("eV", 1),
                 waits=[("eT", mmc), (bsem, blkldc), ("g3", P.snap("g3"))])
            emit("gpsimd", (lambda qt, ob, bpar: lambda h: h.dma_start(
                out=x1s.ap()[qt * 128:(qt + 1) * 128, ob * 512:(ob + 1) * 512], in_=x1blk[bpar]))(qt, ob, bpar),
                inc=("g3", 16), waits=[("eV", P.snap("eV"))])
        ob_mm_snap[ob] = P.snap("eT")
    bar_all()

    if upto == 4:
        return _finish(x1s.ap())

    def ffn_phase(xin, lnmid, lnout, base_t, to_output=False, xout=None):
        xtb = [f32v(B, i * 8192, D) for i in range(2)]
        lstag = [bf16v(B, 16384 + i * D, D) for i in range(2)]
        fjnk = f16v(B, 24576, D)
        for g in range(0, NQ, 2):
            gsem = "s1" if (g // 2) % 2 == 0 else "s2"
            for i, t in enumerate((g, g + 1)):
                w = [("eS", P.snap("eS")), ("eV", P.snap("eV"))] if g >= 2 else []
                emit("sync", (lambda t, i: lambda h: h.dma_start(
                    out=xtb[i], in_=xin.ap()[t * 128:(t + 1) * 128, :]))(t, i),
                    inc=(gsem, 16), waits=w)
            ldc = P.snap(gsem)
            for i, t in enumerate((g, g + 1)):
                st = base_t + t
                emit("vector", (lambda i, st: lambda h: h.reduce_sum(mus[:, st:st + 1], xtb[i], axis=AX.X))(i, st),
                     inc=("eV", 1), waits=[(gsem, ldc)])
                emit("scalar", (lambda i, st: lambda h: h.activation(
                    out=fjnk, in_=xtb[i], func=AF.Square, accum_out=sqs[:, st:st + 1]))(i, st),
                    inc=("eS", 1), waits=[(gsem, ldc)])
            a, b = base_t + g, base_t + g + 2
            emit("vector", ln_stats_chain(a, b), inc=("eV", 1), waits=[("eS", P.snap("eS"))])
            emit("scalar", (lambda a, b: lambda h: h.activation(out=rstd[:, a:b], in_=rstd[:, a:b], func=AF.Sqrt))(a, b),
                 inc=("eS", 1), waits=[("eV", P.snap("eV"))])
            emit("vector", ln_post_chain(a, b), inc=("eV", 1), waits=[("eS", P.snap("eS"))])
            statc = P.snap("eV")
            for i, t in enumerate((g, g + 1)):
                st = base_t + t
                wst = [("g3", P.snap("g3"))] if g >= 2 else []
                emit("scalar", (lambda i, t, st: lambda h: h.activation(
                    out=lstag[i], in_=xtb[i], func=AF.Identity,
                    bias=nbias[:, st:st + 1], scale=rstd[:, st:st + 1]))(i, t, st),
                    inc=("eS", 1), waits=[("eV", statc)] + wst)
                emit("gpsimd", (lambda i, t: lambda h: h.dma_start(
                    out=lnmid.ap()[t * 128:(t + 1) * 128, :], in_=lstag[i]))(i, t),
                    inc=("g3", 16), waits=[("eS", P.snap("eS"))])
        bar_all()
        for c in range(DC):
            emit("sync", (lambda c: lambda h: h.dma_start_transpose(
                bf16v(A, c * 1024, 1024), lnmid.ap()[:, c * 128:(c + 1) * 128]))(c), inc=("s0", 16))
        lnldc = P.snap("s0")
        x2stag = [f32v(B, i * 8192, D) for i in range(2)]
        lnstag2 = [bf16v(B, 16384 + i * D, D) for i in range(2)]
        lowt = [bf16v(Dn, i * 256, 128) for i in range(2)]
        lostag = [bf16v(Dn, 512 + i * 128, 128) for i in range(2)]
        for qt in range(NQ):
            par = qt % 2
            wr_w = [("eV", P.snap("eV"))] if qt >= 1 else []
            for c in range(DC):
                emit("tensor", (lambda qt, c: lambda h: h.matmul(
                    PD[:, 0:NE], bf16v(A, c * 1024 + qt * 128, 128), wr_sb[:, c * NE:(c + 1) * NE],
                    start=(c == 0), stop=(c == DC - 1)))(qt, c),
                    inc=("eT", 1) if c == DC - 1 else None,
                    waits=([("s0", lnldc), ("g0", P.snap("g0"))] + wr_w) if c == 0 else ())
            rc = P.snap("eT")

            def router_sm(h):
                h.tensor_tensor(rlr[:, 0:NE], PD[:, 0:NE], br_sb, op=ALU.add)
                h.drain()
                return h.reduce_max(nmx8, rlr[:, 0:NE], axis=AX.X, negate=True)
            emit("vector", router_sm, inc=("eV", 1), waits=[("eT", rc)])
            emit("scalar", lambda h: h.activation(out=scors[:, 0:NE], in_=rlr[:, 0:NE], func=AF.Exp,
                                                  bias=nmx8, accum_out=ssum8),
                 inc=("eS", 1), waits=[("eV", P.snap("eV"))])

            def router_fin(h):
                h.reciprocal(sr8, ssum8)
                h.drain()
                return h.tensor_scalar(scors[:, 0:NE], scors[:, 0:NE], sr8, 8.0, op0=ALU.mult, op1=ALU.mult)
            emit("vector", router_fin, inc=("eV", 1), waits=[("eS", P.snap("eS"))])
            rfin = P.snap("eV")
            for c in range(DC):
                emit("tensor", (lambda qt, c: lambda h: h.matmul(
                    PE_[:, 0:128], bf16v(A, c * 1024 + qt * 128, 128), at_sb[:, c * 128:(c + 1) * 128],
                    start=(c == 0), stop=(c == DC - 1)))(qt, c),
                    inc=("eT", 1) if c == DC - 1 else None,
                    waits=[("eV", rfin)] if c == 0 else ())
            lc = P.snap("eT")

            def low_evac(par):
                def _f(h):
                    r = None
                    for e in range(NE):
                        r = h.tensor_scalar(lowt[par][:, e * 16:(e + 1) * 16],
                                            PE_[:, e * 16:(e + 1) * 16], scors[:, e:e + 1], None, op0=ALU.mult)
                    return r
                return _f
            emit("vector", low_evac(par), inc=("eV", 1), waits=[("eT", lc)])
            lec = P.snap("eV")
            emit("tensor", (lambda par: lambda h: h.transpose(
                PD[:, 64:128].bitcast(BF16), lowt[par], idt))(par),
                inc=("eT", 1), waits=[("eV", lec), ("s0", cfg_s0)])
            emit("scalar", (lambda par: lambda h: h.copy(lostag[par], PD[:, 64:128].bitcast(BF16)))(par),
                 inc=("eS", 1), waits=[("eT", P.snap("eT"))])
            ltc = P.snap("eS")
            xsem = "s1" if par == 0 else "s2"
            emit("sync", (lambda qt, par: lambda h: h.dma_start(
                out=x2stag[par], in_=xin.ap()[qt * 128:(qt + 1) * 128, :]))(qt, par),
                inc=(xsem, 16), waits=[("g4", P.snap("g4"))] if qt >= 2 else [])
            xldc = P.snap(xsem)
            for ob in range(8):
                pb = pa_blocks[ob % 2]
                wv_ = [("eV", P.snap("eV"))] if ob >= 2 else []
                emit("tensor", (lambda par, ob, pb: lambda h: h.matmul(
                    pb, lostag[par], bc_sb[:, ob * 512:(ob + 1) * 512], start=True, stop=True))(par, ob, pb),
                    inc=("eT", 1), waits=[("eS", ltc)] + wv_)
                emit("vector", (lambda par, ob, pb: lambda h: h.scalar_tensor_tensor(
                    x2stag[par][:, ob * 512:(ob + 1) * 512], pb, 2.0,
                    x2stag[par][:, ob * 512:(ob + 1) * 512], op0=ALU.mult, op1=ALU.add))(par, ob, pb),
                    inc=("eV", 1), waits=[("eT", P.snap("eT")), (xsem, xldc)])
            xec = P.snap("eV")
            dst = y.ap() if to_output else xout.ap()
            emit("gpsimd", (lambda qt, par, dst: lambda h: h.dma_start(
                out=dst[qt * 128:(qt + 1) * 128, :], in_=x2stag[par]))(qt, par, dst),
                inc=("g4", 16), waits=[("eV", xec)])
            if lnout is not None:
                st = base_t + NQ + qt
                emit("vector", (lambda par, st: lambda h: h.reduce_sum(mus[:, st:st + 1], x2stag[par], axis=AX.X))(par, st),
                     inc=("eV", 1), waits=[("eV", xec)])
                emit("scalar", (lambda par, st: lambda h: h.activation(
                    out=fjnk, in_=x2stag[par], func=AF.Square, accum_out=sqs[:, st:st + 1]))(par, st),
                    inc=("eS", 1), waits=[("eV", xec)])
                emit("vector", ln_stats_chain(st, st + 1), inc=("eV", 1), waits=[("eS", P.snap("eS"))])
                emit("scalar", (lambda st: lambda h: h.activation(out=rstd[:, st:st + 1], in_=rstd[:, st:st + 1],
                                                                  func=AF.Sqrt))(st),
                     inc=("eS", 1), waits=[("eV", P.snap("eV"))])
                emit("vector", ln_post_chain(st, st + 1), inc=("eV", 1), waits=[("eS", P.snap("eS"))])
                emit("scalar", (lambda par, st: lambda h: h.activation(
                    out=lnstag2[par], in_=x2stag[par], func=AF.Identity,
                    bias=nbias[:, st:st + 1], scale=rstd[:, st:st + 1]))(par, st),
                    inc=("eS", 1), waits=[("eV", P.snap("eV"))])
                emit("gpsimd", (lambda qt, par: lambda h: h.dma_start(
                    out=lnout.ap()[qt * 128:(qt + 1) * 128, :], in_=lnstag2[par]))(qt, par),
                    inc=("g3", 16), waits=[("eS", P.snap("eS"))])
        bar_all()

    ffn_phase(x1s, ln2s, ln3s, base_t=0, xout=x2s)

    if upto == 5:
        return _finish(x2s.ap())

    for c in range(DC):
        emit("sync", (lambda c: lambda h: h.dma_start_transpose(
            bf16v(A, c * 1024, 1024), ln3s.ap()[:, c * 128:(c + 1) * 128]))(c), inc=("s0", 16))
    l3ldc = P.snap("s0")
    wc_sb = bf16v(C, 0, DC * 512)
    emit("gpsimd", lambda h: h.dma_start(out=wc_sb, in_=wc.ap()), inc=("g7", 16))
    wcldc = P.snap("g7")
    latstag = [bf16v(Dn, i * 512, 512) for i in range(2)]
    for qt in range(NQ):
        par = qt % 2
        pb = pa_blocks[qt % 2]
        w = [("eV", P.snap("eV"))] if qt >= 2 else []
        for c in range(DC):
            emit("tensor", (lambda qt, c, pb: lambda h: h.matmul(
                pb, bf16v(A, c * 1024 + qt * 128, 128), wc_sb[:, c * 512:(c + 1) * 512],
                start=(c == 0), stop=(c == DC - 1)))(qt, c, pb),
                inc=("eT", 1) if c == DC - 1 else None,
                waits=([("s0", l3ldc), ("g7", wcldc)] + w) if c == 0 else ())
        wv2 = [("g3", P.snap("g3"))] if qt >= 2 else []
        emit("vector", (lambda par, pb: lambda h: h.tensor_copy(latstag[par], pb))(par, pb),
             inc=("eV", 1), waits=[("eT", P.snap("eT"))] + wv2)
        emit("gpsimd", (lambda qt, par: lambda h: h.dma_start(
            out=lato.ap()[qt * 128:(qt + 1) * 128, :], in_=latstag[par]))(qt, par),
            inc=("g3", 16), waits=[("eV", P.snap("eV"))])
    emit("gpsimd", lambda h: h.collective_compute(
        "AllGather", ALU.bypass, replica_groups=[[0, 1, 2, 3], [4, 5, 6, 7]],
        ins=[lato.ap().opt()], outs=[latf.ap().opt()]),
        inc=("cc", 1), waits=[("g3", P.snap("g3"))])
    p6_eT = P.snap("eT")
    p6_eV = P.snap("eV")

    if upto == 6:
        emit("gpsimd", None, waits=[("cc", P.snap("cc"))])
        return _finish(latf.ap(), nrows=512)

    qT = bf16v(B, 0, H * 1024)
    wqb = [bf16v(C, i * D, D) for i in range(2)]
    ro1 = f32v(Dn, 2048, 512)
    ro2 = f32v(Dn, 3072, 512)
    ro3 = f32v(Dn, 4096, 512)
    ro4 = f32v(Dn, 5120, 512)
    h_mm_snap = {}
    for h_ in range(H):
        par = h_ % 2
        hsem = "g1" if par == 0 else "g2"
        w = [("eT", h_mm_snap[h_ - 2])] if h_ >= 2 else [("eT", p6_eT), ("eV", p6_eV)]
        emit("gpsimd", (lambda h_, par: lambda h: h.dma_start(out=wqb[par], in_=wq.ap()[h_, :, :]))(h_, par),
             inc=(hsem, 16), waits=w)
        wldc = P.snap(hsem)
        for tb in range(2):
            pb = pa_blocks[(h_ * 2 + tb) % 2]
            wv_ = [("eV", P.snap("eV"))]
            for c in range(DC):
                emit("tensor", (lambda par, c, tb, pb: lambda h: h.matmul(
                    pb, wqb[par][:, c * 128:(c + 1) * 128], bf16v(A, c * 1024 + tb * 512, 512),
                    start=(c == 0), stop=(c == DC - 1)))(par, c, tb, pb),
                    inc=("eT", 1) if c == DC - 1 else None,
                    waits=([(hsem, wldc)] + wv_) if c == 0 else ())
            mmc = P.snap("eT")

            def rope_q(h_, tb, pb):
                lo, hi = tb * 512, (tb + 1) * 512
                dst = qT[:, h_ * 1024 + lo:h_ * 1024 + hi]

                def _f(h):
                    h.tensor_tensor(ro1[0:64, :], pb[0:64, :], cq_sb[0:64, lo:hi], op=ALU.mult)
                    h.tensor_tensor(ro2[0:64, :], pb[64:128, :], sq_sb[0:64, lo:hi], op=ALU.mult)
                    h.tensor_tensor(ro3[0:64, :], pb[64:128, :], cq_sb[64:128, lo:hi], op=ALU.mult)
                    h.tensor_tensor(ro4[0:64, :], pb[0:64, :], sq_sb[64:128, lo:hi], op=ALU.mult)
                    h.drain()
                    h.tensor_tensor(dst[0:64, :], ro1[0:64, :], ro2[0:64, :], op=ALU.subtract)
                    return h.tensor_tensor(dst[64:128, :], ro3[0:64, :], ro4[0:64, :], op=ALU.add)
                return _f
            emit("vector", rope_q(h_, tb, pb), inc=("eV", 1), waits=[("eT", mmc)])
        h_mm_snap[h_] = P.snap("eT")
    bar_all()

    if upto == 7:
        bar_all()
        return _finish(qT)

    latT = bf16v(A, 0, 4 * S)
    emit("sync", None, waits=[("cc", P.snap("cc"))])
    for lc in range(4):
        emit("sync", (lambda lc: lambda h: h.dma_start_transpose(
            bf16v(A, lc * S, S), latf.ap()[:, lc * 128:(lc + 1) * 128]))(lc), inc=("s0", 16))
    latldc = P.snap("s0")
    ce_sb = bf16v(A, 4 * S, S)
    se_sb = bf16v(A, 5 * S, S)
    emit("gpsimd", lambda h: h.dma_start(out=ce_sb, in_=cose.ap()), inc=("g7", 16))
    emit("gpsimd", lambda h: h.dma_start(out=se_sb, in_=sine.ap()), inc=("g7", 16))
    tbldc = P.snap("g7")
    wkb = [bf16v(C, i * 512, 512) for i in range(2)]
    wvb = [bf16v(C, 1024 + i * 1024, 1024) for i in range(2)]
    kstag = [bf16v(C, 3072 + i * 512, 512) for i in range(2)]
    vstag0 = bf16v(A, 6 * S, DC * 128)
    vstag1 = bf16v(A, 7 * S, DC * 128)
    hk_mm_snap = {}
    for h_ in range(H):
        par = h_ % 2
        hsem = "g1" if par == 0 else "g2"
        w = [("eT", hk_mm_snap[h_ - 2])] if h_ >= 2 else []
        emit("gpsimd", (lambda h_, par: lambda h: h.dma_start(out=wkb[par], in_=wk.ap()[h_, :, :]))(h_, par),
             inc=(hsem, 16), waits=w)
        wldc = P.snap(hsem)
        for kb in range(8):
            pb = PD[:, 0:512] if kb % 2 == 0 else PE_[:, 0:512]
            wv_ = [("eV", P.snap("eV"))]
            for lcc in range(4):
                emit("tensor", (lambda par, lcc, kb, pb: lambda h: h.matmul(
                    pb, wkb[par][:, lcc * 128:(lcc + 1) * 128], bf16v(A, lcc * S + kb * 512, 512),
                    start=(lcc == 0), stop=(lcc == 3)))(par, lcc, kb, pb),
                    inc=("eT", 1) if lcc == 3 else None,
                    waits=([(hsem, wldc), ("g7", tbldc), ("s0", latldc)] + wv_) if lcc == 0 else ())
            mmc = P.snap("eT")

            def rope_k(par, kb, pb):
                lo, hi = kb * 512, (kb + 1) * 512
                dst = kstag[par]

                def _f(h):
                    h.tensor_tensor(ro1[0:64, :], pb[0:64, :], ce_sb[0:64, lo:hi], op=ALU.mult)
                    h.tensor_tensor(ro2[0:64, :], pb[64:128, :], se_sb[0:64, lo:hi], op=ALU.mult)
                    h.tensor_tensor(ro3[0:64, :], pb[64:128, :], ce_sb[64:128, lo:hi], op=ALU.mult)
                    h.tensor_tensor(ro4[0:64, :], pb[0:64, :], se_sb[64:128, lo:hi], op=ALU.mult)
                    h.drain()
                    h.tensor_tensor(dst[0:64, :], ro1[0:64, :], ro2[0:64, :], op=ALU.subtract)
                    return h.tensor_tensor(dst[64:128, :], ro3[0:64, :], ro4[0:64, :], op=ALU.add)
                return _f
            wkst = [("x0", P.snap("x0"))] if (h_ * 8 + kb) >= 2 else []
            emit("vector", rope_k(par, kb, pb), inc=("eV", 1), waits=[("eT", mmc)] + wkst)
            emit("scalar", (lambda h_, kb, par: lambda h: h.dma_start(
                out=kts.ap()[h_, :, kb * 512:(kb + 1) * 512], in_=kstag[par]))(h_, kb, par),
                inc=("x0", 16), waits=[("eV", P.snap("eV"))])
        if h_ % 2 == 0:
            gidx = h_ // 2
            vsem = "g5" if gidx % 2 == 0 else "g6"
            emit("gpsimd", (lambda gidx, par: lambda h: h.dma_start(out=wvb[par], in_=wv.ap()[gidx, :, :]))(gidx, par),
                 inc=(vsem, 16), waits=[("eT", P.snap("eT"))] if gidx >= 1 else [("eG", P.snap("eG"))])
            wvldc = P.snap(vsem)
            for kt in range(DC):
                pb2 = PC[:, 0:256] if kt % 2 == 0 else PC[:, 512:768]
                wv_ = [("eV", P.snap("eV"))]
                for lcc in range(4):
                    emit("tensor", (lambda kt, lcc, par, pb2: lambda h: h.matmul(
                        pb2, bf16v(A, lcc * S + kt * 128, 128),
                        wvb[par][:, lcc * 256:(lcc + 1) * 256], start=(lcc == 0), stop=(lcc == 3)))(kt, lcc, par, pb2),
                        inc=("eT", 1) if lcc == 3 else None,
                        waits=([(vsem, wvldc)] + wv_) if lcc == 0 else ())
                mmc = P.snap("eT")

                def v_evac(kt, pb2):
                    def _f(h):
                        h.tensor_copy(vstag0[:, kt * 128:(kt + 1) * 128], pb2[:, 0:128])
                        return h.tensor_copy(vstag1[:, kt * 128:(kt + 1) * 128], pb2[:, 128:256])
                    return _f
                wvs = [("g4", P.snap("g4"))] if kt < 2 and gidx >= 1 else []
                emit("vector", v_evac(kt, pb2), inc=("eV", 1), waits=[("eT", mmc)] + wvs)
            vec = P.snap("eV")
            emit("gpsimd", (lambda h_: lambda h: h.dma_start(out=vts.ap()[h_, :, :], in_=vstag0))(h_),
                 inc=("g4", 16), waits=[("eV", vec)])
            emit("gpsimd", (lambda h_: lambda h: h.dma_start(out=vts.ap()[h_ + 1, :, :], in_=vstag1))(h_),
                 inc=("g4", 16))
        hk_mm_snap[h_] = P.snap("eT")
    bar_all()

    if upto == 8:
        return _finish(kts.ap()[0:8, :, :])

    kbuf = [bf16v(C, i * S, S) for i in range(2)]
    vbuf = [bf16v(C, 2 * S + i * S, S) for i in range(2)]
    ptb = [bf16v(Dn, i * 1024, 1024) for i in range(4)]
    drow = f32v(Dn, 4096, 1024)
    rrow = f32v(Dn, 6144, 1024)
    rbs = f32v(A, 0, 1024)
    sc18 = float(1.0 / np.sqrt(128.0))
    prev_fin_eV = P.snap("eV")
    ha_mm_snap = {}
    for h_ in range(H):
        par = h_ % 2
        hsem = "s1" if par == 0 else "s2"
        wld = [("eT", ha_mm_snap[h_ - 2])] if h_ >= 2 else []
        emit("sync", (lambda h_, par: lambda h: h.dma_start(out=kbuf[par], in_=kts.ap()[h_, :, :]))(h_, par),
             inc=(hsem, 16), waits=wld)
        emit("sync", (lambda h_, par: lambda h: h.dma_start(out=vbuf[par], in_=vts.ap()[h_, :, :]))(h_, par),
             inc=(hsem, 16))
        kvldc = P.snap(hsem)
        def emit_scores(c):
            spb = PA[:, 0:1024] if c % 2 == 0 else PB[:, 0:1024]
            wsc = [("eS", pt_ready[c - 2])] if c >= 2 else []
            if c == 0:
                wsc = wsc + [("eV", prev_fin_eV)]
            for qb in range(2):
                emit("tensor", (lambda h_, c, qb, par, spb: lambda h: h.matmul(
                    spb[:, qb * 512:(qb + 1) * 512], kbuf[par][:, c * 128:(c + 1) * 128],
                    qT[:, h_ * 1024 + qb * 512:h_ * 1024 + (qb + 1) * 512],
                    start=True, stop=True))(h_, c, qb, par, spb),
                    inc=("eT", 1),
                    waits=([(hsem, kvldc)] + wsc) if qb == 0 else ())
            mmc = P.snap("eT")
            wpt = [("eT", P.snap("eT"))] if c >= 4 else []
            emit("scalar", (lambda c, spb: lambda h: h.activation(
                out=ptb[c % 4], in_=spb[:, 0:1024], func=AF.Exp, scale=sc18))(c, spb),
                inc=("eS", 1), waits=[("eT", mmc)] + wpt)

        def emit_pv(c):
            ec = pt_ready[c]
            for qb in range(2):
                emit("tensor", (lambda c, qb, par: lambda h: h.matmul(
                    PC[:, qb * 512:(qb + 1) * 512], vbuf[par][:, c * 128:(c + 1) * 128],
                    ptb[c % 4][:, qb * 512:(qb + 1) * 512],
                    start=(c == 0), stop=(c == DC - 1)))(c, qb, par),
                    inc=("eT", 1), waits=[("eS", ec)] if qb == 0 else ())
                emit("tensor", (lambda c, qb: lambda h: h.matmul(
                    (PD if qb == 0 else PE_)[0:1, :], ones_bf,
                    ptb[c % 4][:, qb * 512:(qb + 1) * 512],
                    start=(c == 0), stop=(c == DC - 1)))(c, qb),
                    inc=("eT", 1))

        pt_ready = {}
        emit_scores(0)
        pt_ready[0] = P.snap("eS")
        for c in range(1, DC):
            emit_scores(c)
            pt_ready[c] = P.snap("eS")
            emit_pv(c - 1)
        emit_pv(DC - 1)
        pvc = P.snap("eT")

        def dfin(h):
            h.tensor_copy(drow[0:1, 0:512], PD[0:1, :])
            h.tensor_copy(drow[0:1, 512:1024], PE_[0:1, :])
            h.drain()
            return h.reciprocal(rrow[0:1, :], drow[0:1, :])
        emit("vector", dfin, inc=("eV", 1), waits=[("eT", pvc)])
        dc_ = P.snap("eV")
        for qb in range(2):
            emit("tensor", (lambda qb: lambda h: h.matmul(
                PA[:, qb * 512:(qb + 1) * 512], onesr[0:1, :],
                rrow[0:1, qb * 512:(qb + 1) * 512], start=True, stop=True))(qb),
                inc=("eT", 1), waits=[("eV", dc_)] if qb == 0 else ())
        bcc = P.snap("eT")

        def ofin(h_):
            def _f(h):
                h.tensor_copy(rbs, PA[:, 0:1024])
                h.drain()
                return h.tensor_tensor(qT[:, h_ * 1024:(h_ + 1) * 1024], PC[:, 0:1024], rbs, op=ALU.mult)
            return _f
        emit("vector", ofin(h_), inc=("eV", 1), waits=[("eT", bcc)])
        prev_fin_eV = P.snap("eV")
        ha_mm_snap[h_] = P.snap("eT")
    bar_all()

    if upto == 9:
        return _finish(qT)

    wob = [bf16v(C, i * DC * 256, DC * 256) for i in range(2)]
    x2blk = [f32v(A, i * 512, 256) for i in range(2)]
    x3blk = [f32v(A, 1024 + i * 512, 256) for i in range(2)]
    obh_mm_snap = {}
    for obh in range(16):
        par = obh % 2
        ob, half = obh // 2, obh % 2
        osem = "g1" if par == 0 else "g2"
        w = [("eT", obh_mm_snap[obh - 2])] if obh >= 2 else []
        wsrc = wo.ap()[obh // 2, :, :].rearrange("p (ec j) -> p ec j", ec=DC)[:, :, (obh % 2) * 256:(obh % 2 + 1) * 256]
        emit("gpsimd", (lambda wsrc, par: lambda h: h.dma_start(out=wob[par], in_=wsrc))(wsrc, par),
             inc=(osem, 16), waits=w)
        wldc = P.snap(osem)
        for qt in range(NQ):
            idx9 = obh * NQ + qt
            bpar = idx9 % 2
            pb = pa_blocks[bpar]
            wblk = [("eV", P.snap("eV"))]
            xsem2 = "x1p" if bpar == 0 else "x2p"
            emit("scalar", (lambda qt, ob, half, bpar: lambda h: h.dma_start(
                out=x2blk[bpar], in_=x2s.ap()[qt * 128:(qt + 1) * 128,
                                              ob * 512 + half * 256:ob * 512 + half * 256 + 256]))(qt, ob, half, bpar),
                inc=(xsem2, 16), waits=wblk)
            xldc = P.snap(xsem2)
            for c in range(DC):
                emit("tensor", (lambda qt, c, par, pb: lambda h: h.matmul(
                    pb[:, 0:256], qT[:, c * 1024 + qt * 128:c * 1024 + qt * 128 + 128],
                    wob[par][:, c * 256:(c + 1) * 256],
                    start=(c == 0), stop=(c == DC - 1)))(qt, c, par, pb),
                    inc=("eT", 1) if c == DC - 1 else None,
                    waits=([(osem, wldc)] + ([("eV", P.snap("eV"))] if obh * NQ + qt >= 2 else [])) if c == 0 else ())
            mmc = P.snap("eT")
            emit("vector", (lambda bpar, pb: lambda h: h.tensor_tensor(
                x3blk[bpar], pb[:, 0:256], x2blk[bpar], op=ALU.add))(bpar, pb),
                inc=("eV", 1), waits=[("eT", mmc), (xsem2, xldc), ("g3", P.snap("g3"))])
            emit("gpsimd", (lambda qt, ob, half, bpar: lambda h: h.dma_start(
                out=x3s.ap()[qt * 128:(qt + 1) * 128,
                             ob * 512 + half * 256:ob * 512 + half * 256 + 256], in_=x3blk[bpar]))(qt, ob, half, bpar),
                inc=("g3", 16), waits=[("eV", P.snap("eV"))])
        obh_mm_snap[obh] = P.snap("eT")
    bar_all()

    if upto == 10:
        return _finish(x3s.ap())

    ffn_phase(x3s, ln2s, None, base_t=16, to_output=True)

    return _finish()


_NC_CACHE = None
UPTO = 99


def kernel(L0_attn_a_minibatch, L0_attn_a_eta, L0_attn_b_Wq, L0_attn_b_Wc, L0_attn_b_Wk,
           L0_attn_b_Wv, L0_attn_b_Wo, L0_ffn_A, L0_ffn_B, L0_ffn_Wr, L0_ffn_br):
    global _NC_CACHE
    x = np.asarray(L0_attn_a_minibatch, dtype=np.float32)
    eta = np.asarray(L0_attn_a_eta, dtype=np.float32).reshape(1)
    Wq = np.asarray(L0_attn_b_Wq, dtype=np.float32)
    Wc = np.asarray(L0_attn_b_Wc, dtype=np.float32)
    Wk = np.asarray(L0_attn_b_Wk, dtype=np.float32)
    Wv = np.asarray(L0_attn_b_Wv, dtype=np.float32)
    Wo = np.asarray(L0_attn_b_Wo, dtype=np.float32)
    A_ = np.asarray(L0_ffn_A, dtype=np.float32)
    B_ = np.asarray(L0_ffn_B, dtype=np.float32)
    Wr = np.asarray(L0_ffn_Wr, dtype=np.float32)
    br = np.asarray(L0_ffn_br, dtype=np.float32)

    wq_dev = np.ascontiguousarray(Wq.reshape(DC, 128, H, DH).transpose(2, 1, 0, 3).reshape(H, 128, DC * 128))
    wc_dev = np.ascontiguousarray(Wc.reshape(DC, 128, DL).transpose(1, 0, 2))
    wk_dev = np.ascontiguousarray(Wk.reshape(4, 128, H, DH).transpose(2, 1, 0, 3).reshape(H, 128, 4 * 128))
    wv_dev = np.ascontiguousarray(Wv.reshape(4, 128, 16, 256).transpose(2, 1, 0, 3).reshape(16, 128, 4 * 256))
    wo_dev = np.ascontiguousarray(Wo.reshape(DC, 128, 8, 512).transpose(2, 1, 0, 3).reshape(8, 128, DC * 512))
    at_dev = np.ascontiguousarray(A_.reshape(128, D).T.reshape(DC, 128, 128).transpose(1, 0, 2))
    bc_dev = np.ascontiguousarray(B_.transpose(0, 2, 1).reshape(128, D))
    wr_dev = np.ascontiguousarray(Wr.T.reshape(DC, 128, NE).transpose(1, 0, 2))
    br_dev = np.ascontiguousarray(np.broadcast_to(br[None, :], (128, NE)))
    eta_dev = np.ascontiguousarray(np.broadcast_to(eta[None, :], (128, 1)))

    inv = 1.0 / 10000.0 ** (np.arange(0, DH, 2, dtype=np.float64) / DH)
    pos = np.arange(S, dtype=np.float64)
    fr = pos[:, None] * inv[None, :]
    emb = np.concatenate([fr, fr], axis=-1)
    cosT = np.ascontiguousarray(np.cos(emb).T.astype(np.float32))
    sinT = np.ascontiguousarray(np.sin(emb).T.astype(np.float32))

    if _NC_CACHE is None:
        _NC_CACHE = build_program(UPTO)
    nc = _NC_CACHE

    in_maps = []
    for core in range(8):
        b = core // 4
        s0 = (core % 4) * QO
        in_maps.append({
            "xf": np.ascontiguousarray(x[b]),
            "xo": np.ascontiguousarray(x[b, s0:s0 + QO]),
            "wq": wq_dev, "wc": wc_dev, "wk": wk_dev, "wv": wv_dev, "wo": wo_dev,
            "at_": at_dev, "bc_": bc_dev, "wr_": wr_dev, "brt": br_dev, "etat": eta_dev,
            "cose": cosT, "sine": sinT,
            "cosq": np.ascontiguousarray(cosT[:, s0:s0 + QO]),
            "sinq": np.ascontiguousarray(sinT[:, s0:s0 + QO]),
        })
    res = run_bass_kernel_spmd(nc, in_maps, core_ids=list(range(8)))
    out = np.empty((2, S, D), dtype=np.float32)
    for core in range(8):
        b = core // 4
        s0 = (core % 4) * QO
        out[b, s0:s0 + QO] = res.results[core]["y"]
    return out


# version 03
# speedup vs baseline: 1.1229x; 1.0227x over previous
import numpy as np
import ml_dtypes
import concourse.bass as bass
import concourse.mybir as mybir
from concourse.bass_utils import run_bass_kernel_spmd

F32 = mybir.dt.float32
F16 = mybir.dt.float16
BF16 = mybir.dt.bfloat16
AF = mybir.ActivationFunctionType
ALU = mybir.AluOpType
AX = mybir.AxisListType

D = 4096
S = 4096
QO = 1024
NT = 32
NQ = 8
DC = 32
DL = 512
H = 32
DH = 128
NE = 8
ENGS = ("sync", "scalar", "vector", "tensor", "gpsimd")


AUTOSEM = {"scalar": "eS", "vector": "eV", "tensor": "eT"}


class Plan:
    def __init__(self):
        self.ops = {e: [] for e in ENGS}
        self.cnt = {}

    def emit(self, eng, fn, inc=None, waits=()):
        s = AUTOSEM.get(eng)
        if s is not None and fn is not None:
            waits = tuple(waits) + ((s, self.cnt[s]),)
        self.ops[eng].append((tuple(waits), fn, inc))
        if inc is not None:
            self.cnt[inc[0]] += inc[1]

    def sem(self, name):
        self.cnt[name] = 0

    def snap(self, sem):
        return self.cnt[sem]


def build_program(upto=99, p1n=40, p1trans=True):
    nc = bass.Bass("TRN2", target_bir_lowering=False)

    xf = nc.dram_tensor("xf", [S, D], F32, kind="ExternalInput")
    xo = nc.dram_tensor("xo", [QO, D], F32, kind="ExternalInput")
    wq = nc.dram_tensor("wq", [H, 128, DC * 128], F32, kind="ExternalInput")
    wc = nc.dram_tensor("wc", [128, DC, DL], F32, kind="ExternalInput")
    wk = nc.dram_tensor("wk", [H, 128, 4 * 128], F32, kind="ExternalInput")
    wv = nc.dram_tensor("wv", [16, 128, 4 * 256], F32, kind="ExternalInput")
    wo = nc.dram_tensor("wo", [8, 128, DC * 512], F32, kind="ExternalInput")
    at_ = nc.dram_tensor("at_", [128, DC, 128], F32, kind="ExternalInput")
    bc_ = nc.dram_tensor("bc_", [128, D], F32, kind="ExternalInput")
    wr_ = nc.dram_tensor("wr_", [128, DC, NE], F32, kind="ExternalInput")
    brt = nc.dram_tensor("brt", [128, NE], F32, kind="ExternalInput")
    etat = nc.dram_tensor("etat", [128, 1], F32, kind="ExternalInput")
    cose = nc.dram_tensor("cose", [128, S], F32, kind="ExternalInput")
    sine = nc.dram_tensor("sine", [128, S], F32, kind="ExternalInput")
    cosq = nc.dram_tensor("cosq", [128, QO], F32, kind="ExternalInput")
    sinq = nc.dram_tensor("sinq", [128, QO], F32, kind="ExternalInput")
    y = nc.dram_tensor("y", [QO, D], F32, kind="ExternalOutput")

    m1s = nc.dram_tensor("m1s", [128, NT, D], F16)
    m1os = nc.dram_tensor("m1os", [128, NQ, D], F16)
    m1ts = nc.dram_tensor("m1ts", [128, NT, D], F16)
    m1nts = nc.dram_tensor("m1nts", [128, NQ, D], F16)
    sS = nc.dram_tensor("sS", [QO, S], F32)
    pP = nc.dram_tensor("pP", [QO, S], F16)
    x1s = nc.dram_tensor("x1s", [QO, D], F32)
    ln2s = nc.dram_tensor("ln2s", [QO, D], BF16)
    x2s = nc.dram_tensor("x2s", [QO, D], F32)
    ln3s = nc.dram_tensor("ln3s", [QO, D], BF16)
    lato = nc.dram_tensor("lato", [QO, DL], BF16)
    latf = nc.dram_tensor("latf", [S, DL], BF16)
    kts = nc.dram_tensor("kts", [H, 128, S], BF16)
    vts = nc.dram_tensor("vts", [H, 128, S], BF16)
    x3s = nc.dram_tensor("x3s", [QO, D], F32)

    import contextlib
    es = contextlib.ExitStack()
    sb = lambda *a, **k: es.enter_context(nc.sbuf_tensor(*a, **k))
    psb = lambda *a, **k: es.enter_context(nc.psum_tensor(*a, **k))

    A = sb("A", [128, 32768], F16, is_transpose_dest=True)
    B = sb("B", [128, 32768], F16)
    C = sb("C", [128, 16384], F16)
    Dn = sb("Dn", [128, 8192], F16)
    G = sb("G", [128, 12288], F16)

    def f32v(ar, off, n):
        return ar[:, off:off + 2 * n].bitcast(F32)

    def f16v(ar, off, n):
        return ar[:, off:off + n]

    def bf16v(ar, off, n):
        return ar[:, off:off + n].bitcast(BF16)

    o = 0
    def galloc(n):
        nonlocal o
        r = o
        o += n + (n & 1)
        return r
    at_sb = bf16v(G, galloc(DC * 128), DC * 128)
    bc_sb = bf16v(G, galloc(D), D)
    wr_sb = bf16v(G, galloc(DC * NE), DC * NE)
    br_sb = f32v(G, galloc(2 * NE), NE)
    eta_sb = f32v(G, galloc(2), 1)
    ones_bf = bf16v(G, galloc(2), 1)
    onesr = f32v(G, galloc(256), 128)
    cq_sb = bf16v(G, galloc(QO), QO)
    sq_sb = bf16v(G, galloc(QO), QO)
    mus = f32v(G, galloc(80), 40)
    sqs = f32v(G, galloc(80), 40)
    var_ = f32v(G, galloc(80), 40)
    nmu = f32v(G, galloc(80), 40)
    rstd = f32v(G, galloc(80), 40)
    nrstd = f32v(G, galloc(80), 40)
    nbias = f32v(G, galloc(80), 40)
    pbias = f32v(G, galloc(80), 40)
    rm = f32v(G, galloc(2 * NQ), NQ)
    nrm = f32v(G, galloc(2 * NQ), NQ)
    sums_a = f32v(G, galloc(2 * NQ), NQ)
    etars = f32v(G, galloc(2 * NQ), NQ)
    pm = f32v(G, galloc(128), 64)
    tdum = f32v(G, galloc(32), 16)
    idt = bf16v(G, galloc(128), 128)
    idt16 = f16v(G, galloc(128), 128)
    rlr = f32v(G, galloc(2 * NE), NE)
    nmx8 = f32v(G, galloc(2), 1)
    ssum8 = f32v(G, galloc(2), 1)
    sr8 = f32v(G, galloc(2), 1)
    scors = f32v(G, galloc(2 * NE), NE)
    assert o <= 12288, o

    PA = psb("PA", [128, 1024], F32)
    PB = psb("PB", [128, 1024], F32)
    PC = psb("PC", [128, 1024], F32)
    PD = psb("PD", [128, 512], F32)
    PE_ = psb("PE_", [128, 512], F32)
    pa_blocks = [PA[:, 0:512], PA[:, 512:1024], PB[:, 0:512], PB[:, 512:1024]]

    P = Plan()
    sem_objs = {}
    for name in ("g0", "g1", "g2", "g3", "g4", "g5", "g6", "g7", "s0", "s1", "s2",
                 "x0", "x1p", "x2p", "eV", "eT", "eS", "eG", "cc", "bar"):
        sem_objs[name] = es.enter_context(nc.semaphore(name))
        P.sem(name)

    emit = P.emit

    DSEMS = ("g0", "g1", "g2", "g3", "g4", "g5", "g6", "g7", "s0", "s1", "s2", "x0", "x1p", "x2p")

    def bar_all():
        k = P.snap("bar")
        for e in ENGS:
            def _inc(h):
                h.sem_inc(sem_objs["bar"], 1)
            pre = [(s_, P.snap(s_)) for s_ in DSEMS] if e == "gpsimd" else []
            emit(e, _inc, waits=pre)
            P.cnt["bar"] += 1
        tgt = k + 5
        for e in ENGS:
            emit(e, None, waits=(("bar", tgt),))

    emit("gpsimd", lambda h: h.dma_start(out=at_sb, in_=at_.ap()), inc=("g0", 16))
    emit("gpsimd", lambda h: h.dma_start(out=bc_sb, in_=bc_.ap()), inc=("g0", 16))
    emit("gpsimd", lambda h: h.dma_start(out=wr_sb, in_=wr_.ap()), inc=("g0", 16))
    emit("gpsimd", lambda h: h.dma_start(out=cq_sb, in_=cosq.ap()), inc=("g0", 16))
    emit("gpsimd", lambda h: h.dma_start(out=sq_sb, in_=sinq.ap()), inc=("g0", 16))
    emit("sync", lambda h: h.dma_start(out=br_sb, in_=brt.ap()), inc=("s0", 16))
    emit("sync", lambda h: h.dma_start(out=eta_sb, in_=etat.ap()), inc=("s0", 16))
    emit("gpsimd", lambda h: h.memset(ones_bf, 1.0), inc=("eG", 1))
    emit("gpsimd", lambda h: h.memset(onesr[0:1, :], 1.0), inc=("eG", 1))
    ident_np = np.eye(128, dtype=ml_dtypes.bfloat16)
    identt = nc.inline_tensor(ident_np, "identt")
    ident16_np = np.eye(128, dtype=np.float16)
    identt16 = nc.inline_tensor(ident16_np, "identt16")
    emit("sync", lambda h: h.dma_start(out=idt, in_=identt.ap()), inc=("s0", 16))
    emit("sync", lambda h: h.dma_start(out=idt16, in_=identt16.ap()), inc=("s0", 16))
    cfg_s0 = P.snap("s0")


    def _finish(dump=None, nrows=QO):
        if dump is not None:
            emit("gpsimd", lambda h: h.dma_start(out=y.ap()[0:nrows, :], in_=dump), inc=("g3", 16))
        emit("gpsimd", None, waits=[(s_, P.snap(s_)) for s_ in DSEMS])
        with nc.Block() as block:
            def replay(eng_name):
                def _f(h):
                    for waits, fn, inc in P.ops[eng_name]:
                        for (semname, cval) in waits:
                            if cval > 0:
                                h.wait_ge(sem_objs[semname], cval)
                        if fn is None:
                            continue
                        r = fn(h)
                        if inc is not None:
                            r.then_inc(sem_objs[inc[0]], inc[1])
                return _f
            block.sync(replay("sync"))
            block.scalar(replay("scalar"))
            block.vector(replay("vector"))
            block.tensor(replay("tensor"))
            block.gpsimd(replay("gpsimd"))
        es.close()
        return nc

    if upto == 0:
        return _finish()

    xt_bufs = [f32v(A, i * 8192, D) for i in range(4)]
    m1stag = [f16v(B, i * D, D) for i in range(2)]
    trstag = [f16v(B, 2 * D + i * D, D) for i in range(2)]
    negstag = [f16v(B, 4 * D + i * D, D) for i in range(2)]
    jnk = f16v(B, 6 * D, D)

    def ln_stats_chain(a, b, eps=1e-5):
        def _f(h):
            h.tensor_scalar(mus[:, a:b], mus[:, a:b], 1.0 / D, None, op0=ALU.mult)
            h.tensor_scalar(var_[:, a:b], sqs[:, a:b], 1.0 / D, eps, op0=ALU.mult, op1=ALU.add)
            h.drain()
            h.tensor_tensor(sqs[:, a:b], mus[:, a:b], mus[:, a:b], op=ALU.mult)
            h.tensor_scalar(nmu[:, a:b], mus[:, a:b], -1.0, None, op0=ALU.mult)
            h.drain()
            h.tensor_tensor(var_[:, a:b], var_[:, a:b], sqs[:, a:b], op=ALU.subtract)
            h.drain()
            return h.reciprocal(rstd[:, a:b], var_[:, a:b])
        return _f

    def ln_post_chain(a, b):
        def _f(h):
            h.tensor_tensor(nbias[:, a:b], nmu[:, a:b], rstd[:, a:b], op=ALU.mult)
            h.tensor_scalar(nrstd[:, a:b], rstd[:, a:b], -1.0, None, op0=ALU.mult)
            h.drain()
            return h.tensor_scalar(pbias[:, a:b], nbias[:, a:b], -1.0, None, op0=ALU.mult)
        return _f

    n_all = min(NT + NQ, p1n)
    for g in range(0, n_all, 4):
        gsem = "s1" if (g // 4) % 2 == 0 else "s2"
        gtiles = list(range(g, min(g + 4, n_all)))
        for i, t in enumerate(gtiles):
            src = xf.ap()[t * 128:(t + 1) * 128, :] if t < NT else xo.ap()[(t - NT) * 128:(t - NT + 1) * 128, :]
            w = []
            if g >= 4:
                w = [("eS", P.snap("eS")), ("eV", P.snap("eV")), ("eT", P.snap("eT"))]
            emit("sync", (lambda s_, d_: lambda h: h.dma_start(out=d_, in_=s_))(src, xt_bufs[i]),
                 inc=(gsem, 16), waits=w)
        ldc = P.snap(gsem)
        for i, t in enumerate(gtiles):
            emit("vector", (lambda i, t: lambda h: h.reduce_sum(mus[:, t:t + 1], xt_bufs[i], axis=AX.X))(i, t),
                 inc=("eV", 1), waits=[(gsem, ldc)])
            emit("scalar", (lambda i, t: lambda h: h.activation(out=jnk, in_=xt_bufs[i],
                                                               func=AF.Square, accum_out=sqs[:, t:t + 1]))(i, t),
                 inc=("eS", 1), waits=[(gsem, ldc)])
        a, b = gtiles[0], gtiles[0] + len(gtiles)
        emit("vector", ln_stats_chain(a, b), inc=("eV", 1), waits=[("eS", P.snap("eS"))])
        emit("scalar", (lambda a, b: lambda h: h.activation(out=rstd[:, a:b], in_=rstd[:, a:b], func=AF.Sqrt))(a, b),
             inc=("eS", 1), waits=[("eV", P.snap("eV"))])
        emit("vector", ln_post_chain(a, b), inc=("eV", 1), waits=[("eS", P.snap("eS"))])
        statc = P.snap("eV")
        for i, t in enumerate(gtiles):
            par = t % 2
            own = t >= NT
            wst = [("g3", P.snap("g3")), ("g4", P.snap("g4"))] if t >= 2 else []
            emit("scalar", (lambda i, t, par: lambda h: h.activation(
                out=m1stag[par], in_=xt_bufs[i], func=AF.Identity,
                bias=nbias[:, t:t + 1], scale=rstd[:, t:t + 1]))(i, t, par),
                inc=("eS", 1), waits=[("eV", statc)] + wst)
            sc_m1 = P.snap("eS")
            dstm = m1os.ap()[:, t - NT, :] if own else m1s.ap()[:, t, :]
            emit("gpsimd", (lambda par, dstm: lambda h: h.dma_start(out=dstm, in_=m1stag[par]))(par, dstm),
                 inc=("g3", 16), waits=[("eS", sc_m1)])
            if own:
                emit("scalar", (lambda i, t, par: lambda h: h.activation(
                    out=negstag[par], in_=xt_bufs[i], func=AF.Identity,
                    bias=pbias[:, t:t + 1], scale=nrstd[:, t:t + 1]))(i, t, par),
                    inc=("eS", 1))
            srcstag = negstag[par] if own else m1stag[par]
            wsrc = [("eS", P.snap("eS")), ("s0", P.snap("s0"))]
            for dc in range(DC if p1trans else 0):
                pv = PD[:, 0:64].bitcast(F16) if dc % 2 == 0 else PE_[:, 0:64].bitcast(F16)
                tw = (wsrc if dc == 0 else []) + [("eV", P.snap("eV"))]
                emit("tensor", (lambda srcstag, dc, pv: lambda h: h.transpose(
                    pv, srcstag[:, dc * 128:(dc + 1) * 128], idt16))(srcstag, dc, pv),
                    inc=("eT", 1), waits=tw)
                tc = P.snap("eT")
                cw = [("eT", tc)] + ([("g4", P.snap("g4"))] if dc == 0 else [])
                emit("vector", (lambda par, dc, pv: lambda h: h.tensor_copy(
                    trstag[par][:, dc * 128:(dc + 1) * 128], pv))(par, dc, pv),
                    inc=("eV", 1), waits=cw)
            if p1trans:
                dstt = m1nts.ap()[:, t - NT, :] if own else m1ts.ap()[:, t, :]
                emit("gpsimd", (lambda par, dstt: lambda h: h.dma_start(out=dstt, in_=trstag[par]))(par, dstt),
                     inc=("g4", 16), waits=[("eV", P.snap("eV"))])
    bar_all()

    if upto == 1:
        return _finish(m1os.ap())

    Q = f16v(A, 0, NQ * D)
    emit("sync", lambda h: h.dma_start(out=Q, in_=m1nts.ap()), inc=("s0", 16))
    qldc = P.snap("s0")
    rbuf = [f16v(B, i * (4 * D), 4 * D) for i in range(2)]

    def q_ap(qt, dc):
        off = (qt * DC + dc) * 128
        return Q[:, off:off + 128]

    def rb_ap(par, dc):
        v = rbuf[par].rearrange("p (tt dc j) -> p tt (dc j)", tt=4, dc=DC)
        return v[:, :, dc * 128:(dc + 1) * 128]

    sstag = [f32v(C, i * 1024, 512) for i in range(4)]
    sp_mm_snap = {}
    sp_st_snap = {}
    sp_ev_snap = {}
    for sp in range(8):
        par = sp % 2
        spsem = "s1" if par == 0 else "s2"
        w = [("eT", sp_mm_snap[sp - 2])] if sp >= 2 else []
        for i in range(4):
            emit("sync", (lambda sp, i, par: lambda h: h.dma_start(
                out=f16v(B, par * 4 * D + i * D, D), in_=m1ts.ap()[:, sp * 4 + i, :]))(sp, i, par),
                inc=(spsem, 16), waits=w if i == 0 else ())
        ldc = P.snap(spsem)
        for qt in range(NQ):
            idx = sp * NQ + qt
            blk = pa_blocks[idx % 4]
            wpsum = [("eV", sp_ev_snap[idx - 4])] if idx >= 4 else []
            for dc in range(DC):
                emit("tensor", (lambda qt, dc, par, blk: lambda h: h.matmul(
                    blk, q_ap(qt, dc), rb_ap(par, dc), start=(dc == 0), stop=(dc == DC - 1)))(qt, dc, par, blk),
                    inc=("eT", 1) if dc == DC - 1 else None,
                    waits=([(spsem, ldc), ("s0", qldc)] + wpsum) if dc == 0 else ())
            mmc = P.snap("eT")

            def s_evac(qt, sp, idx, blk):
                def _f(h):
                    h.tensor_copy(sstag[idx % 4], blk)
                    return h.reduce_max(pm[:, qt * 8 + sp:qt * 8 + sp + 1], blk, axis=AX.X)
                return _f
            wstore = [("x0", P.snap("x0"))] if idx >= 4 else []
            emit("vector", s_evac(qt, sp, idx, blk), inc=("eV", 1), waits=[("eT", mmc)] + wstore)
            sp_ev_snap[idx] = P.snap("eV")
            emit("scalar", (lambda qt, sp, idx: lambda h: h.dma_start(
                out=sS.ap()[qt * 128:(qt + 1) * 128, sp * 512:(sp + 1) * 512], in_=sstag[idx % 4]))(qt, sp, idx),
                inc=("x0", 16), waits=[("eV", P.snap("eV"))])
            sp_st_snap[idx] = P.snap("x0")
        sp_mm_snap[sp] = P.snap("eT")

    def rowmax_fin(h):
        for qt in range(NQ):
            h.reduce_max(rm[:, qt:qt + 1], pm[:, qt * 8:(qt + 1) * 8], axis=AX.X)
        h.drain()
        return h.tensor_scalar(nrm[:, 0:NQ], rm[:, 0:NQ], -1.0, None, op0=ALU.mult)
    emit("vector", rowmax_fin, inc=("eV", 1), waits=[("eV", P.snap("eV"))])
    bar_all()

    if upto == 2:
        return _finish(sS.ap())

    sbuf_s = [f32v(C, i * 8192, D) for i in range(2)]
    pstag = [f16v(Dn, i * D, D) for i in range(2)]
    for qt in range(NQ):
        par = qt % 2
        qsem = "s1" if par == 0 else "s2"
        w = [("eS", P.snap("eS"))] if qt >= 2 else []
        emit("sync", (lambda qt, par: lambda h: h.dma_start(
            out=sbuf_s[par], in_=sS.ap()[qt * 128:(qt + 1) * 128, :]))(qt, par),
            inc=(qsem, 16), waits=w)
        ldc = P.snap(qsem)
        wst = [("g3", P.snap("g3"))] if qt >= 2 else []
        emit("scalar", (lambda qt, par: lambda h: h.activation(
            out=pstag[par], in_=sbuf_s[par], func=AF.Exp,
            bias=nrm[:, qt:qt + 1], accum_out=sums_a[:, qt:qt + 1]))(qt, par),
            inc=("eS", 1), waits=[(qsem, ldc)] + wst)
        emit("gpsimd", (lambda qt, par: lambda h: h.dma_start(
            out=pP.ap()[qt * 128:(qt + 1) * 128, :], in_=pstag[par]))(qt, par),
            inc=("g3", 16), waits=[("eS", P.snap("eS"))])

    def recip_chain(h):
        h.reciprocal(etars[:, 0:NQ], sums_a[:, 0:NQ])
        h.drain()
        return h.tensor_scalar(etars[:, 0:NQ], etars[:, 0:NQ], eta_sb, None, op0=ALU.mult)
    emit("vector", recip_chain, inc=("eV", 1), waits=[("eS", P.snap("eS"))])
    bar_all()

    if upto == 3:
        return _finish(pP.ap())

    for c in range(DC):
        emit("sync", (lambda c: lambda h: h.dma_start_transpose(
            f16v(A, c * 1024, 1024), pP.ap()[:, c * 128:(c + 1) * 128]))(c), inc=("s0", 16))
    ptldc = P.snap("s0")
    mbuf = [f16v(B, i * DC * 512, DC * 512) for i in range(2)]
    xblk = [f32v(C, i * 1024, 512) for i in range(4)]
    m1oblk = [f16v(C, 4096 + i * 512, 512) for i in range(4)]
    xm1blk = [f32v(C, 6144 + i * 1024, 512) for i in range(4)]
    x1blk = [f32v(C, 10240 + i * 1024, 512) for i in range(4)]
    ob_mm_snap = {}
    p4_ev = {}
    for ob in range(8):
        par = ob % 2
        obsem = "s1" if par == 0 else "s2"
        w = [("eT", ob_mm_snap[ob - 2])] if ob >= 2 else []
        emit("sync", (lambda ob, par: lambda h: h.dma_start(
            out=mbuf[par], in_=m1s.ap()[:, :, ob * 512:(ob + 1) * 512]))(ob, par),
            inc=(obsem, 16), waits=w)
        mldc = P.snap(obsem)
        for qt in range(NQ):
            idx = ob * NQ + qt
            bpar = idx % 4
            blk = pa_blocks[idx % 4]
            wblk = [("eV", p4_ev[idx - 4])] if idx >= 4 else []
            bsem = "x1p" if bpar % 2 == 0 else "x2p"
            emit("scalar", (lambda qt, ob, bpar: lambda h: h.dma_start(
                out=xblk[bpar], in_=xo.ap()[qt * 128:(qt + 1) * 128, ob * 512:(ob + 1) * 512]))(qt, ob, bpar),
                inc=(bsem, 16), waits=wblk)
            emit("scalar", (lambda qt, ob, bpar: lambda h: h.dma_start(
                out=m1oblk[bpar], in_=m1os.ap()[:, qt, ob * 512:(ob + 1) * 512]))(qt, ob, bpar),
                inc=(bsem, 16))
            blkldc = P.snap(bsem)
            wpsum = [("eV", p4_ev[idx - 4])] if idx >= 4 else []
            for c in range(DC):
                emit("tensor", (lambda qt, ob, c, par, blk: lambda h: h.matmul(
                    blk, f16v(A, c * 1024 + qt * 128, 128), mbuf[par][:, c * 512:(c + 1) * 512],
                    start=(c == 0), stop=(c == DC - 1)))(qt, ob, c, par, blk),
                    inc=("eT", 1) if c == DC - 1 else None,
                    waits=([(obsem, mldc), ("s0", ptldc)] + wpsum) if c == 0 else ())
            mmc = P.snap("eT")

            def evac(qt, bpar, blk):
                def _f(h):
                    h.tensor_tensor(xm1blk[bpar], xblk[bpar], m1oblk[bpar], op=ALU.add)
                    h.drain()
                    return h.scalar_tensor_tensor(x1blk[bpar], blk, etars[:, qt:qt + 1], xm1blk[bpar],
                                                  op0=ALU.mult, op1=ALU.add)
                return _f
            emit("vector", evac(qt, bpar, blk), inc=("eV", 1),
                 waits=[("eT", mmc), (bsem, blkldc), ("g3", P.snap("g3"))])
            p4_ev[idx] = P.snap("eV")
            emit("gpsimd", (lambda qt, ob, bpar: lambda h: h.dma_start(
                out=x1s.ap()[qt * 128:(qt + 1) * 128, ob * 512:(ob + 1) * 512], in_=x1blk[bpar]))(qt, ob, bpar),
                inc=("g3", 16), waits=[("eV", P.snap("eV"))])
        ob_mm_snap[ob] = P.snap("eT")
    bar_all()

    if upto == 4:
        return _finish(x1s.ap())

    def ffn_phase(xin, lnmid, lnout, base_t, to_output=False, xout=None):
        xtb = [f32v(B, i * 8192, D) for i in range(2)]
        lstag = [bf16v(B, 16384 + i * D, D) for i in range(2)]
        fjnk = f16v(B, 24576, D)
        for g in range(0, NQ, 2):
            gsem = "s1" if (g // 2) % 2 == 0 else "s2"
            for i, t in enumerate((g, g + 1)):
                w = [("eS", P.snap("eS")), ("eV", P.snap("eV"))] if g >= 2 else []
                emit("sync", (lambda t, i: lambda h: h.dma_start(
                    out=xtb[i], in_=xin.ap()[t * 128:(t + 1) * 128, :]))(t, i),
                    inc=(gsem, 16), waits=w)
            ldc = P.snap(gsem)
            for i, t in enumerate((g, g + 1)):
                st = base_t + t
                emit("vector", (lambda i, st: lambda h: h.reduce_sum(mus[:, st:st + 1], xtb[i], axis=AX.X))(i, st),
                     inc=("eV", 1), waits=[(gsem, ldc)])
                emit("scalar", (lambda i, st: lambda h: h.activation(
                    out=fjnk, in_=xtb[i], func=AF.Square, accum_out=sqs[:, st:st + 1]))(i, st),
                    inc=("eS", 1), waits=[(gsem, ldc)])
            a, b = base_t + g, base_t + g + 2
            emit("vector", ln_stats_chain(a, b), inc=("eV", 1), waits=[("eS", P.snap("eS"))])
            emit("scalar", (lambda a, b: lambda h: h.activation(out=rstd[:, a:b], in_=rstd[:, a:b], func=AF.Sqrt))(a, b),
                 inc=("eS", 1), waits=[("eV", P.snap("eV"))])
            emit("vector", ln_post_chain(a, b), inc=("eV", 1), waits=[("eS", P.snap("eS"))])
            statc = P.snap("eV")
            for i, t in enumerate((g, g + 1)):
                st = base_t + t
                wst = [("g3", P.snap("g3"))] if g >= 2 else []
                emit("scalar", (lambda i, t, st: lambda h: h.activation(
                    out=lstag[i], in_=xtb[i], func=AF.Identity,
                    bias=nbias[:, st:st + 1], scale=rstd[:, st:st + 1]))(i, t, st),
                    inc=("eS", 1), waits=[("eV", statc)] + wst)
                emit("gpsimd", (lambda i, t: lambda h: h.dma_start(
                    out=lnmid.ap()[t * 128:(t + 1) * 128, :], in_=lstag[i]))(i, t),
                    inc=("g3", 16), waits=[("eS", P.snap("eS"))])
        bar_all()
        for c in range(DC):
            emit("sync", (lambda c: lambda h: h.dma_start_transpose(
                bf16v(A, c * 1024, 1024), lnmid.ap()[:, c * 128:(c + 1) * 128]))(c), inc=("s0", 16))
        lnldc = P.snap("s0")
        x2stag = [f32v(B, i * 8192, D) for i in range(2)]
        lnstag2 = [bf16v(B, 16384 + i * D, D) for i in range(2)]
        lowt = [bf16v(Dn, i * 256, 128) for i in range(2)]
        lostag = [bf16v(Dn, 512 + i * 128, 128) for i in range(2)]
        for qt in range(NQ):
            par = qt % 2
            wr_w = [("eV", P.snap("eV"))] if qt >= 1 else []
            for c in range(DC):
                emit("tensor", (lambda qt, c: lambda h: h.matmul(
                    PD[:, 0:NE], bf16v(A, c * 1024 + qt * 128, 128), wr_sb[:, c * NE:(c + 1) * NE],
                    start=(c == 0), stop=(c == DC - 1)))(qt, c),
                    inc=("eT", 1) if c == DC - 1 else None,
                    waits=([("s0", lnldc), ("g0", P.snap("g0"))] + wr_w) if c == 0 else ())
            rc = P.snap("eT")

            def router_sm(h):
                h.tensor_tensor(rlr[:, 0:NE], PD[:, 0:NE], br_sb, op=ALU.add)
                h.drain()
                return h.reduce_max(nmx8, rlr[:, 0:NE], axis=AX.X, negate=True)
            emit("vector", router_sm, inc=("eV", 1), waits=[("eT", rc)])
            emit("scalar", lambda h: h.activation(out=scors[:, 0:NE], in_=rlr[:, 0:NE], func=AF.Exp,
                                                  bias=nmx8, accum_out=ssum8),
                 inc=("eS", 1), waits=[("eV", P.snap("eV"))])

            def router_fin(h):
                h.reciprocal(sr8, ssum8)
                h.drain()
                return h.tensor_scalar(scors[:, 0:NE], scors[:, 0:NE], sr8, 8.0, op0=ALU.mult, op1=ALU.mult)
            emit("vector", router_fin, inc=("eV", 1), waits=[("eS", P.snap("eS"))])
            rfin = P.snap("eV")
            for c in range(DC):
                emit("tensor", (lambda qt, c: lambda h: h.matmul(
                    PE_[:, 0:128], bf16v(A, c * 1024 + qt * 128, 128), at_sb[:, c * 128:(c + 1) * 128],
                    start=(c == 0), stop=(c == DC - 1)))(qt, c),
                    inc=("eT", 1) if c == DC - 1 else None,
                    waits=[("eV", rfin)] if c == 0 else ())
            lc = P.snap("eT")

            def low_evac(par):
                def _f(h):
                    r = None
                    for e in range(NE):
                        r = h.tensor_scalar(lowt[par][:, e * 16:(e + 1) * 16],
                                            PE_[:, e * 16:(e + 1) * 16], scors[:, e:e + 1], None, op0=ALU.mult)
                    return r
                return _f
            emit("vector", low_evac(par), inc=("eV", 1), waits=[("eT", lc)])
            lec = P.snap("eV")
            emit("tensor", (lambda par: lambda h: h.transpose(
                PD[:, 64:128].bitcast(BF16), lowt[par], idt))(par),
                inc=("eT", 1), waits=[("eV", lec), ("s0", cfg_s0)])
            emit("scalar", (lambda par: lambda h: h.copy(lostag[par], PD[:, 64:128].bitcast(BF16)))(par),
                 inc=("eS", 1), waits=[("eT", P.snap("eT"))])
            ltc = P.snap("eS")
            xsem = "s1" if par == 0 else "s2"
            emit("sync", (lambda qt, par: lambda h: h.dma_start(
                out=x2stag[par], in_=xin.ap()[qt * 128:(qt + 1) * 128, :]))(qt, par),
                inc=(xsem, 16), waits=[("g4", P.snap("g4"))] if qt >= 2 else [])
            xldc = P.snap(xsem)
            for ob in range(8):
                pb = pa_blocks[ob % 2]
                wv_ = [("eV", P.snap("eV"))] if ob >= 2 else []
                emit("tensor", (lambda par, ob, pb: lambda h: h.matmul(
                    pb, lostag[par], bc_sb[:, ob * 512:(ob + 1) * 512], start=True, stop=True))(par, ob, pb),
                    inc=("eT", 1), waits=[("eS", ltc)] + wv_)
                emit("vector", (lambda par, ob, pb: lambda h: h.scalar_tensor_tensor(
                    x2stag[par][:, ob * 512:(ob + 1) * 512], pb, 2.0,
                    x2stag[par][:, ob * 512:(ob + 1) * 512], op0=ALU.mult, op1=ALU.add))(par, ob, pb),
                    inc=("eV", 1), waits=[("eT", P.snap("eT")), (xsem, xldc)])
            xec = P.snap("eV")
            dst = y.ap() if to_output else xout.ap()
            emit("gpsimd", (lambda qt, par, dst: lambda h: h.dma_start(
                out=dst[qt * 128:(qt + 1) * 128, :], in_=x2stag[par]))(qt, par, dst),
                inc=("g4", 16), waits=[("eV", xec)])
            if lnout is not None:
                st = base_t + NQ + qt
                emit("vector", (lambda par, st: lambda h: h.reduce_sum(mus[:, st:st + 1], x2stag[par], axis=AX.X))(par, st),
                     inc=("eV", 1), waits=[("eV", xec)])
                emit("scalar", (lambda par, st: lambda h: h.activation(
                    out=fjnk, in_=x2stag[par], func=AF.Square, accum_out=sqs[:, st:st + 1]))(par, st),
                    inc=("eS", 1), waits=[("eV", xec)])
                emit("vector", ln_stats_chain(st, st + 1), inc=("eV", 1), waits=[("eS", P.snap("eS"))])
                emit("scalar", (lambda st: lambda h: h.activation(out=rstd[:, st:st + 1], in_=rstd[:, st:st + 1],
                                                                  func=AF.Sqrt))(st),
                     inc=("eS", 1), waits=[("eV", P.snap("eV"))])
                emit("vector", ln_post_chain(st, st + 1), inc=("eV", 1), waits=[("eS", P.snap("eS"))])
                emit("scalar", (lambda par, st: lambda h: h.activation(
                    out=lnstag2[par], in_=x2stag[par], func=AF.Identity,
                    bias=nbias[:, st:st + 1], scale=rstd[:, st:st + 1]))(par, st),
                    inc=("eS", 1), waits=[("eV", P.snap("eV"))])
                emit("gpsimd", (lambda qt, par: lambda h: h.dma_start(
                    out=lnout.ap()[qt * 128:(qt + 1) * 128, :], in_=lnstag2[par]))(qt, par),
                    inc=("g3", 16), waits=[("eS", P.snap("eS"))])
        bar_all()

    ffn_phase(x1s, ln2s, ln3s, base_t=0, xout=x2s)

    if upto == 5:
        return _finish(x2s.ap())

    for c in range(DC):
        emit("sync", (lambda c: lambda h: h.dma_start_transpose(
            bf16v(A, c * 1024, 1024), ln3s.ap()[:, c * 128:(c + 1) * 128]))(c), inc=("s0", 16))
    l3ldc = P.snap("s0")
    wc_sb = bf16v(C, 0, DC * 512)
    emit("gpsimd", lambda h: h.dma_start(out=wc_sb, in_=wc.ap()), inc=("g7", 16))
    wcldc = P.snap("g7")
    latstag = [bf16v(Dn, i * 512, 512) for i in range(2)]
    for qt in range(NQ):
        par = qt % 2
        pb = pa_blocks[qt % 2]
        w = [("eV", P.snap("eV"))] if qt >= 2 else []
        for c in range(DC):
            emit("tensor", (lambda qt, c, pb: lambda h: h.matmul(
                pb, bf16v(A, c * 1024 + qt * 128, 128), wc_sb[:, c * 512:(c + 1) * 512],
                start=(c == 0), stop=(c == DC - 1)))(qt, c, pb),
                inc=("eT", 1) if c == DC - 1 else None,
                waits=([("s0", l3ldc), ("g7", wcldc)] + w) if c == 0 else ())
        wv2 = [("g3", P.snap("g3"))] if qt >= 2 else []
        emit("vector", (lambda par, pb: lambda h: h.tensor_copy(latstag[par], pb))(par, pb),
             inc=("eV", 1), waits=[("eT", P.snap("eT"))] + wv2)
        emit("gpsimd", (lambda qt, par: lambda h: h.dma_start(
            out=lato.ap()[qt * 128:(qt + 1) * 128, :], in_=latstag[par]))(qt, par),
            inc=("g3", 16), waits=[("eV", P.snap("eV"))])
    emit("gpsimd", lambda h: h.collective_compute(
        "AllGather", ALU.bypass, replica_groups=[[0, 1, 2, 3], [4, 5, 6, 7]],
        ins=[lato.ap().opt()], outs=[latf.ap().opt()]),
        inc=("cc", 1), waits=[("g3", P.snap("g3"))])
    p6_eT = P.snap("eT")
    p6_eV = P.snap("eV")

    if upto == 6:
        emit("gpsimd", None, waits=[("cc", P.snap("cc"))])
        return _finish(latf.ap(), nrows=512)

    qT = bf16v(B, 0, H * 1024)
    wqb = [bf16v(C, i * D, D) for i in range(2)]
    ro1 = f32v(Dn, 2048, 512)
    ro2 = f32v(Dn, 3072, 512)
    ro3 = f32v(Dn, 4096, 512)
    ro4 = f32v(Dn, 5120, 512)
    h_mm_snap = {}
    for h_ in range(H):
        par = h_ % 2
        hsem = "g1" if par == 0 else "g2"
        w = [("eT", h_mm_snap[h_ - 2])] if h_ >= 2 else [("eT", p6_eT), ("eV", p6_eV)]
        emit("gpsimd", (lambda h_, par: lambda h: h.dma_start(out=wqb[par], in_=wq.ap()[h_, :, :]))(h_, par),
             inc=(hsem, 16), waits=w)
        wldc = P.snap(hsem)
        for tb in range(2):
            pb = pa_blocks[(h_ * 2 + tb) % 2]
            wv_ = [("eV", P.snap("eV"))]
            for c in range(DC):
                emit("tensor", (lambda par, c, tb, pb: lambda h: h.matmul(
                    pb, wqb[par][:, c * 128:(c + 1) * 128], bf16v(A, c * 1024 + tb * 512, 512),
                    start=(c == 0), stop=(c == DC - 1)))(par, c, tb, pb),
                    inc=("eT", 1) if c == DC - 1 else None,
                    waits=([(hsem, wldc)] + wv_) if c == 0 else ())
            mmc = P.snap("eT")

            def rope_q(h_, tb, pb):
                lo, hi = tb * 512, (tb + 1) * 512
                dst = qT[:, h_ * 1024 + lo:h_ * 1024 + hi]

                def _f(h):
                    h.tensor_tensor(ro1[0:64, :], pb[0:64, :], cq_sb[0:64, lo:hi], op=ALU.mult)
                    h.tensor_tensor(ro2[0:64, :], pb[64:128, :], sq_sb[0:64, lo:hi], op=ALU.mult)
                    h.tensor_tensor(ro3[0:64, :], pb[64:128, :], cq_sb[64:128, lo:hi], op=ALU.mult)
                    h.tensor_tensor(ro4[0:64, :], pb[0:64, :], sq_sb[64:128, lo:hi], op=ALU.mult)
                    h.drain()
                    h.tensor_tensor(dst[0:64, :], ro1[0:64, :], ro2[0:64, :], op=ALU.subtract)
                    return h.tensor_tensor(dst[64:128, :], ro3[0:64, :], ro4[0:64, :], op=ALU.add)
                return _f
            emit("vector", rope_q(h_, tb, pb), inc=("eV", 1), waits=[("eT", mmc)])
        h_mm_snap[h_] = P.snap("eT")
    bar_all()

    if upto == 7:
        bar_all()
        return _finish(qT)

    latT = bf16v(A, 0, 4 * S)
    emit("sync", None, waits=[("cc", P.snap("cc"))])
    for lc in range(4):
        emit("sync", (lambda lc: lambda h: h.dma_start_transpose(
            bf16v(A, lc * S, S), latf.ap()[:, lc * 128:(lc + 1) * 128]))(lc), inc=("s0", 16))
    latldc = P.snap("s0")
    ce_sb = bf16v(A, 4 * S, S)
    se_sb = bf16v(A, 5 * S, S)
    emit("gpsimd", lambda h: h.dma_start(out=ce_sb, in_=cose.ap()), inc=("g7", 16))
    emit("gpsimd", lambda h: h.dma_start(out=se_sb, in_=sine.ap()), inc=("g7", 16))
    tbldc = P.snap("g7")
    wkb = [bf16v(C, i * 512, 512) for i in range(2)]
    wvb = [bf16v(C, 1024 + i * 1024, 1024) for i in range(2)]
    kstag = [bf16v(C, 3072 + i * 512, 512) for i in range(2)]
    vstag0 = bf16v(A, 6 * S, DC * 128)
    vstag1 = bf16v(A, 7 * S, DC * 128)
    hk_mm_snap = {}
    for h_ in range(H):
        par = h_ % 2
        hsem = "g1" if par == 0 else "g2"
        w = [("eT", hk_mm_snap[h_ - 2])] if h_ >= 2 else []
        emit("gpsimd", (lambda h_, par: lambda h: h.dma_start(out=wkb[par], in_=wk.ap()[h_, :, :]))(h_, par),
             inc=(hsem, 16), waits=w)
        wldc = P.snap(hsem)
        for kb in range(8):
            pb = PD[:, 0:512] if kb % 2 == 0 else PE_[:, 0:512]
            wv_ = [("eV", P.snap("eV"))]
            for lcc in range(4):
                emit("tensor", (lambda par, lcc, kb, pb: lambda h: h.matmul(
                    pb, wkb[par][:, lcc * 128:(lcc + 1) * 128], bf16v(A, lcc * S + kb * 512, 512),
                    start=(lcc == 0), stop=(lcc == 3)))(par, lcc, kb, pb),
                    inc=("eT", 1) if lcc == 3 else None,
                    waits=([(hsem, wldc), ("g7", tbldc), ("s0", latldc)] + wv_) if lcc == 0 else ())
            mmc = P.snap("eT")

            def rope_k(par, kb, pb):
                lo, hi = kb * 512, (kb + 1) * 512
                dst = kstag[par]

                def _f(h):
                    h.tensor_tensor(ro1[0:64, :], pb[0:64, :], ce_sb[0:64, lo:hi], op=ALU.mult)
                    h.tensor_tensor(ro2[0:64, :], pb[64:128, :], se_sb[0:64, lo:hi], op=ALU.mult)
                    h.tensor_tensor(ro3[0:64, :], pb[64:128, :], ce_sb[64:128, lo:hi], op=ALU.mult)
                    h.tensor_tensor(ro4[0:64, :], pb[0:64, :], se_sb[64:128, lo:hi], op=ALU.mult)
                    h.drain()
                    h.tensor_tensor(dst[0:64, :], ro1[0:64, :], ro2[0:64, :], op=ALU.subtract)
                    return h.tensor_tensor(dst[64:128, :], ro3[0:64, :], ro4[0:64, :], op=ALU.add)
                return _f
            wkst = [("x0", P.snap("x0"))] if (h_ * 8 + kb) >= 2 else []
            emit("vector", rope_k(par, kb, pb), inc=("eV", 1), waits=[("eT", mmc)] + wkst)
            emit("scalar", (lambda h_, kb, par: lambda h: h.dma_start(
                out=kts.ap()[h_, :, kb * 512:(kb + 1) * 512], in_=kstag[par]))(h_, kb, par),
                inc=("x0", 16), waits=[("eV", P.snap("eV"))])
        if h_ % 2 == 0:
            gidx = h_ // 2
            vsem = "g5" if gidx % 2 == 0 else "g6"
            emit("gpsimd", (lambda gidx, par: lambda h: h.dma_start(out=wvb[par], in_=wv.ap()[gidx, :, :]))(gidx, par),
                 inc=(vsem, 16), waits=[("eT", P.snap("eT"))] if gidx >= 1 else [("eG", P.snap("eG"))])
            wvldc = P.snap(vsem)
            for kt in range(DC):
                pb2 = PC[:, 0:256] if kt % 2 == 0 else PC[:, 512:768]
                wv_ = [("eV", P.snap("eV"))]
                for lcc in range(4):
                    emit("tensor", (lambda kt, lcc, par, pb2: lambda h: h.matmul(
                        pb2, bf16v(A, lcc * S + kt * 128, 128),
                        wvb[par][:, lcc * 256:(lcc + 1) * 256], start=(lcc == 0), stop=(lcc == 3)))(kt, lcc, par, pb2),
                        inc=("eT", 1) if lcc == 3 else None,
                        waits=([(vsem, wvldc)] + wv_) if lcc == 0 else ())
                mmc = P.snap("eT")

                def v_evac(kt, pb2):
                    def _f(h):
                        h.tensor_copy(vstag0[:, kt * 128:(kt + 1) * 128], pb2[:, 0:128])
                        return h.tensor_copy(vstag1[:, kt * 128:(kt + 1) * 128], pb2[:, 128:256])
                    return _f
                wvs = [("g4", P.snap("g4"))] if kt < 2 and gidx >= 1 else []
                emit("vector", v_evac(kt, pb2), inc=("eV", 1), waits=[("eT", mmc)] + wvs)
            vec = P.snap("eV")
            emit("gpsimd", (lambda h_: lambda h: h.dma_start(out=vts.ap()[h_, :, :], in_=vstag0))(h_),
                 inc=("g4", 16), waits=[("eV", vec)])
            emit("gpsimd", (lambda h_: lambda h: h.dma_start(out=vts.ap()[h_ + 1, :, :], in_=vstag1))(h_),
                 inc=("g4", 16))
        hk_mm_snap[h_] = P.snap("eT")
    bar_all()

    if upto == 8:
        return _finish(kts.ap()[0:8, :, :])

    kbuf = [bf16v(C, i * S, S) for i in range(2)]
    vbuf = [bf16v(C, 2 * S + i * S, S) for i in range(2)]
    ptb = [bf16v(Dn, i * 1024, 1024) for i in range(4)]
    drow = f32v(Dn, 4096, 1024)
    rrow = f32v(Dn, 6144, 1024)
    rbs = f32v(A, 0, 1024)
    sc18 = float(1.0 / np.sqrt(128.0))
    prev_fin_eV = P.snap("eV")
    ha_mm_snap = {}
    for h_ in range(H):
        par = h_ % 2
        hsem = "s1" if par == 0 else "s2"
        wld = [("eT", ha_mm_snap[h_ - 2])] if h_ >= 2 else []
        emit("sync", (lambda h_, par: lambda h: h.dma_start(out=kbuf[par], in_=kts.ap()[h_, :, :]))(h_, par),
             inc=(hsem, 16), waits=wld)
        emit("sync", (lambda h_, par: lambda h: h.dma_start(out=vbuf[par], in_=vts.ap()[h_, :, :]))(h_, par),
             inc=(hsem, 16))
        kvldc = P.snap(hsem)
        def emit_scores(c):
            spb = PA[:, 0:1024] if c % 2 == 0 else PB[:, 0:1024]
            wsc = [("eS", pt_ready[c - 2])] if c >= 2 else []
            if c == 0:
                wsc = wsc + [("eV", prev_fin_eV)]
            for qb in range(2):
                emit("tensor", (lambda h_, c, qb, par, spb: lambda h: h.matmul(
                    spb[:, qb * 512:(qb + 1) * 512], kbuf[par][:, c * 128:(c + 1) * 128],
                    qT[:, h_ * 1024 + qb * 512:h_ * 1024 + (qb + 1) * 512],
                    start=True, stop=True))(h_, c, qb, par, spb),
                    inc=("eT", 1),
                    waits=([(hsem, kvldc)] + wsc) if qb == 0 else ())
            mmc = P.snap("eT")
            wpt = [("eT", P.snap("eT"))] if c >= 4 else []
            emit("scalar", (lambda c, spb: lambda h: h.activation(
                out=ptb[c % 4], in_=spb[:, 0:1024], func=AF.Exp, scale=sc18))(c, spb),
                inc=("eS", 1), waits=[("eT", mmc)] + wpt)

        def emit_pv(c):
            ec = pt_ready[c]
            for qb in range(2):
                emit("tensor", (lambda c, qb, par: lambda h: h.matmul(
                    PC[:, qb * 512:(qb + 1) * 512], vbuf[par][:, c * 128:(c + 1) * 128],
                    ptb[c % 4][:, qb * 512:(qb + 1) * 512],
                    start=(c == 0), stop=(c == DC - 1)))(c, qb, par),
                    inc=("eT", 1), waits=[("eS", ec)] if qb == 0 else ())
                emit("tensor", (lambda c, qb: lambda h: h.matmul(
                    (PD if qb == 0 else PE_)[0:1, :], ones_bf,
                    ptb[c % 4][:, qb * 512:(qb + 1) * 512],
                    start=(c == 0), stop=(c == DC - 1)))(c, qb),
                    inc=("eT", 1))

        pt_ready = {}
        emit_scores(0)
        pt_ready[0] = P.snap("eS")
        for c in range(1, DC):
            emit_scores(c)
            pt_ready[c] = P.snap("eS")
            emit_pv(c - 1)
        emit_pv(DC - 1)
        pvc = P.snap("eT")

        def dfin(h):
            h.tensor_copy(drow[0:1, 0:512], PD[0:1, :])
            h.tensor_copy(drow[0:1, 512:1024], PE_[0:1, :])
            h.drain()
            return h.reciprocal(rrow[0:1, :], drow[0:1, :])
        emit("vector", dfin, inc=("eV", 1), waits=[("eT", pvc)])
        dc_ = P.snap("eV")
        for qb in range(2):
            emit("tensor", (lambda qb: lambda h: h.matmul(
                PA[:, qb * 512:(qb + 1) * 512], onesr[0:1, :],
                rrow[0:1, qb * 512:(qb + 1) * 512], start=True, stop=True))(qb),
                inc=("eT", 1), waits=[("eV", dc_)] if qb == 0 else ())
        bcc = P.snap("eT")

        def ofin(h_):
            def _f(h):
                h.tensor_copy(rbs, PA[:, 0:1024])
                h.drain()
                return h.tensor_tensor(qT[:, h_ * 1024:(h_ + 1) * 1024], PC[:, 0:1024], rbs, op=ALU.mult)
            return _f
        emit("vector", ofin(h_), inc=("eV", 1), waits=[("eT", bcc)])
        prev_fin_eV = P.snap("eV")
        ha_mm_snap[h_] = P.snap("eT")
    bar_all()

    if upto == 9:
        return _finish(qT)

    wob = [bf16v(C, i * DC * 256, DC * 256) for i in range(2)]
    x2blk = [f32v(A, i * 512, 256) for i in range(4)]
    x3blk = [f32v(A, 2048 + i * 512, 256) for i in range(4)]
    obh_mm_snap = {}
    p9_ev = {}
    for obh in range(16):
        par = obh % 2
        ob, half = obh // 2, obh % 2
        osem = "g1" if par == 0 else "g2"
        w = [("eT", obh_mm_snap[obh - 2])] if obh >= 2 else []
        wsrc = wo.ap()[obh // 2, :, :].rearrange("p (ec j) -> p ec j", ec=DC)[:, :, (obh % 2) * 256:(obh % 2 + 1) * 256]
        emit("gpsimd", (lambda wsrc, par: lambda h: h.dma_start(out=wob[par], in_=wsrc))(wsrc, par),
             inc=(osem, 16), waits=w)
        wldc = P.snap(osem)
        for qt in range(NQ):
            idx9 = obh * NQ + qt
            bpar = idx9 % 4
            pb = pa_blocks[bpar]
            wblk = [("eV", p9_ev[idx9 - 4])] if idx9 >= 4 else []
            xsem2 = "x1p" if bpar % 2 == 0 else "x2p"
            emit("scalar", (lambda qt, ob, half, bpar: lambda h: h.dma_start(
                out=x2blk[bpar], in_=x2s.ap()[qt * 128:(qt + 1) * 128,
                                              ob * 512 + half * 256:ob * 512 + half * 256 + 256]))(qt, ob, half, bpar),
                inc=(xsem2, 16), waits=wblk)
            xldc = P.snap(xsem2)
            for c in range(DC):
                emit("tensor", (lambda qt, c, par, pb: lambda h: h.matmul(
                    pb[:, 0:256], qT[:, c * 1024 + qt * 128:c * 1024 + qt * 128 + 128],
                    wob[par][:, c * 256:(c + 1) * 256],
                    start=(c == 0), stop=(c == DC - 1)))(qt, c, par, pb),
                    inc=("eT", 1) if c == DC - 1 else None,
                    waits=([(osem, wldc)] + ([("eV", p9_ev[idx9 - 4])] if idx9 >= 4 else [])) if c == 0 else ())
            mmc = P.snap("eT")
            emit("vector", (lambda bpar, pb: lambda h: h.tensor_tensor(
                x3blk[bpar], pb[:, 0:256], x2blk[bpar], op=ALU.add))(bpar, pb),
                inc=("eV", 1), waits=[("eT", mmc), (xsem2, xldc), ("g3", P.snap("g3"))])
            p9_ev[idx9] = P.snap("eV")
            emit("gpsimd", (lambda qt, ob, half, bpar: lambda h: h.dma_start(
                out=x3s.ap()[qt * 128:(qt + 1) * 128,
                             ob * 512 + half * 256:ob * 512 + half * 256 + 256], in_=x3blk[bpar]))(qt, ob, half, bpar),
                inc=("g3", 16), waits=[("eV", P.snap("eV"))])
        obh_mm_snap[obh] = P.snap("eT")
    bar_all()

    if upto == 10:
        return _finish(x3s.ap())

    ffn_phase(x3s, ln2s, None, base_t=16, to_output=True)

    return _finish()


_NC_CACHE = None
UPTO = 99


def kernel(L0_attn_a_minibatch, L0_attn_a_eta, L0_attn_b_Wq, L0_attn_b_Wc, L0_attn_b_Wk,
           L0_attn_b_Wv, L0_attn_b_Wo, L0_ffn_A, L0_ffn_B, L0_ffn_Wr, L0_ffn_br):
    global _NC_CACHE
    x = np.asarray(L0_attn_a_minibatch, dtype=np.float32)
    eta = np.asarray(L0_attn_a_eta, dtype=np.float32).reshape(1)
    Wq = np.asarray(L0_attn_b_Wq, dtype=np.float32)
    Wc = np.asarray(L0_attn_b_Wc, dtype=np.float32)
    Wk = np.asarray(L0_attn_b_Wk, dtype=np.float32)
    Wv = np.asarray(L0_attn_b_Wv, dtype=np.float32)
    Wo = np.asarray(L0_attn_b_Wo, dtype=np.float32)
    A_ = np.asarray(L0_ffn_A, dtype=np.float32)
    B_ = np.asarray(L0_ffn_B, dtype=np.float32)
    Wr = np.asarray(L0_ffn_Wr, dtype=np.float32)
    br = np.asarray(L0_ffn_br, dtype=np.float32)

    wq_dev = np.ascontiguousarray(Wq.reshape(DC, 128, H, DH).transpose(2, 1, 0, 3).reshape(H, 128, DC * 128))
    wc_dev = np.ascontiguousarray(Wc.reshape(DC, 128, DL).transpose(1, 0, 2))
    wk_dev = np.ascontiguousarray(Wk.reshape(4, 128, H, DH).transpose(2, 1, 0, 3).reshape(H, 128, 4 * 128))
    wv_dev = np.ascontiguousarray(Wv.reshape(4, 128, 16, 256).transpose(2, 1, 0, 3).reshape(16, 128, 4 * 256))
    wo_dev = np.ascontiguousarray(Wo.reshape(DC, 128, 8, 512).transpose(2, 1, 0, 3).reshape(8, 128, DC * 512))
    at_dev = np.ascontiguousarray(A_.reshape(128, D).T.reshape(DC, 128, 128).transpose(1, 0, 2))
    bc_dev = np.ascontiguousarray(B_.transpose(0, 2, 1).reshape(128, D))
    wr_dev = np.ascontiguousarray(Wr.T.reshape(DC, 128, NE).transpose(1, 0, 2))
    br_dev = np.ascontiguousarray(np.broadcast_to(br[None, :], (128, NE)))
    eta_dev = np.ascontiguousarray(np.broadcast_to(eta[None, :], (128, 1)))

    inv = 1.0 / 10000.0 ** (np.arange(0, DH, 2, dtype=np.float64) / DH)
    pos = np.arange(S, dtype=np.float64)
    fr = pos[:, None] * inv[None, :]
    emb = np.concatenate([fr, fr], axis=-1)
    cosT = np.ascontiguousarray(np.cos(emb).T.astype(np.float32))
    sinT = np.ascontiguousarray(np.sin(emb).T.astype(np.float32))

    if _NC_CACHE is None:
        _NC_CACHE = build_program(UPTO)
    nc = _NC_CACHE

    in_maps = []
    for core in range(8):
        b = core // 4
        s0 = (core % 4) * QO
        in_maps.append({
            "xf": np.ascontiguousarray(x[b]),
            "xo": np.ascontiguousarray(x[b, s0:s0 + QO]),
            "wq": wq_dev, "wc": wc_dev, "wk": wk_dev, "wv": wv_dev, "wo": wo_dev,
            "at_": at_dev, "bc_": bc_dev, "wr_": wr_dev, "brt": br_dev, "etat": eta_dev,
            "cose": cosT, "sine": sinT,
            "cosq": np.ascontiguousarray(cosT[:, s0:s0 + QO]),
            "sinq": np.ascontiguousarray(sinT[:, s0:s0 + QO]),
        })
    res = run_bass_kernel_spmd(nc, in_maps, core_ids=list(range(8)))
    out = np.empty((2, S, D), dtype=np.float32)
    for core in range(8):
        b = core // 4
        s0 = (core % 4) * QO
        out[b, s0:s0 + QO] = res.results[core]["y"]
    return out


# version 04
# speedup vs baseline: 1.1794x; 1.0504x over previous
import numpy as np
import ml_dtypes
import concourse.bass as bass
import concourse.mybir as mybir
from concourse.bass_utils import run_bass_kernel_spmd

F32 = mybir.dt.float32
F16 = mybir.dt.float16
BF16 = mybir.dt.bfloat16
AF = mybir.ActivationFunctionType
ALU = mybir.AluOpType
AX = mybir.AxisListType

D = 4096
S = 4096
QO = 1024
NT = 32
NQ = 8
DC = 32
DL = 512
H = 32
DH = 128
NE = 8
ENGS = ("sync", "scalar", "vector", "tensor", "gpsimd")


AUTOSEM = {"scalar": "eS", "vector": "eV", "tensor": "eT"}


class Plan:
    def __init__(self):
        self.ops = {e: [] for e in ENGS}
        self.cnt = {}

    def emit(self, eng, fn, inc=None, waits=()):
        s = AUTOSEM.get(eng)
        if s is not None and fn is not None:
            waits = tuple(waits) + ((s, self.cnt[s]),)
        self.ops[eng].append((tuple(waits), fn, inc))
        if inc is not None:
            self.cnt[inc[0]] += inc[1]

    def sem(self, name):
        self.cnt[name] = 0

    def snap(self, sem):
        return self.cnt[sem]


def build_program(upto=99, p1n=40, p1trans=True):
    nc = bass.Bass("TRN2", target_bir_lowering=False)

    xf = nc.dram_tensor("xf", [S, D], F32, kind="ExternalInput")
    xo = nc.dram_tensor("xo", [QO, D], F32, kind="ExternalInput")
    wq = nc.dram_tensor("wq", [H, 128, DC * 128], F32, kind="ExternalInput")
    wc = nc.dram_tensor("wc", [128, DC, DL], F32, kind="ExternalInput")
    wk = nc.dram_tensor("wk", [H, 128, 4 * 128], F32, kind="ExternalInput")
    wv = nc.dram_tensor("wv", [16, 128, 4 * 256], F32, kind="ExternalInput")
    wo = nc.dram_tensor("wo", [8, 128, DC * 512], F32, kind="ExternalInput")
    at_ = nc.dram_tensor("at_", [128, DC, 128], F32, kind="ExternalInput")
    bc_ = nc.dram_tensor("bc_", [128, D], F32, kind="ExternalInput")
    wr_ = nc.dram_tensor("wr_", [128, DC, NE], F32, kind="ExternalInput")
    brt = nc.dram_tensor("brt", [128, NE], F32, kind="ExternalInput")
    etat = nc.dram_tensor("etat", [128, 1], F32, kind="ExternalInput")
    cose = nc.dram_tensor("cose", [128, S], F32, kind="ExternalInput")
    sine = nc.dram_tensor("sine", [128, S], F32, kind="ExternalInput")
    cosq = nc.dram_tensor("cosq", [128, QO], F32, kind="ExternalInput")
    sinq = nc.dram_tensor("sinq", [128, QO], F32, kind="ExternalInput")
    y = nc.dram_tensor("y", [QO, D], F32, kind="ExternalOutput")

    m1s = nc.dram_tensor("m1s", [128, NT, D], F16)
    m1os = nc.dram_tensor("m1os", [128, NQ, D], F16)
    m1ts = nc.dram_tensor("m1ts", [128, NT, D], F16)
    m1nts = nc.dram_tensor("m1nts", [128, NQ, D], F16)
    sS = nc.dram_tensor("sS", [QO, S], F32)
    pP = nc.dram_tensor("pP", [QO, S], F16)
    x1s = nc.dram_tensor("x1s", [QO, D], F32)
    ln2s = nc.dram_tensor("ln2s", [QO, D], BF16)
    x2s = nc.dram_tensor("x2s", [QO, D], F32)
    ln3s = nc.dram_tensor("ln3s", [QO, D], BF16)
    lato = nc.dram_tensor("lato", [QO, DL], BF16)
    latf = nc.dram_tensor("latf", [S, DL], BF16)
    kts = nc.dram_tensor("kts", [H, 128, S], BF16)
    vts = nc.dram_tensor("vts", [H, 128, S], BF16)
    x3s = nc.dram_tensor("x3s", [QO, D], F32)

    import contextlib
    es = contextlib.ExitStack()
    sb = lambda *a, **k: es.enter_context(nc.sbuf_tensor(*a, **k))
    psb = lambda *a, **k: es.enter_context(nc.psum_tensor(*a, **k))

    A = sb("A", [128, 32768], F16, is_transpose_dest=True)
    B = sb("B", [128, 32768], F16)
    C = sb("C", [128, 16384], F16)
    Dn = sb("Dn", [128, 8192], F16)
    G = sb("G", [128, 12288], F16)

    def f32v(ar, off, n):
        return ar[:, off:off + 2 * n].bitcast(F32)

    def f16v(ar, off, n):
        return ar[:, off:off + n]

    def bf16v(ar, off, n):
        return ar[:, off:off + n].bitcast(BF16)

    o = 0
    def galloc(n):
        nonlocal o
        r = o
        o += n + (n & 1)
        return r
    at_sb = bf16v(G, galloc(DC * 128), DC * 128)
    bc_sb = bf16v(G, galloc(D), D)
    wr_sb = bf16v(G, galloc(DC * NE), DC * NE)
    br_sb = f32v(G, galloc(2 * NE), NE)
    eta_sb = f32v(G, galloc(2), 1)
    ones_bf = bf16v(G, galloc(2), 1)
    onesr = f32v(G, galloc(256), 128)
    cq_sb = bf16v(G, galloc(QO), QO)
    sq_sb = bf16v(G, galloc(QO), QO)
    mus = f32v(G, galloc(80), 40)
    sqs = f32v(G, galloc(80), 40)
    var_ = f32v(G, galloc(80), 40)
    nmu = f32v(G, galloc(80), 40)
    rstd = f32v(G, galloc(80), 40)
    nrstd = f32v(G, galloc(80), 40)
    nbias = f32v(G, galloc(80), 40)
    pbias = f32v(G, galloc(80), 40)
    rm = f32v(G, galloc(2 * NQ), NQ)
    nrm = f32v(G, galloc(2 * NQ), NQ)
    sums_a = f32v(G, galloc(2 * NQ), NQ)
    etars = f32v(G, galloc(2 * NQ), NQ)
    pm = f32v(G, galloc(128), 64)
    tdum = f32v(G, galloc(32), 16)
    idt = bf16v(G, galloc(128), 128)
    idt16 = f16v(G, galloc(128), 128)
    rlr = f32v(G, galloc(2 * NE), NE)
    nmx8 = f32v(G, galloc(2), 1)
    ssum8 = f32v(G, galloc(2), 1)
    sr8 = f32v(G, galloc(2), 1)
    scors = f32v(G, galloc(2 * NE), NE)
    assert o <= 12288, o

    PA = psb("PA", [128, 1024], F32)
    PB = psb("PB", [128, 1024], F32)
    PC = psb("PC", [128, 1024], F32)
    PD = psb("PD", [128, 512], F32)
    PE_ = psb("PE_", [128, 512], F32)
    pa_blocks = [PA[:, 0:512], PA[:, 512:1024], PB[:, 0:512], PB[:, 512:1024]]

    P = Plan()
    sem_objs = {}
    for name in ("g0", "g1", "g2", "g3", "g4", "g5", "g6", "g7", "s0", "s1", "s2",
                 "x0", "x1p", "x2p", "eV", "eT", "eS", "eG", "cc", "bar"):
        sem_objs[name] = es.enter_context(nc.semaphore(name))
        P.sem(name)

    emit = P.emit

    DSEMS = ("g0", "g1", "g2", "g3", "g4", "g5", "g6", "g7", "s0", "s1", "s2", "x0", "x1p", "x2p")

    def bar_all():
        k = P.snap("bar")
        for e in ENGS:
            def _inc(h):
                h.sem_inc(sem_objs["bar"], 1)
            pre = [(s_, P.snap(s_)) for s_ in DSEMS] if e == "gpsimd" else []
            emit(e, _inc, waits=pre)
            P.cnt["bar"] += 1
        tgt = k + 5
        for e in ENGS:
            emit(e, None, waits=(("bar", tgt),))

    emit("gpsimd", lambda h: h.dma_start(out=at_sb, in_=at_.ap()), inc=("g0", 16))
    emit("gpsimd", lambda h: h.dma_start(out=bc_sb, in_=bc_.ap()), inc=("g0", 16))
    emit("gpsimd", lambda h: h.dma_start(out=wr_sb, in_=wr_.ap()), inc=("g0", 16))
    emit("gpsimd", lambda h: h.dma_start(out=cq_sb, in_=cosq.ap()), inc=("g0", 16))
    emit("gpsimd", lambda h: h.dma_start(out=sq_sb, in_=sinq.ap()), inc=("g0", 16))
    emit("sync", lambda h: h.dma_start(out=br_sb, in_=brt.ap()), inc=("s0", 16))
    emit("sync", lambda h: h.dma_start(out=eta_sb, in_=etat.ap()), inc=("s0", 16))
    emit("gpsimd", lambda h: h.memset(ones_bf, 1.0), inc=("eG", 1))
    emit("gpsimd", lambda h: h.memset(onesr[0:1, :], 1.0), inc=("eG", 1))
    ident_np = np.eye(128, dtype=ml_dtypes.bfloat16)
    identt = nc.inline_tensor(ident_np, "identt")
    ident16_np = np.eye(128, dtype=np.float16)
    identt16 = nc.inline_tensor(ident16_np, "identt16")
    emit("sync", lambda h: h.dma_start(out=idt, in_=identt.ap()), inc=("s0", 16))
    emit("sync", lambda h: h.dma_start(out=idt16, in_=identt16.ap()), inc=("s0", 16))
    cfg_s0 = P.snap("s0")


    def _finish(dump=None, nrows=QO):
        if dump is not None:
            emit("gpsimd", lambda h: h.dma_start(out=y.ap()[0:nrows, :], in_=dump), inc=("g3", 16))
        emit("gpsimd", None, waits=[(s_, P.snap(s_)) for s_ in DSEMS])
        with nc.Block() as block:
            def replay(eng_name):
                def _f(h):
                    for waits, fn, inc in P.ops[eng_name]:
                        for (semname, cval) in waits:
                            if cval > 0:
                                h.wait_ge(sem_objs[semname], cval)
                        if fn is None:
                            continue
                        r = fn(h)
                        if inc is not None:
                            r.then_inc(sem_objs[inc[0]], inc[1])
                return _f
            block.sync(replay("sync"))
            block.scalar(replay("scalar"))
            block.vector(replay("vector"))
            block.tensor(replay("tensor"))
            block.gpsimd(replay("gpsimd"))
        es.close()
        return nc

    if upto == 0:
        return _finish()

    xt_bufs = [f32v(A, i * 8192, D) for i in range(4)]
    m1stag = [f16v(B, i * D, D) for i in range(2)]
    trstag = [f16v(B, 2 * D + i * D, D) for i in range(2)]
    negstag = [f16v(B, 4 * D + i * D, D) for i in range(2)]
    jnk = f16v(B, 6 * D, D)

    def ln_stats_chain(a, b, eps=1e-5):
        def _f(h):
            h.tensor_scalar(mus[:, a:b], mus[:, a:b], 1.0 / D, None, op0=ALU.mult)
            h.tensor_scalar(var_[:, a:b], sqs[:, a:b], 1.0 / D, eps, op0=ALU.mult, op1=ALU.add)
            h.drain()
            h.tensor_tensor(sqs[:, a:b], mus[:, a:b], mus[:, a:b], op=ALU.mult)
            h.tensor_scalar(nmu[:, a:b], mus[:, a:b], -1.0, None, op0=ALU.mult)
            h.drain()
            h.tensor_tensor(var_[:, a:b], var_[:, a:b], sqs[:, a:b], op=ALU.subtract)
            h.drain()
            return h.reciprocal(rstd[:, a:b], var_[:, a:b])
        return _f

    def ln_post_chain(a, b):
        def _f(h):
            h.tensor_tensor(nbias[:, a:b], nmu[:, a:b], rstd[:, a:b], op=ALU.mult)
            h.tensor_scalar(nrstd[:, a:b], rstd[:, a:b], -1.0, None, op0=ALU.mult)
            h.drain()
            return h.tensor_scalar(pbias[:, a:b], nbias[:, a:b], -1.0, None, op0=ALU.mult)
        return _f

    n_all = min(NT + NQ, p1n)
    for g in range(0, n_all, 4):
        gsem = "s1" if (g // 4) % 2 == 0 else "s2"
        gtiles = list(range(g, min(g + 4, n_all)))
        for i, t in enumerate(gtiles):
            src = xf.ap()[t * 128:(t + 1) * 128, :] if t < NT else xo.ap()[(t - NT) * 128:(t - NT + 1) * 128, :]
            w = []
            if g >= 4:
                w = [("eS", P.snap("eS")), ("eV", P.snap("eV")), ("eT", P.snap("eT"))]
            emit("sync", (lambda s_, d_: lambda h: h.dma_start(out=d_, in_=s_))(src, xt_bufs[i]),
                 inc=(gsem, 16), waits=w)
        ldc = P.snap(gsem)
        for i, t in enumerate(gtiles):
            emit("vector", (lambda i, t: lambda h: h.reduce_sum(mus[:, t:t + 1], xt_bufs[i], axis=AX.X))(i, t),
                 inc=("eV", 1), waits=[(gsem, ldc)])
            emit("scalar", (lambda i, t: lambda h: h.activation(out=jnk, in_=xt_bufs[i],
                                                               func=AF.Square, accum_out=sqs[:, t:t + 1]))(i, t),
                 inc=("eS", 1), waits=[(gsem, ldc)])
        a, b = gtiles[0], gtiles[0] + len(gtiles)
        emit("vector", ln_stats_chain(a, b), inc=("eV", 1), waits=[("eS", P.snap("eS"))])
        emit("scalar", (lambda a, b: lambda h: h.activation(out=rstd[:, a:b], in_=rstd[:, a:b], func=AF.Sqrt))(a, b),
             inc=("eS", 1), waits=[("eV", P.snap("eV"))])
        emit("vector", ln_post_chain(a, b), inc=("eV", 1), waits=[("eS", P.snap("eS"))])
        statc = P.snap("eV")
        for i, t in enumerate(gtiles):
            par = t % 2
            own = t >= NT
            wst = [("g3", P.snap("g3")), ("g4", P.snap("g4"))] if t >= 2 else []
            emit("scalar", (lambda i, t, par: lambda h: h.activation(
                out=m1stag[par], in_=xt_bufs[i], func=AF.Identity,
                bias=nbias[:, t:t + 1], scale=rstd[:, t:t + 1]))(i, t, par),
                inc=("eS", 1), waits=[("eV", statc)] + wst)
            sc_m1 = P.snap("eS")
            dstm = m1os.ap()[:, t - NT, :] if own else m1s.ap()[:, t, :]
            emit("gpsimd", (lambda par, dstm: lambda h: h.dma_start(out=dstm, in_=m1stag[par]))(par, dstm),
                 inc=("g3", 16), waits=[("eS", sc_m1)])
            if own:
                emit("scalar", (lambda i, t, par: lambda h: h.activation(
                    out=negstag[par], in_=xt_bufs[i], func=AF.Identity,
                    bias=pbias[:, t:t + 1], scale=nrstd[:, t:t + 1]))(i, t, par),
                    inc=("eS", 1))
            srcstag = negstag[par] if own else m1stag[par]
            wsrc = [("eS", P.snap("eS")), ("s0", P.snap("s0"))]
            pvslots = [PD[:, 0:64].bitcast(F16), PE_[:, 0:64].bitcast(F16),
                       PA[:, 0:64].bitcast(F16), PB[:, 0:64].bitcast(F16)]
            cp_snap = {}
            for dc in range(DC if p1trans else 0):
                pv = pvslots[dc % 4]
                tw = (wsrc if dc == 0 else []) + [("eV", cp_snap[dc - 4] if dc >= 4 else P.snap("eV"))]
                emit("tensor", (lambda srcstag, dc, pv: lambda h: h.transpose(
                    pv, srcstag[:, dc * 128:(dc + 1) * 128], idt16))(srcstag, dc, pv),
                    inc=("eT", 1), waits=tw)
                tc = P.snap("eT")
                cw = [("eT", tc)] + ([("g4", P.snap("g4"))] if dc == 0 else [])
                emit("vector", (lambda par, dc, pv: lambda h: h.tensor_copy(
                    trstag[par][:, dc * 128:(dc + 1) * 128], pv))(par, dc, pv),
                    inc=("eV", 1), waits=cw)
                cp_snap[dc] = P.snap("eV")
            if p1trans:
                dstt = m1nts.ap()[:, t - NT, :] if own else m1ts.ap()[:, t, :]
                emit("gpsimd", (lambda par, dstt: lambda h: h.dma_start(out=dstt, in_=trstag[par]))(par, dstt),
                     inc=("g4", 16), waits=[("eV", P.snap("eV"))])
    bar_all()

    if upto == 1:
        return _finish(m1os.ap())

    Q = f16v(A, 0, NQ * D)
    emit("sync", lambda h: h.dma_start(out=Q, in_=m1nts.ap()), inc=("s0", 16))
    qldc = P.snap("s0")
    rbuf = [f16v(B, i * (4 * D), 4 * D) for i in range(2)]

    def q_ap(qt, dc):
        off = (qt * DC + dc) * 128
        return Q[:, off:off + 128]

    def rb_ap(par, dc):
        v = rbuf[par].rearrange("p (tt dc j) -> p tt (dc j)", tt=4, dc=DC)
        return v[:, :, dc * 128:(dc + 1) * 128]

    sstag = [f32v(C, i * 1024, 512) for i in range(4)]
    sp_mm_snap = {}
    sp_st_snap = {}
    sp_ev_snap = {}
    for sp in range(8):
        par = sp % 2
        spsem = "s1" if par == 0 else "s2"
        w = [("eT", sp_mm_snap[sp - 2])] if sp >= 2 else []
        for i in range(4):
            emit("sync", (lambda sp, i, par: lambda h: h.dma_start(
                out=f16v(B, par * 4 * D + i * D, D), in_=m1ts.ap()[:, sp * 4 + i, :]))(sp, i, par),
                inc=(spsem, 16), waits=w if i == 0 else ())
        ldc = P.snap(spsem)
        for qt in range(NQ):
            idx = sp * NQ + qt
            blk = pa_blocks[idx % 4]
            wpsum = [("eV", sp_ev_snap[idx - 4])] if idx >= 4 else []
            for dc in range(DC):
                emit("tensor", (lambda qt, dc, par, blk: lambda h: h.matmul(
                    blk, q_ap(qt, dc), rb_ap(par, dc), start=(dc == 0), stop=(dc == DC - 1)))(qt, dc, par, blk),
                    inc=("eT", 1) if dc == DC - 1 else None,
                    waits=([(spsem, ldc), ("s0", qldc)] + wpsum) if dc == 0 else ())
            mmc = P.snap("eT")

            def s_evac(qt, sp, idx, blk):
                def _f(h):
                    h.tensor_copy(sstag[idx % 4], blk)
                    return h.reduce_max(pm[:, qt * 8 + sp:qt * 8 + sp + 1], blk, axis=AX.X)
                return _f
            wstore = [("x0", P.snap("x0"))] if idx >= 4 else []
            emit("vector", s_evac(qt, sp, idx, blk), inc=("eV", 1), waits=[("eT", mmc)] + wstore)
            sp_ev_snap[idx] = P.snap("eV")
            emit("scalar", (lambda qt, sp, idx: lambda h: h.dma_start(
                out=sS.ap()[qt * 128:(qt + 1) * 128, sp * 512:(sp + 1) * 512], in_=sstag[idx % 4]))(qt, sp, idx),
                inc=("x0", 16), waits=[("eV", P.snap("eV"))])
            sp_st_snap[idx] = P.snap("x0")
        sp_mm_snap[sp] = P.snap("eT")

    def rowmax_fin(h):
        for qt in range(NQ):
            h.reduce_max(rm[:, qt:qt + 1], pm[:, qt * 8:(qt + 1) * 8], axis=AX.X)
        h.drain()
        return h.tensor_scalar(nrm[:, 0:NQ], rm[:, 0:NQ], -1.0, None, op0=ALU.mult)
    emit("vector", rowmax_fin, inc=("eV", 1), waits=[("eV", P.snap("eV"))])
    bar_all()

    if upto == 2:
        return _finish(sS.ap())

    sbuf_s = [f32v(C, i * 8192, D) for i in range(2)]
    pstag = [f16v(Dn, i * D, D) for i in range(2)]
    for qt in range(NQ):
        par = qt % 2
        qsem = "s1" if par == 0 else "s2"
        w = [("eS", P.snap("eS"))] if qt >= 2 else []
        emit("sync", (lambda qt, par: lambda h: h.dma_start(
            out=sbuf_s[par], in_=sS.ap()[qt * 128:(qt + 1) * 128, :]))(qt, par),
            inc=(qsem, 16), waits=w)
        ldc = P.snap(qsem)
        wst = [("g3", P.snap("g3"))] if qt >= 2 else []
        emit("scalar", (lambda qt, par: lambda h: h.activation(
            out=pstag[par], in_=sbuf_s[par], func=AF.Exp,
            bias=nrm[:, qt:qt + 1], accum_out=sums_a[:, qt:qt + 1]))(qt, par),
            inc=("eS", 1), waits=[(qsem, ldc)] + wst)
        emit("gpsimd", (lambda qt, par: lambda h: h.dma_start(
            out=pP.ap()[qt * 128:(qt + 1) * 128, :], in_=pstag[par]))(qt, par),
            inc=("g3", 16), waits=[("eS", P.snap("eS"))])

    def recip_chain(h):
        h.reciprocal(etars[:, 0:NQ], sums_a[:, 0:NQ])
        h.drain()
        return h.tensor_scalar(etars[:, 0:NQ], etars[:, 0:NQ], eta_sb, None, op0=ALU.mult)
    emit("vector", recip_chain, inc=("eV", 1), waits=[("eS", P.snap("eS"))])
    bar_all()

    if upto == 3:
        return _finish(pP.ap())

    for c in range(DC):
        emit("sync", (lambda c: lambda h: h.dma_start_transpose(
            f16v(A, c * 1024, 1024), pP.ap()[:, c * 128:(c + 1) * 128]))(c), inc=("s0", 16))
    ptldc = P.snap("s0")
    mbuf = [f16v(B, i * DC * 512, DC * 512) for i in range(2)]
    xblk = [f32v(C, i * 1024, 512) for i in range(4)]
    m1oblk = [f16v(C, 4096 + i * 512, 512) for i in range(4)]
    xm1blk = [f32v(C, 6144 + i * 1024, 512) for i in range(4)]
    x1blk = [f32v(C, 10240 + i * 1024, 512) for i in range(4)]
    ob_mm_snap = {}
    p4_ev = {}
    for ob in range(8):
        par = ob % 2
        obsem = "s1" if par == 0 else "s2"
        w = [("eT", ob_mm_snap[ob - 2])] if ob >= 2 else []
        emit("sync", (lambda ob, par: lambda h: h.dma_start(
            out=mbuf[par], in_=m1s.ap()[:, :, ob * 512:(ob + 1) * 512]))(ob, par),
            inc=(obsem, 16), waits=w)
        mldc = P.snap(obsem)
        for qt in range(NQ):
            idx = ob * NQ + qt
            bpar = idx % 4
            blk = pa_blocks[idx % 4]
            wblk = [("eV", p4_ev[idx - 4])] if idx >= 4 else []
            bsem = "x1p" if bpar % 2 == 0 else "x2p"
            emit("scalar", (lambda qt, ob, bpar: lambda h: h.dma_start(
                out=xblk[bpar], in_=xo.ap()[qt * 128:(qt + 1) * 128, ob * 512:(ob + 1) * 512]))(qt, ob, bpar),
                inc=(bsem, 16), waits=wblk)
            emit("scalar", (lambda qt, ob, bpar: lambda h: h.dma_start(
                out=m1oblk[bpar], in_=m1os.ap()[:, qt, ob * 512:(ob + 1) * 512]))(qt, ob, bpar),
                inc=(bsem, 16))
            blkldc = P.snap(bsem)
            wpsum = [("eV", p4_ev[idx - 4])] if idx >= 4 else []
            for c in range(DC):
                emit("tensor", (lambda qt, ob, c, par, blk: lambda h: h.matmul(
                    blk, f16v(A, c * 1024 + qt * 128, 128), mbuf[par][:, c * 512:(c + 1) * 512],
                    start=(c == 0), stop=(c == DC - 1)))(qt, ob, c, par, blk),
                    inc=("eT", 1) if c == DC - 1 else None,
                    waits=([(obsem, mldc), ("s0", ptldc)] + wpsum) if c == 0 else ())
            mmc = P.snap("eT")

            def evac(qt, bpar, blk):
                def _f(h):
                    h.tensor_tensor(xm1blk[bpar], xblk[bpar], m1oblk[bpar], op=ALU.add)
                    h.drain()
                    return h.scalar_tensor_tensor(x1blk[bpar], blk, etars[:, qt:qt + 1], xm1blk[bpar],
                                                  op0=ALU.mult, op1=ALU.add)
                return _f
            emit("vector", evac(qt, bpar, blk), inc=("eV", 1),
                 waits=[("eT", mmc), (bsem, blkldc), ("g3", P.snap("g3"))])
            p4_ev[idx] = P.snap("eV")
            emit("gpsimd", (lambda qt, ob, bpar: lambda h: h.dma_start(
                out=x1s.ap()[qt * 128:(qt + 1) * 128, ob * 512:(ob + 1) * 512], in_=x1blk[bpar]))(qt, ob, bpar),
                inc=("g3", 16), waits=[("eV", P.snap("eV"))])
        ob_mm_snap[ob] = P.snap("eT")
    bar_all()

    if upto == 4:
        return _finish(x1s.ap())

    def ffn_phase(xin, lnmid, lnout, base_t, to_output=False, xout=None):
        xtb = [f32v(B, i * 8192, D) for i in range(2)]
        lstag = [bf16v(B, 16384 + i * D, D) for i in range(2)]
        fjnk = f16v(B, 24576, D)
        for g in range(0, NQ, 2):
            gsem = "s1" if (g // 2) % 2 == 0 else "s2"
            for i, t in enumerate((g, g + 1)):
                w = [("eS", P.snap("eS")), ("eV", P.snap("eV"))] if g >= 2 else []
                emit("sync", (lambda t, i: lambda h: h.dma_start(
                    out=xtb[i], in_=xin.ap()[t * 128:(t + 1) * 128, :]))(t, i),
                    inc=(gsem, 16), waits=w)
            ldc = P.snap(gsem)
            for i, t in enumerate((g, g + 1)):
                st = base_t + t
                emit("vector", (lambda i, st: lambda h: h.reduce_sum(mus[:, st:st + 1], xtb[i], axis=AX.X))(i, st),
                     inc=("eV", 1), waits=[(gsem, ldc)])
                emit("scalar", (lambda i, st: lambda h: h.activation(
                    out=fjnk, in_=xtb[i], func=AF.Square, accum_out=sqs[:, st:st + 1]))(i, st),
                    inc=("eS", 1), waits=[(gsem, ldc)])
            a, b = base_t + g, base_t + g + 2
            emit("vector", ln_stats_chain(a, b), inc=("eV", 1), waits=[("eS", P.snap("eS"))])
            emit("scalar", (lambda a, b: lambda h: h.activation(out=rstd[:, a:b], in_=rstd[:, a:b], func=AF.Sqrt))(a, b),
                 inc=("eS", 1), waits=[("eV", P.snap("eV"))])
            emit("vector", ln_post_chain(a, b), inc=("eV", 1), waits=[("eS", P.snap("eS"))])
            statc = P.snap("eV")
            for i, t in enumerate((g, g + 1)):
                st = base_t + t
                wst = [("g3", P.snap("g3"))] if g >= 2 else []
                emit("scalar", (lambda i, t, st: lambda h: h.activation(
                    out=lstag[i], in_=xtb[i], func=AF.Identity,
                    bias=nbias[:, st:st + 1], scale=rstd[:, st:st + 1]))(i, t, st),
                    inc=("eS", 1), waits=[("eV", statc)] + wst)
                emit("gpsimd", (lambda i, t: lambda h: h.dma_start(
                    out=lnmid.ap()[t * 128:(t + 1) * 128, :], in_=lstag[i]))(i, t),
                    inc=("g3", 16), waits=[("eS", P.snap("eS"))])
        bar_all()
        for c in range(DC):
            emit("sync", (lambda c: lambda h: h.dma_start_transpose(
                bf16v(A, c * 1024, 1024), lnmid.ap()[:, c * 128:(c + 1) * 128]))(c), inc=("s0", 16))
        lnldc = P.snap("s0")
        x2stag = [f32v(B, i * 8192, D) for i in range(2)]
        lnstag2 = [bf16v(B, 16384 + i * D, D) for i in range(2)]
        lowt = [bf16v(Dn, i * 256, 128) for i in range(2)]
        lostag = [bf16v(Dn, 512 + i * 128, 128) for i in range(2)]
        for qt in range(NQ):
            par = qt % 2
            wr_w = [("eV", P.snap("eV"))] if qt >= 1 else []
            for c in range(DC):
                emit("tensor", (lambda qt, c: lambda h: h.matmul(
                    PD[:, 0:NE], bf16v(A, c * 1024 + qt * 128, 128), wr_sb[:, c * NE:(c + 1) * NE],
                    start=(c == 0), stop=(c == DC - 1)))(qt, c),
                    inc=("eT", 1) if c == DC - 1 else None,
                    waits=([("s0", lnldc), ("g0", P.snap("g0"))] + wr_w) if c == 0 else ())
            rc = P.snap("eT")

            def router_sm(h):
                h.tensor_tensor(rlr[:, 0:NE], PD[:, 0:NE], br_sb, op=ALU.add)
                h.drain()
                return h.reduce_max(nmx8, rlr[:, 0:NE], axis=AX.X, negate=True)
            emit("vector", router_sm, inc=("eV", 1), waits=[("eT", rc)])
            emit("scalar", lambda h: h.activation(out=scors[:, 0:NE], in_=rlr[:, 0:NE], func=AF.Exp,
                                                  bias=nmx8, accum_out=ssum8),
                 inc=("eS", 1), waits=[("eV", P.snap("eV"))])

            def router_fin(h):
                h.reciprocal(sr8, ssum8)
                h.drain()
                return h.tensor_scalar(scors[:, 0:NE], scors[:, 0:NE], sr8, 8.0, op0=ALU.mult, op1=ALU.mult)
            emit("vector", router_fin, inc=("eV", 1), waits=[("eS", P.snap("eS"))])
            rfin = P.snap("eV")
            for c in range(DC):
                emit("tensor", (lambda qt, c: lambda h: h.matmul(
                    PE_[:, 0:128], bf16v(A, c * 1024 + qt * 128, 128), at_sb[:, c * 128:(c + 1) * 128],
                    start=(c == 0), stop=(c == DC - 1)))(qt, c),
                    inc=("eT", 1) if c == DC - 1 else None,
                    waits=[("eV", rfin)] if c == 0 else ())
            lc = P.snap("eT")

            def low_evac(par):
                def _f(h):
                    r = None
                    for e in range(NE):
                        r = h.tensor_scalar(lowt[par][:, e * 16:(e + 1) * 16],
                                            PE_[:, e * 16:(e + 1) * 16], scors[:, e:e + 1], None, op0=ALU.mult)
                    return r
                return _f
            emit("vector", low_evac(par), inc=("eV", 1), waits=[("eT", lc)])
            lec = P.snap("eV")
            emit("tensor", (lambda par: lambda h: h.transpose(
                PD[:, 64:128].bitcast(BF16), lowt[par], idt))(par),
                inc=("eT", 1), waits=[("eV", lec), ("s0", cfg_s0)])
            emit("scalar", (lambda par: lambda h: h.copy(lostag[par], PD[:, 64:128].bitcast(BF16)))(par),
                 inc=("eS", 1), waits=[("eT", P.snap("eT"))])
            ltc = P.snap("eS")
            xsem = "s1" if par == 0 else "s2"
            emit("sync", (lambda qt, par: lambda h: h.dma_start(
                out=x2stag[par], in_=xin.ap()[qt * 128:(qt + 1) * 128, :]))(qt, par),
                inc=(xsem, 16), waits=[("g4", P.snap("g4"))] if qt >= 2 else [])
            xldc = P.snap(xsem)
            for ob in range(8):
                pb = pa_blocks[ob % 2]
                wv_ = [("eV", P.snap("eV"))] if ob >= 2 else []
                emit("tensor", (lambda par, ob, pb: lambda h: h.matmul(
                    pb, lostag[par], bc_sb[:, ob * 512:(ob + 1) * 512], start=True, stop=True))(par, ob, pb),
                    inc=("eT", 1), waits=[("eS", ltc)] + wv_)
                emit("vector", (lambda par, ob, pb: lambda h: h.scalar_tensor_tensor(
                    x2stag[par][:, ob * 512:(ob + 1) * 512], pb, 2.0,
                    x2stag[par][:, ob * 512:(ob + 1) * 512], op0=ALU.mult, op1=ALU.add))(par, ob, pb),
                    inc=("eV", 1), waits=[("eT", P.snap("eT")), (xsem, xldc)])
            xec = P.snap("eV")
            dst = y.ap() if to_output else xout.ap()
            emit("gpsimd", (lambda qt, par, dst: lambda h: h.dma_start(
                out=dst[qt * 128:(qt + 1) * 128, :], in_=x2stag[par]))(qt, par, dst),
                inc=("g4", 16), waits=[("eV", xec)])
            if lnout is not None:
                st = base_t + NQ + qt
                emit("vector", (lambda par, st: lambda h: h.reduce_sum(mus[:, st:st + 1], x2stag[par], axis=AX.X))(par, st),
                     inc=("eV", 1), waits=[("eV", xec)])
                emit("scalar", (lambda par, st: lambda h: h.activation(
                    out=fjnk, in_=x2stag[par], func=AF.Square, accum_out=sqs[:, st:st + 1]))(par, st),
                    inc=("eS", 1), waits=[("eV", xec)])
                emit("vector", ln_stats_chain(st, st + 1), inc=("eV", 1), waits=[("eS", P.snap("eS"))])
                emit("scalar", (lambda st: lambda h: h.activation(out=rstd[:, st:st + 1], in_=rstd[:, st:st + 1],
                                                                  func=AF.Sqrt))(st),
                     inc=("eS", 1), waits=[("eV", P.snap("eV"))])
                emit("vector", ln_post_chain(st, st + 1), inc=("eV", 1), waits=[("eS", P.snap("eS"))])
                emit("scalar", (lambda par, st: lambda h: h.activation(
                    out=lnstag2[par], in_=x2stag[par], func=AF.Identity,
                    bias=nbias[:, st:st + 1], scale=rstd[:, st:st + 1]))(par, st),
                    inc=("eS", 1), waits=[("eV", P.snap("eV"))])
                emit("gpsimd", (lambda qt, par: lambda h: h.dma_start(
                    out=lnout.ap()[qt * 128:(qt + 1) * 128, :], in_=lnstag2[par]))(qt, par),
                    inc=("g3", 16), waits=[("eS", P.snap("eS"))])
        bar_all()

    ffn_phase(x1s, ln2s, ln3s, base_t=0, xout=x2s)

    if upto == 5:
        return _finish(x2s.ap())

    for c in range(DC):
        emit("sync", (lambda c: lambda h: h.dma_start_transpose(
            bf16v(A, c * 1024, 1024), ln3s.ap()[:, c * 128:(c + 1) * 128]))(c), inc=("s0", 16))
    l3ldc = P.snap("s0")
    wc_sb = bf16v(C, 0, DC * 512)
    emit("gpsimd", lambda h: h.dma_start(out=wc_sb, in_=wc.ap()), inc=("g7", 16))
    wcldc = P.snap("g7")
    latstag = [bf16v(Dn, i * 512, 512) for i in range(2)]
    for qt in range(NQ):
        par = qt % 2
        pb = pa_blocks[qt % 2]
        w = [("eV", P.snap("eV"))] if qt >= 2 else []
        for c in range(DC):
            emit("tensor", (lambda qt, c, pb: lambda h: h.matmul(
                pb, bf16v(A, c * 1024 + qt * 128, 128), wc_sb[:, c * 512:(c + 1) * 512],
                start=(c == 0), stop=(c == DC - 1)))(qt, c, pb),
                inc=("eT", 1) if c == DC - 1 else None,
                waits=([("s0", l3ldc), ("g7", wcldc)] + w) if c == 0 else ())
        wv2 = [("g3", P.snap("g3"))] if qt >= 2 else []
        emit("vector", (lambda par, pb: lambda h: h.tensor_copy(latstag[par], pb))(par, pb),
             inc=("eV", 1), waits=[("eT", P.snap("eT"))] + wv2)
        emit("gpsimd", (lambda qt, par: lambda h: h.dma_start(
            out=lato.ap()[qt * 128:(qt + 1) * 128, :], in_=latstag[par]))(qt, par),
            inc=("g3", 16), waits=[("eV", P.snap("eV"))])
    emit("gpsimd", lambda h: h.collective_compute(
        "AllGather", ALU.bypass, replica_groups=[[0, 1, 2, 3], [4, 5, 6, 7]],
        ins=[lato.ap().opt()], outs=[latf.ap().opt()]),
        inc=("cc", 1), waits=[("g3", P.snap("g3"))])
    p6_eT = P.snap("eT")
    p6_eV = P.snap("eV")

    if upto == 6:
        emit("gpsimd", None, waits=[("cc", P.snap("cc"))])
        return _finish(latf.ap(), nrows=512)

    qT = bf16v(B, 0, H * 1024)
    wqb = [bf16v(C, i * D, D) for i in range(2)]
    ro1 = f32v(Dn, 2048, 512)
    ro2 = f32v(Dn, 3072, 512)
    ro3 = f32v(Dn, 4096, 512)
    ro4 = f32v(Dn, 5120, 512)
    h_mm_snap = {}
    for h_ in range(H):
        par = h_ % 2
        hsem = "g1" if par == 0 else "g2"
        w = [("eT", h_mm_snap[h_ - 2])] if h_ >= 2 else [("eT", p6_eT), ("eV", p6_eV)]
        emit("gpsimd", (lambda h_, par: lambda h: h.dma_start(out=wqb[par], in_=wq.ap()[h_, :, :]))(h_, par),
             inc=(hsem, 16), waits=w)
        wldc = P.snap(hsem)
        for tb in range(2):
            pb = pa_blocks[(h_ * 2 + tb) % 2]
            wv_ = [("eV", P.snap("eV"))]
            for c in range(DC):
                emit("tensor", (lambda par, c, tb, pb: lambda h: h.matmul(
                    pb, wqb[par][:, c * 128:(c + 1) * 128], bf16v(A, c * 1024 + tb * 512, 512),
                    start=(c == 0), stop=(c == DC - 1)))(par, c, tb, pb),
                    inc=("eT", 1) if c == DC - 1 else None,
                    waits=([(hsem, wldc)] + wv_) if c == 0 else ())
            mmc = P.snap("eT")

            def rope_q(h_, tb, pb):
                lo, hi = tb * 512, (tb + 1) * 512
                dst = qT[:, h_ * 1024 + lo:h_ * 1024 + hi]

                def _f(h):
                    h.tensor_tensor(ro1[0:64, :], pb[0:64, :], cq_sb[0:64, lo:hi], op=ALU.mult)
                    h.tensor_tensor(ro2[0:64, :], pb[64:128, :], sq_sb[0:64, lo:hi], op=ALU.mult)
                    h.tensor_tensor(ro3[0:64, :], pb[64:128, :], cq_sb[64:128, lo:hi], op=ALU.mult)
                    h.tensor_tensor(ro4[0:64, :], pb[0:64, :], sq_sb[64:128, lo:hi], op=ALU.mult)
                    h.drain()
                    h.tensor_tensor(dst[0:64, :], ro1[0:64, :], ro2[0:64, :], op=ALU.subtract)
                    return h.tensor_tensor(dst[64:128, :], ro3[0:64, :], ro4[0:64, :], op=ALU.add)
                return _f
            emit("vector", rope_q(h_, tb, pb), inc=("eV", 1), waits=[("eT", mmc)])
        h_mm_snap[h_] = P.snap("eT")
    bar_all()

    if upto == 7:
        bar_all()
        return _finish(qT)

    latT = bf16v(A, 0, 4 * S)
    emit("sync", None, waits=[("cc", P.snap("cc"))])
    for lc in range(4):
        emit("sync", (lambda lc: lambda h: h.dma_start_transpose(
            bf16v(A, lc * S, S), latf.ap()[:, lc * 128:(lc + 1) * 128]))(lc), inc=("s0", 16))
    latldc = P.snap("s0")
    ce_sb = bf16v(A, 4 * S, S)
    se_sb = bf16v(A, 5 * S, S)
    emit("gpsimd", lambda h: h.dma_start(out=ce_sb, in_=cose.ap()), inc=("g7", 16))
    emit("gpsimd", lambda h: h.dma_start(out=se_sb, in_=sine.ap()), inc=("g7", 16))
    tbldc = P.snap("g7")
    wkb = [bf16v(C, i * 512, 512) for i in range(2)]
    wvb = [bf16v(C, 1024 + i * 1024, 1024) for i in range(2)]
    kstag = [bf16v(C, 3072 + i * 512, 512) for i in range(2)]
    vstag0 = bf16v(A, 6 * S, DC * 128)
    vstag1 = bf16v(A, 7 * S, DC * 128)
    hk_mm_snap = {}
    for h_ in range(H):
        par = h_ % 2
        hsem = "g1" if par == 0 else "g2"
        w = [("eT", hk_mm_snap[h_ - 2])] if h_ >= 2 else []
        emit("gpsimd", (lambda h_, par: lambda h: h.dma_start(out=wkb[par], in_=wk.ap()[h_, :, :]))(h_, par),
             inc=(hsem, 16), waits=w)
        wldc = P.snap(hsem)
        for kb in range(8):
            pb = PD[:, 0:512] if kb % 2 == 0 else PE_[:, 0:512]
            wv_ = [("eV", P.snap("eV"))]
            for lcc in range(4):
                emit("tensor", (lambda par, lcc, kb, pb: lambda h: h.matmul(
                    pb, wkb[par][:, lcc * 128:(lcc + 1) * 128], bf16v(A, lcc * S + kb * 512, 512),
                    start=(lcc == 0), stop=(lcc == 3)))(par, lcc, kb, pb),
                    inc=("eT", 1) if lcc == 3 else None,
                    waits=([(hsem, wldc), ("g7", tbldc), ("s0", latldc)] + wv_) if lcc == 0 else ())
            mmc = P.snap("eT")

            def rope_k(par, kb, pb):
                lo, hi = kb * 512, (kb + 1) * 512
                dst = kstag[par]

                def _f(h):
                    h.tensor_tensor(ro1[0:64, :], pb[0:64, :], ce_sb[0:64, lo:hi], op=ALU.mult)
                    h.tensor_tensor(ro2[0:64, :], pb[64:128, :], se_sb[0:64, lo:hi], op=ALU.mult)
                    h.tensor_tensor(ro3[0:64, :], pb[64:128, :], ce_sb[64:128, lo:hi], op=ALU.mult)
                    h.tensor_tensor(ro4[0:64, :], pb[0:64, :], se_sb[64:128, lo:hi], op=ALU.mult)
                    h.drain()
                    h.tensor_tensor(dst[0:64, :], ro1[0:64, :], ro2[0:64, :], op=ALU.subtract)
                    return h.tensor_tensor(dst[64:128, :], ro3[0:64, :], ro4[0:64, :], op=ALU.add)
                return _f
            wkst = [("x0", P.snap("x0"))] if (h_ * 8 + kb) >= 2 else []
            emit("vector", rope_k(par, kb, pb), inc=("eV", 1), waits=[("eT", mmc)] + wkst)
            emit("scalar", (lambda h_, kb, par: lambda h: h.dma_start(
                out=kts.ap()[h_, :, kb * 512:(kb + 1) * 512], in_=kstag[par]))(h_, kb, par),
                inc=("x0", 16), waits=[("eV", P.snap("eV"))])
        if h_ % 2 == 0:
            gidx = h_ // 2
            vsem = "g5" if gidx % 2 == 0 else "g6"
            emit("gpsimd", (lambda gidx, par: lambda h: h.dma_start(out=wvb[par], in_=wv.ap()[gidx, :, :]))(gidx, par),
                 inc=(vsem, 16), waits=[("eT", P.snap("eT"))] if gidx >= 1 else [("eG", P.snap("eG"))])
            wvldc = P.snap(vsem)
            for kt in range(DC):
                pb2 = PC[:, 0:256] if kt % 2 == 0 else PC[:, 512:768]
                wv_ = [("eV", P.snap("eV"))]
                for lcc in range(4):
                    emit("tensor", (lambda kt, lcc, par, pb2: lambda h: h.matmul(
                        pb2, bf16v(A, lcc * S + kt * 128, 128),
                        wvb[par][:, lcc * 256:(lcc + 1) * 256], start=(lcc == 0), stop=(lcc == 3)))(kt, lcc, par, pb2),
                        inc=("eT", 1) if lcc == 3 else None,
                        waits=([(vsem, wvldc)] + wv_) if lcc == 0 else ())
                mmc = P.snap("eT")

                def v_evac(kt, pb2):
                    def _f(h):
                        h.tensor_copy(vstag0[:, kt * 128:(kt + 1) * 128], pb2[:, 0:128])
                        return h.tensor_copy(vstag1[:, kt * 128:(kt + 1) * 128], pb2[:, 128:256])
                    return _f
                wvs = [("g4", P.snap("g4"))] if kt < 2 and gidx >= 1 else []
                emit("vector", v_evac(kt, pb2), inc=("eV", 1), waits=[("eT", mmc)] + wvs)
            vec = P.snap("eV")
            emit("gpsimd", (lambda h_: lambda h: h.dma_start(out=vts.ap()[h_, :, :], in_=vstag0))(h_),
                 inc=("g4", 16), waits=[("eV", vec)])
            emit("gpsimd", (lambda h_: lambda h: h.dma_start(out=vts.ap()[h_ + 1, :, :], in_=vstag1))(h_),
                 inc=("g4", 16))
        hk_mm_snap[h_] = P.snap("eT")
    bar_all()

    if upto == 8:
        return _finish(kts.ap()[0:8, :, :])

    kbuf = [bf16v(C, i * S, S) for i in range(2)]
    vbuf = [bf16v(C, 2 * S + i * S, S) for i in range(2)]
    ptb = [bf16v(Dn, i * 1024, 1024) for i in range(4)]
    drow = f32v(Dn, 4096, 1024)
    rrow = f32v(Dn, 6144, 1024)
    rbs = f32v(A, 0, 1024)
    sc18 = float(1.0 / np.sqrt(128.0))
    prev_fin_eV = P.snap("eV")
    ha_mm_snap = {}
    for h_ in range(H):
        par = h_ % 2
        hsem = "s1" if par == 0 else "s2"
        wld = [("eT", ha_mm_snap[h_ - 2])] if h_ >= 2 else []
        emit("sync", (lambda h_, par: lambda h: h.dma_start(out=kbuf[par], in_=kts.ap()[h_, :, :]))(h_, par),
             inc=(hsem, 16), waits=wld)
        emit("sync", (lambda h_, par: lambda h: h.dma_start(out=vbuf[par], in_=vts.ap()[h_, :, :]))(h_, par),
             inc=(hsem, 16))
        kvldc = P.snap(hsem)
        def emit_scores(c):
            spb = PA[:, 0:1024] if c % 2 == 0 else PB[:, 0:1024]
            wsc = [("eS", pt_ready[c - 2])] if c >= 2 else []
            if c == 0:
                wsc = wsc + [("eV", prev_fin_eV)]
            for qb in range(2):
                emit("tensor", (lambda h_, c, qb, par, spb: lambda h: h.matmul(
                    spb[:, qb * 512:(qb + 1) * 512], kbuf[par][:, c * 128:(c + 1) * 128],
                    qT[:, h_ * 1024 + qb * 512:h_ * 1024 + (qb + 1) * 512],
                    start=True, stop=True))(h_, c, qb, par, spb),
                    inc=("eT", 1),
                    waits=([(hsem, kvldc)] + wsc) if qb == 0 else ())
            mmc = P.snap("eT")
            wpt = [("eT", P.snap("eT"))] if c >= 4 else []
            emit("scalar", (lambda c, spb: lambda h: h.activation(
                out=ptb[c % 4], in_=spb[:, 0:1024], func=AF.Exp, scale=sc18))(c, spb),
                inc=("eS", 1), waits=[("eT", mmc)] + wpt)

        def emit_pv(c):
            ec = pt_ready[c]
            for qb in range(2):
                emit("tensor", (lambda c, qb, par: lambda h: h.matmul(
                    PC[:, qb * 512:(qb + 1) * 512], vbuf[par][:, c * 128:(c + 1) * 128],
                    ptb[c % 4][:, qb * 512:(qb + 1) * 512],
                    start=(c == 0), stop=(c == DC - 1)))(c, qb, par),
                    inc=("eT", 1), waits=[("eS", ec)] if qb == 0 else ())
                emit("tensor", (lambda c, qb: lambda h: h.matmul(
                    (PD if qb == 0 else PE_)[0:1, :], ones_bf,
                    ptb[c % 4][:, qb * 512:(qb + 1) * 512],
                    start=(c == 0), stop=(c == DC - 1)))(c, qb),
                    inc=("eT", 1))

        pt_ready = {}
        emit_scores(0)
        pt_ready[0] = P.snap("eS")
        for c in range(1, DC):
            emit_scores(c)
            pt_ready[c] = P.snap("eS")
            emit_pv(c - 1)
        emit_pv(DC - 1)
        pvc = P.snap("eT")

        def dfin(h):
            h.tensor_copy(drow[0:1, 0:512], PD[0:1, :])
            h.tensor_copy(drow[0:1, 512:1024], PE_[0:1, :])
            h.drain()
            return h.reciprocal(rrow[0:1, :], drow[0:1, :])
        emit("vector", dfin, inc=("eV", 1), waits=[("eT", pvc)])
        dc_ = P.snap("eV")
        for qb in range(2):
            emit("tensor", (lambda qb: lambda h: h.matmul(
                PA[:, qb * 512:(qb + 1) * 512], onesr[0:1, :],
                rrow[0:1, qb * 512:(qb + 1) * 512], start=True, stop=True))(qb),
                inc=("eT", 1), waits=[("eV", dc_)] if qb == 0 else ())
        bcc = P.snap("eT")

        def ofin(h_):
            def _f(h):
                h.tensor_copy(rbs, PA[:, 0:1024])
                h.drain()
                return h.tensor_tensor(qT[:, h_ * 1024:(h_ + 1) * 1024], PC[:, 0:1024], rbs, op=ALU.mult)
            return _f
        emit("vector", ofin(h_), inc=("eV", 1), waits=[("eT", bcc)])
        prev_fin_eV = P.snap("eV")
        ha_mm_snap[h_] = P.snap("eT")
    bar_all()

    if upto == 9:
        return _finish(qT)

    wob = [bf16v(C, i * DC * 256, DC * 256) for i in range(2)]
    x2blk = [f32v(A, i * 512, 256) for i in range(4)]
    x3blk = [f32v(A, 2048 + i * 512, 256) for i in range(4)]
    obh_mm_snap = {}
    p9_ev = {}
    for obh in range(16):
        par = obh % 2
        ob, half = obh // 2, obh % 2
        osem = "g1" if par == 0 else "g2"
        w = [("eT", obh_mm_snap[obh - 2])] if obh >= 2 else []
        wsrc = wo.ap()[obh // 2, :, :].rearrange("p (ec j) -> p ec j", ec=DC)[:, :, (obh % 2) * 256:(obh % 2 + 1) * 256]
        emit("gpsimd", (lambda wsrc, par: lambda h: h.dma_start(out=wob[par], in_=wsrc))(wsrc, par),
             inc=(osem, 16), waits=w)
        wldc = P.snap(osem)
        for qt in range(NQ):
            idx9 = obh * NQ + qt
            bpar = idx9 % 4
            pb = pa_blocks[bpar]
            wblk = [("eV", p9_ev[idx9 - 4])] if idx9 >= 4 else []
            xsem2 = "x1p" if bpar % 2 == 0 else "x2p"
            emit("scalar", (lambda qt, ob, half, bpar: lambda h: h.dma_start(
                out=x2blk[bpar], in_=x2s.ap()[qt * 128:(qt + 1) * 128,
                                              ob * 512 + half * 256:ob * 512 + half * 256 + 256]))(qt, ob, half, bpar),
                inc=(xsem2, 16), waits=wblk)
            xldc = P.snap(xsem2)
            for c in range(DC):
                emit("tensor", (lambda qt, c, par, pb: lambda h: h.matmul(
                    pb[:, 0:256], qT[:, c * 1024 + qt * 128:c * 1024 + qt * 128 + 128],
                    wob[par][:, c * 256:(c + 1) * 256],
                    start=(c == 0), stop=(c == DC - 1)))(qt, c, par, pb),
                    inc=("eT", 1) if c == DC - 1 else None,
                    waits=([(osem, wldc)] + ([("eV", p9_ev[idx9 - 4])] if idx9 >= 4 else [])) if c == 0 else ())
            mmc = P.snap("eT")
            emit("vector", (lambda bpar, pb: lambda h: h.tensor_tensor(
                x3blk[bpar], pb[:, 0:256], x2blk[bpar], op=ALU.add))(bpar, pb),
                inc=("eV", 1), waits=[("eT", mmc), (xsem2, xldc), ("g3", P.snap("g3"))])
            p9_ev[idx9] = P.snap("eV")
            emit("gpsimd", (lambda qt, ob, half, bpar: lambda h: h.dma_start(
                out=x3s.ap()[qt * 128:(qt + 1) * 128,
                             ob * 512 + half * 256:ob * 512 + half * 256 + 256], in_=x3blk[bpar]))(qt, ob, half, bpar),
                inc=("g3", 16), waits=[("eV", P.snap("eV"))])
        obh_mm_snap[obh] = P.snap("eT")
    bar_all()

    if upto == 10:
        return _finish(x3s.ap())

    ffn_phase(x3s, ln2s, None, base_t=16, to_output=True)

    return _finish()


_NC_CACHE = None
UPTO = 99


def kernel(L0_attn_a_minibatch, L0_attn_a_eta, L0_attn_b_Wq, L0_attn_b_Wc, L0_attn_b_Wk,
           L0_attn_b_Wv, L0_attn_b_Wo, L0_ffn_A, L0_ffn_B, L0_ffn_Wr, L0_ffn_br):
    global _NC_CACHE
    x = np.asarray(L0_attn_a_minibatch, dtype=np.float32)
    eta = np.asarray(L0_attn_a_eta, dtype=np.float32).reshape(1)
    Wq = np.asarray(L0_attn_b_Wq, dtype=np.float32)
    Wc = np.asarray(L0_attn_b_Wc, dtype=np.float32)
    Wk = np.asarray(L0_attn_b_Wk, dtype=np.float32)
    Wv = np.asarray(L0_attn_b_Wv, dtype=np.float32)
    Wo = np.asarray(L0_attn_b_Wo, dtype=np.float32)
    A_ = np.asarray(L0_ffn_A, dtype=np.float32)
    B_ = np.asarray(L0_ffn_B, dtype=np.float32)
    Wr = np.asarray(L0_ffn_Wr, dtype=np.float32)
    br = np.asarray(L0_ffn_br, dtype=np.float32)

    wq_dev = np.ascontiguousarray(Wq.reshape(DC, 128, H, DH).transpose(2, 1, 0, 3).reshape(H, 128, DC * 128))
    wc_dev = np.ascontiguousarray(Wc.reshape(DC, 128, DL).transpose(1, 0, 2))
    wk_dev = np.ascontiguousarray(Wk.reshape(4, 128, H, DH).transpose(2, 1, 0, 3).reshape(H, 128, 4 * 128))
    wv_dev = np.ascontiguousarray(Wv.reshape(4, 128, 16, 256).transpose(2, 1, 0, 3).reshape(16, 128, 4 * 256))
    wo_dev = np.ascontiguousarray(Wo.reshape(DC, 128, 8, 512).transpose(2, 1, 0, 3).reshape(8, 128, DC * 512))
    at_dev = np.ascontiguousarray(A_.reshape(128, D).T.reshape(DC, 128, 128).transpose(1, 0, 2))
    bc_dev = np.ascontiguousarray(B_.transpose(0, 2, 1).reshape(128, D))
    wr_dev = np.ascontiguousarray(Wr.T.reshape(DC, 128, NE).transpose(1, 0, 2))
    br_dev = np.ascontiguousarray(np.broadcast_to(br[None, :], (128, NE)))
    eta_dev = np.ascontiguousarray(np.broadcast_to(eta[None, :], (128, 1)))

    inv = 1.0 / 10000.0 ** (np.arange(0, DH, 2, dtype=np.float64) / DH)
    pos = np.arange(S, dtype=np.float64)
    fr = pos[:, None] * inv[None, :]
    emb = np.concatenate([fr, fr], axis=-1)
    cosT = np.ascontiguousarray(np.cos(emb).T.astype(np.float32))
    sinT = np.ascontiguousarray(np.sin(emb).T.astype(np.float32))

    if _NC_CACHE is None:
        _NC_CACHE = build_program(UPTO)
    nc = _NC_CACHE

    in_maps = []
    for core in range(8):
        b = core // 4
        s0 = (core % 4) * QO
        in_maps.append({
            "xf": np.ascontiguousarray(x[b]),
            "xo": np.ascontiguousarray(x[b, s0:s0 + QO]),
            "wq": wq_dev, "wc": wc_dev, "wk": wk_dev, "wv": wv_dev, "wo": wo_dev,
            "at_": at_dev, "bc_": bc_dev, "wr_": wr_dev, "brt": br_dev, "etat": eta_dev,
            "cose": cosT, "sine": sinT,
            "cosq": np.ascontiguousarray(cosT[:, s0:s0 + QO]),
            "sinq": np.ascontiguousarray(sinT[:, s0:s0 + QO]),
        })
    res = run_bass_kernel_spmd(nc, in_maps, core_ids=list(range(8)))
    out = np.empty((2, S, D), dtype=np.float32)
    for core in range(8):
        b = core // 4
        s0 = (core % 4) * QO
        out[b, s0:s0 + QO] = res.results[core]["y"]
    return out
